# version 18
# baseline (speedup 1.0000x reference)
import numpy as np
import ml_dtypes
from contextlib import ExitStack

import concourse.bass as bass
import concourse.mybir as mybir
from concourse.bass_utils import run_bass_kernel_spmd

F32 = mybir.dt.float32
BF16 = mybir.dt.bfloat16
AF = mybir.ActivationFunctionType
ALU = mybir.AluOpType
AX = mybir.AxisListType

MAGIC = float(np.float32(1.5 * 2 ** 23))
BF = ml_dtypes.bfloat16

B, T, C, H, D = 4, 4096, 1024, 4, 256
L = 128
CC = 8
NOWN = 16
NPRE = 8
INNER = 4096
NIC = 32
NSL = 8

POS_ORDER = ["q8", "k8", "q8o", "k8o", "q16", "k16", "q32", "k32",
             "q64", "k64", "qint", "kst"]
TI = {n: i for i, n in enumerate(POS_ORDER)}


class Res:
    __slots__ = ("name", "writer", "readers")

    def __init__(self, name):
        self.name, self.writer, self.readers = name, None, []


class Task:
    __slots__ = ("tid", "engine", "stream", "fn", "deps", "inc", "cum", "needed")


class Sched:
    ENGINES = ("tensor", "vector", "scalar", "sync", "gpsimd")

    def __init__(self):
        self.tasks = []

    def task(self, engine, fn, reads=(), writes=(), deps=(), dma=False):
        t = Task()
        t.tid = len(self.tasks)
        t.engine = engine
        t.stream = ("dma_" + engine) if dma else engine
        t.fn = fn
        t.inc = 16 if dma else 1
        dep = set()
        for r in reads:
            if r.writer is not None:
                dep.add(r.writer)
        for w in writes:
            if w.writer is not None:
                dep.add(w.writer)
            for rd in w.readers:
                dep.add(rd)
        for d in deps:
            if d is not None:
                dep.add(d)
        t.deps = sorted(dep, key=lambda d: d.tid)
        for d in t.deps:
            assert d.tid < t.tid
        for r in reads:
            r.readers.append(t)
        for w in writes:
            w.writer = t
            w.readers = []
        self.tasks.append(t)
        return t

    def barrier(self):
        last = {}
        for t in self.tasks:
            last[t.stream] = t
        deps = list(last.values())
        for e in self.ENGINES:
            self.task(e, lambda eng: eng.nop(), deps=deps)

    def emit(self, nc, block):
        NSEM = 6
        streams = {}
        for t in self.tasks:
            streams.setdefault(t.stream, []).append(t)
        sems, guards = {}, []
        for s, ts in streams.items():
            if ts and ts[0].inc == 16:
                hs = []
                for i in range(NSEM):
                    g = nc.semaphore(f"sem_{s}_{i}")
                    hs.append(g.__enter__())
                    guards.append(g)
                sems[s] = hs
            else:
                g = nc.semaphore("sem_" + s)
                sems[s] = g.__enter__()
                guards.append(g)

        def skip_edge(d, t):
            return False

        for t in self.tasks:
            t.needed = (t.inc == 16)
        for t in self.tasks:
            for d in t.deps:
                if not skip_edge(d, t):
                    d.needed = True

        consumers = {}
        for t in self.tasks:
            for d in t.deps:
                if d.inc == 16:
                    consumers.setdefault(d.tid, []).append(t.tid)

        wave_wait = {}
        totals = {}
        for s, ts in streams.items():
            if not ts or ts[0].inc != 16:
                c = 0
                for t in ts:
                    if t.needed:
                        c += t.inc
                    t.cum = c
                totals[s] = c
                continue
            cums = [0] * NSEM
            hist = []
            for w, t in enumerate(ts):
                si = w % NSEM
                cums[si] += 16
                t.cum = (si, cums[si])
                hist.append((si, cums[si]))
                if w >= NSEM:
                    wave_wait[t.tid] = hist[w - NSEM]
            totals[s] = cums

        def body_for(engine):
            mine = [t for t in self.tasks if t.engine == engine]

            def body(eng):
                marks = {}
                for t in mine:
                    want = {}
                    if t.inc == 16 and t.tid in wave_wait:
                        si, v = wave_wait[t.tid]
                        want[(t.stream, si)] = v
                    for d in t.deps:
                        if skip_edge(d, t):
                            continue
                        assert d.needed
                        if d.inc == 16:
                            si, v = d.cum
                            key = (d.stream, si)
                        else:
                            key, v = (d.stream, 0), d.cum
                        want[key] = max(want.get(key, 0), v)
                    for (s, si), v in sorted(want.items()):
                        if marks.get((s, si), 0) < v:
                            h = sems[s][si] if isinstance(sems[s], list) else sems[s]
                            eng.wait_ge(h, v)
                            marks[(s, si)] = v
                    inst = t.fn(eng)
                    if t.needed:
                        assert inst is not None, f"task {t.tid} ({engine}) returned None"
                        if t.inc == 16:
                            inst.then_inc(sems[t.stream][t.cum[0]], 16)
                        else:
                            inst.then_inc(sems[t.stream], 1)
                if engine == "sync" and "dma_sync" in totals:
                    for i, v in enumerate(totals["dma_sync"]):
                        if v:
                            eng.wait_ge(sems["dma_sync"][i], v)
            return body

        for e in self.ENGINES:
            getattr(block, e)(body_for(e))
        return guards


def build_program(n_own=NOWN, n_pre=NPRE, debug_taps=(), debug_spill=False,
                  detect_races=False):
    nc = bass.Bass("TRN2", target_bir_lowering=False,
                   detect_race_conditions=detect_races)
    _lp = nc.allow_low_precision("bf16 matmul operands by design; fp32 accumulate")
    _lp.__enter__()
    nc._lp_guard = _lp
    S = Sched()
    NTOK = n_own * L
    NPRET = max(n_pre, 1) * L

    def din(name, shape, dt=F32):
        return nc.dram_tensor(name, shape, dt, kind="ExternalInput").ap()

    xin = din("xin", [NTOK, C])
    xpre = din("xpre", [NPRET, C])
    flag = din("flag", [128, 1])
    td = din("td", [1, C])
    nsc_bc = din("nsc_bc", [128, C])
    mix_bc = din("mix_bc", [128, 4, C])
    wqT = din("wqT", [C, C]); wkT = din("wkT", [C, C]); wvT = din("wvT", [C, C])
    wgT = din("wgT", [C, C]); woT = din("woT", [C, C])
    wgateT = din("wgateT", [C, INNER]); wupT = din("wupT", [C, INNER])
    wdownT = din("wdownT", [INNER, C])
    ident_in = din("ident_i", [128, 128], BF16)
    masks_in = din("masks_i", [128, 5, 128])
    posv_in = din("posv", [1, 12, 128])
    y = nc.dram_tensor("y", [NTOK, C], F32, kind="ExternalOutput").ap()

    tap_reg = {}
    def emit_taps(point):
        for tn in list(debug_taps):
            if tn in tap_reg:
                ap, res, when = tap_reg[tn]
                if when != point:
                    continue
                o = nc.dram_tensor("tap_" + tn, list(ap.shape), ap.dtype,
                                   kind="ExternalOutput").ap()
                S.task("sync", lambda e, o=o, s=ap: e.dma_start(o, s),
                       reads=[res], dma=True)

    _ki = "ExternalOutput" if debug_spill else "Internal"
    xmid_d = nc.dram_tensor("xmid_d", [NTOK, C], F32, kind=_ki).ap()
    hq_d = nc.dram_tensor("hq_d", [n_own, 4, 128, 1024], BF16, kind=_ki).ap()
    gq_d = nc.dram_tensor("gq_d", [2, CC, 128, INNER], BF16, kind=_ki).ap()
    dq_d = nc.dram_tensor("dq_d", [NIC, 128, C], BF16, kind=_ki).ap()
    r_xmidd, r_hqd, r_gqd, r_dqd = Res("xmid_d"), Res("hq_d"), Res("gq_d"), Res("dq_d")

    ctx_common = ExitStack()
    ctx_attn = ExitStack()
    ctx_pre = ExitStack()

    def mksb(ctx, name, shape, dt):
        t = ctx.enter_context(nc.sbuf_tensor(name, shape, dt))
        return t, Res(name)

    pb = [ctx_common.enter_context(nc.psum_tensor(f"pb{i}", [128, 512], F32))
          for i in range(7)]
    pbt = ctx_common.enter_context(nc.psum_tensor("pbt", [128, 1024], BF16))
    rb = [Res(f"pb{i}") for i in range(7)]
    rbt = Res("pbt")

    ident, r_ident = mksb(ctx_common, "ident", [128, 128], BF16)
    mgv_m, r_mgv_m = mksb(ctx_common, "mgv_m", [128, 3], F32)
    alpha, r_alpha = mksb(ctx_common, "alpha", [128, max(n_own, 1)], F32)

    A = ctx_attn

    ones1, r_ones1 = mksb(A, "ones1", [1, 128], F32)
    onesc_bf, r_onesc = mksb(A, "onesc_bf", [128, 1], BF16)
    onesf_col, r_onesf = mksb(A, "onesf_col", [128, 1], F32)
    flag_s, r_flag = mksb(A, "flag_s", [128, 1], F32)
    nsc_s, r_nsc = mksb(A, "nsc_s", [128, C], F32)
    mixb, r_mixb = mksb(A, "mixb", [128, 4, C], BF16)
    wL, r_wL = mksb(A, "wL", [128, CC], F32)
    TAB, r_TAB = mksb(A, "TAB", [128, 12, C], BF16)
    mgv, r_mgv = mksb(A, "mgv", [128, 5], F32)
    asum, r_asum = mksb(A, "asum", [128, 3], F32)
    Ast, r_Ast = mksb(A, "Ast", [128, CC, 257], F32)
    Abf, r_Abf = mksb(A, "Abf", [128, CC, 257], BF16)
    sx, r_sx = zip(*[mksb(A, f"sx{i}", [128, C], F32) for i in range(2)])
    ln1f, r_ln1f = zip(*[mksb(A, f"ln1f{i}", [128, CC, 132], BF16) for i in range(2)])
    mixo, r_mixo = zip(*[mksb(A, n, [128, CC, 128], BF16)
                         for n in ("xq", "xk", "xv")])
    mixg, r_mixg = zip(*[mksb(A, f"xg{i}", [128, CC, 128], BF16) for i in range(2)])
    diff, r_diff = mksb(A, "diff", [128, CC, 128], BF16)
    qex_tm, r_qex_tm = mksb(A, "qex_tm", [128, C], BF16)
    kex_tm, r_kex_tm = mksb(A, "kex_tm", [128, C], BF16)
    qex_fm, r_qex_fm = mksb(A, "qex_fm", [128, C], BF16)
    kex_fm, r_kex_fm = mksb(A, "kex_fm", [128, C], BF16)
    v_tm, r_v_tm = mksb(A, "v_tm", [128, C], BF16)
    khst, r_khst = mksb(A, "khst", [128, C], BF16)
    qtint, r_qtint = mksb(A, "qtint", [128, C], BF16)
    mkq1, r_mkq1 = mksb(A, "mkq", [128, C], BF16)
    mkk1, r_mkk1 = mksb(A, "mkk", [128, C], BF16)
    mkq, r_mkq = (mkq1, mkq1), (r_mkq1, r_mkq1)
    mkk, r_mkk = (mkk1, mkk1), (r_mkk1, r_mkk1)
    Psb, r_Psb = mksb(A, "Psb", [128, 512], BF16)
    attf, r_attf = mksb(A, "attf", [128, CC, 128], BF16)
    rzrow, r_rzrow = mksb(A, "rzrow", [1, 512], F32)
    rzbc, r_rzbc = mksb(A, "rzbc", [128, 512], BF16)
    scr1, r_scr1 = mksb(A, "scr1", [128, C], F32)
    ln1tm_t, r_ln1tm = mksb(A, "ln1tm", [128, C], BF16)
    scr3, r_scr3 = mksb(A, "scr3", [128, C], F32)
    st, r_st = zip(*[mksb(A, f"st{i}", [128, 8], F32) for i in range(4)])
    stL, r_stL = mksb(A, "stL", [128, 10], F32)
    wq_s, r_wq = mksb(A, "wq_s", [128, CC, C], BF16)
    wk_s, r_wk = mksb(A, "wk_s", [128, CC, C], BF16)
    wv_s, r_wv = mksb(A, "wv_s", [128, CC, C], BF16)
    wg_s, r_wg = mksb(A, "wg_s", [128, CC, C], BF16)
    wo_s, r_wo = mksb(A, "wo_s", [128, CC, C], BF16)

    c128, r_c128 = mksb(ctx_pre, "c128", [1, 1], F32)
    lw, r_lw = mksb(ctx_pre, "lw", [1, C], F32)
    posv_s, r_posv = mksb(ctx_pre, "posv_s", [1, 12, 128], F32)

    def dma_in(dst_ap, src_ap, wres, engine="sync", rres=()):
        return S.task(engine, lambda e, d=dst_ap, s=src_ap: e.dma_start(d, s),
                      reads=rres, writes=[wres], dma=True)

    dma_in(ident[:, :], ident_in, r_ident)
    dma_in(posv_s[:, :, :], posv_in, r_posv)
    dma_in(flag_s[:, :], flag, r_flag)
    dma_in(lw[:, :], td, r_lw)
    dma_in(nsc_s[:, :], nsc_bc, r_nsc)

    for wdst, rres, wsrc in ((wq_s, r_wq, wqT), (wk_s, r_wk, wkT), (wv_s, r_wv, wvT),
                             (wg_s, r_wg, wgT), (wo_s, r_wo, woT)):
        src = wsrc.rearrange("(cc p) o -> p cc o", p=128)
        S.task("gpsimd", lambda e, d=wdst, s=src: e.dma_start(d[:, :, :], s),
               writes=[rres], dma=True)

    S.task("vector", lambda e: e.memset(ones1[:, :], 1.0), writes=[r_ones1])
    S.task("vector", lambda e: e.memset(onesc_bf[:, :], 1.0), writes=[r_onesc])
    S.task("vector", lambda e: e.memset(onesf_col[:, :], 1.0), writes=[r_onesf])
    S.task("vector", lambda e: e.memset(c128[:, :], 128.0), writes=[r_c128])
    S.task("vector", lambda e: e.memset(Ast[:, :, :], 0.0), writes=[r_Ast])
    S.task("vector", lambda e: e.memset(Abf[:, :, :], 0.0), writes=[r_Abf])
    S.task("vector", lambda e: e.memset(asum[:, :], 0.0), writes=[r_asum])
    S.task("vector", lambda e: e.memset(mgv[:, 3:4], MAGIC), writes=[r_mgv])
    S.task("vector", lambda e: e.memset(mgv[:, 4:5], -(MAGIC - 1.0)), writes=[r_mgv])
    S.task("gpsimd", lambda e: e.dma_start(mixb[:, :, :], mix_bc),
           writes=[r_mixb], dma=True)
    S.task("scalar", lambda e: e.activation(lw[:, :], lw[:, :], AF.Exp),
           reads=[r_lw], writes=[r_lw])
    S.task("vector", lambda e: e.tensor_scalar(lw[:, :], lw[:, :], -1.0, None, ALU.mult),
           reads=[r_lw], writes=[r_lw])

    for ti, pname in enumerate(POS_ORDER):
        pvec = posv_s[:, ti, :]
        if pname == "kst":
            def mk_out(e, pv=pvec):
                e.matmul(pb[0][:, :], pv, lw[:, 0:512], start=True, stop=True)
                return e.matmul(pb[1][:, :], pv, lw[:, 512:1024], start=True, stop=True)
        else:
            def mk_out(e, pv=pvec):
                last = None
                for cc in range(CC):
                    bank = pb[cc // 4]
                    col = (cc % 4) * 128
                    last = e.matmul(bank[:, col:col + 128],
                                    lw[:, cc * 128:(cc + 1) * 128], pv,
                                    start=(cc % 4 == 0), stop=(cc % 4 == 3))
                return last
        S.task("tensor", mk_out, reads=[r_lw, r_posv], writes=[rb[0], rb[1]])
        def mk_clamp(e):
            e.tensor_scalar(scr1[:, 0:512], pb[0][:, :], 30.0, None, ALU.min)
            return e.tensor_scalar(scr1[:, 512:1024], pb[1][:, :], 30.0, None, ALU.min)
        S.task("vector", mk_clamp, reads=[rb[0], rb[1]], writes=[r_scr1])
        S.task("scalar", lambda e, ti=ti: e.activation(TAB[:, ti, :], scr1[:, :], AF.Exp),
               reads=[r_scr1], writes=[r_TAB])

    def mk_wl(e):
        last = None
        for cc in range(CC):
            last = e.matmul(pb[2][:, cc:cc + 1], lw[:, cc * 128:(cc + 1) * 128],
                            c128[:, :], start=(cc == 0), stop=(cc == CC - 1))
        return last
    S.task("tensor", mk_wl, reads=[r_lw, r_c128], writes=[rb[2]])
    S.task("scalar", lambda e: e.activation(wL[:, :], pb[2][:, 0:CC], AF.Exp),
           reads=[rb[2]], writes=[r_wL])
    S.barrier()
    ctx_pre.close()
    masks, r_masks = mksb(A, "masks", [128, 5, 128], F32)
    dma_in(masks[:, :, :], masks_in, r_masks)
    stg, r_stg = zip(*[mksb(A, f"stg{i}", [128, 1024], F32) for i in range(2)])
    stgs, r_stgs = mksb(A, "stgs", [128, 1024], F32)

    def prep_jobs(pass2):
        jobs = []
        specs = ((0, wgateT, CC, INNER, 1024), (1, wupT, CC, INNER, 1024),
                 (2, wdownT, NIC, C, 1024))
        for m, src, rows, cols, w in specs:
            per_row = cols // w
            for r0 in range(rows):
                for h in range(per_row):
                    c0 = h * w
                    sl = src[r0 * 128:(r0 + 1) * 128, c0:c0 + w]
                    if not pass2:
                        jobs.append((m, sl, None, w))
                    else:
                        if m < 2:
                            dst = gq_d[m, r0, :, c0:c0 + w]
                        else:
                            dst = dq_d[r0, :, c0:c0 + w]
                        jobs.append((m, sl, dst, w))
        return jobs

    prep_state = {"phase": 1, "jobs": prep_jobs(False), "i": 0,
                  "pending": [], "done": False}

    def finish_pass1():
        def sums(e):
            last = None
            for m in range(3):
                last = e.matmul(pb[2][0:1, m:m + 1], asum[:, m:m + 1],
                                onesf_col[:, 0:1], start=(m == 0), stop=(m == 2))
            return last
        S.task("tensor", sums, reads=[r_asum, r_onesf], writes=[rb[2]])
        def mw(e):
            n0 = float(C * INNER)
            e.tensor_scalar(st[3][0:1, 0:3], pb[2][0:1, 0:3], 1.0 / n0, 1e-5,
                            ALU.mult, ALU.max)
            e.engine_nop()
            return e.reciprocal(st[3][0:1, 3:6], st[3][0:1, 0:3])
        S.task("vector", mw, reads=[rb[2]], writes=[r_st[3]])
        S.task("tensor",
               lambda e: e.matmul(pb[2][:, 0:6], ones1[:, :], st[3][0:1, 0:6],
                                  start=True, stop=True),
               reads=[r_st[3], r_ones1], writes=[rb[2]])
        def split(e):
            e.tensor_copy(mgv_m[:, :], pb[2][:, 0:3])
            return e.tensor_copy(mgv[:, 0:3], pb[2][:, 3:6])
        S.task("vector", split, reads=[rb[2]], writes=[r_mgv, r_mgv_m])

    def prep_compute(bi, m, dst, w, phase, act=False):
        if phase == 1:
            S.task("scalar",
                   lambda e, bi=bi, w=w: e.activation(
                       stgs[:, 0:w], stg[bi][:, 0:w], AF.Abs,
                       accum_out=st[3][:, 6:7]),
                   reads=[r_stg[bi]], writes=[r_stgs, r_st[3]])
            S.task("vector",
                   lambda e, m=m: e.tensor_tensor(
                       asum[:, m:m + 1], asum[:, m:m + 1], st[3][:, 6:7],
                       ALU.add),
                   reads=[r_st[3], r_asum], writes=[r_asum])
        elif act:
            S.task("scalar",
                   lambda e, bi=bi, m=m, w=w: e.activation(
                       stg[bi][:, 0:w], stg[bi][:, 0:w], AF.Identity,
                       bias=mgv[:, 3:4], scale=mgv[:, m:m + 1]),
                   reads=[r_stg[bi], r_mgv], writes=[r_stg[bi]])
            S.task("scalar",
                   lambda e, bi=bi, w=w: e.activation(
                       stg[bi][:, 0:w], stg[bi][:, 0:w], AF.Relu,
                       bias=mgv[:, 4:5]),
                   reads=[r_stg[bi], r_mgv], writes=[r_stg[bi]])
            S.task("vector",
                   lambda e, bi=bi, w=w: e.tensor_scalar(
                       stgs[:, :].bitcast(BF16)[:, bi * 1024:bi * 1024 + w],
                       stg[bi][:, 0:w], 2.0, -1.0, ALU.min, ALU.add),
                   reads=[r_stg[bi]], writes=[r_stgs])
            S.task("sync",
                   lambda e, bi=bi, dst=dst, w=w: e.dma_start(
                       dst, stgs[:, :].bitcast(BF16)[:, bi * 1024:bi * 1024 + w]),
                   reads=[r_stgs], writes=[r_gqd if m < 2 else r_dqd],
                   dma=True)
        else:
            def q1(e, bi=bi, m=m, w=w):
                e.tensor_scalar(stgs[:, 0:w], stg[bi][:, 0:w], mgv[:, m:m + 1],
                                MAGIC, ALU.mult, ALU.add)
                e.engine_nop()
                return e.tensor_scalar(stgs[:, 0:w], stgs[:, 0:w], MAGIC, -1.0,
                                       ALU.subtract, ALU.max)
            S.task("vector", q1, reads=[r_stg[bi], r_mgv], writes=[r_stgs])
            S.task("vector",
                   lambda e, bi=bi, w=w: e.tensor_scalar(
                       stg[bi][:, :].bitcast(BF16)[:, 0:w], stgs[:, 0:w], 1.0,
                       None, ALU.min),
                   reads=[r_stgs], writes=[r_stg[bi]])
            S.task("sync",
                   lambda e, bi=bi, dst=dst, w=w: e.dma_start(
                       dst, stg[bi][:, :].bitcast(BF16)[:, 0:w]),
                   reads=[r_stg[bi]], writes=[r_gqd if m < 2 else r_dqd],
                   dma=True)

    def prep_point(act=False):
        ps = prep_state
        if ps["done"]:
            return True
        for bi, m, dst, w, phase in ps["pending"]:
            prep_compute(bi, m, dst, w, phase, act=act)
        ps["pending"] = []
        if ps["i"] >= len(ps["jobs"]):
            if ps["phase"] == 1:
                finish_pass1()
                ps["phase"], ps["jobs"], ps["i"] = 2, prep_jobs(True), 0
            else:
                ps["done"] = True
                return True
        for bi in range(2):
            if ps["i"] < len(ps["jobs"]):
                m, src_, dst, w = ps["jobs"][ps["i"]]
                ps["i"] += 1
                dma_in(stg[bi][:, 0:w], src_, r_stg[bi], engine="gpsimd")
                ps["pending"].append((bi, m, dst, w, ps["phase"]))
        return False

    cstate = {"par": 0, "first": True}
    chctx = {}

    def attn_frontA(kind, ci):
        par = cstate["par"]; cstate["par"] ^= 1
        first = cstate["first"]; cstate["first"] = False
        chctx[(kind, ci)] = (par, first)
        own = (kind == "own")
        xsrc = (xin if own else xpre)[ci * L:(ci + 1) * L, :]
        dma_in(sx[par][:, :], xsrc, r_sx[par], engine="scalar")

        stv, r_stv = st[0], r_st[0]
        dsink = diff[:, :, :]
        def stats(e, par=par):
            e.scalar_tensor_tensor(dsink, sx[par][:, :].rearrange("p (a b) -> p a b", a=CC),
                                   1.0, sx[par][:, :].rearrange("p (a b) -> p a b", a=CC),
                                   ALU.mult, ALU.mult, accum_out=stv[:, 0:1])
            return e.tensor_scalar(dsink, sx[par][:, :].rearrange("p (a b) -> p a b", a=CC),
                                   1.0, 0.0, ALU.mult, ALU.add, accum_out=stv[:, 1:2])
        S.task("vector", stats, reads=[r_sx[par]], writes=[r_diff, r_stv])
        def stats2(e):
            e.tensor_scalar(stv[:, 2:3], stv[:, 1:2], 1.0 / C, None, ALU.mult)
            e.engine_nop()
            e.tensor_tensor(stv[:, 3:4], stv[:, 2:3], stv[:, 2:3], ALU.mult)
            e.engine_nop()
            e.scalar_tensor_tensor(stv[:, 4:5], stv[:, 0:1], 1.0 / C, stv[:, 3:4],
                                   ALU.mult, ALU.subtract)
            e.engine_nop()
            return e.tensor_scalar(stv[:, 4:5], stv[:, 4:5], 1e-5, None, ALU.add)
        S.task("vector", stats2, reads=[r_stv], writes=[r_stv])
        S.task("scalar", lambda e: e.activation(stv[:, 5:6], stv[:, 4:5], AF.Ln),
               reads=[r_stv], writes=[r_stv])
        S.task("scalar", lambda e: e.activation(stv[:, 6:7], stv[:, 5:6], AF.Exp,
                                                scale=-0.5),
               reads=[r_stv], writes=[r_stv])
        def stats4(e):
            e.tensor_tensor(stv[:, 7:8], stv[:, 2:3], stv[:, 6:7], ALU.mult)
            e.engine_nop()
            return e.tensor_scalar(stv[:, 7:8], stv[:, 7:8], -1.0, None, ALU.mult)
        S.task("vector", stats4, reads=[r_stv], writes=[r_stv])
        S.task("scalar",
               lambda e, par=par: e.activation(ln1tm_t[:, 0:C], sx[par][:, :],
                                               AF.Identity, bias=stv[:, 7:8],
                                               scale=stv[:, 6:7]),
               reads=[r_sx[par], r_stv], writes=[r_ln1tm])

    def attn_frontB(kind, ci):
        par, first = chctx[(kind, ci)]
        own = (kind == "own")
        def ln1T(e):
            last = None
            for cc in range(CC):
                last = e.matmul(pbt[:, cc * 128:(cc + 1) * 128],
                                ln1tm_t[:, cc * 128:(cc + 1) * 128], ident[:, :],
                                is_transpose=True, start=(cc == 0), stop=(cc == CC - 1))
            return last
        S.task("tensor", ln1T, reads=[r_ln1tm, r_ident], writes=[rbt])
        if first:
            S.task("vector", lambda e, par=par: e.memset(ln1f[par][:, :, 0:1], 0.0),
                   writes=[r_ln1f[par]])
        else:
            S.task("vector",
                   lambda e, par=par: e.tensor_copy(ln1f[par][:, :, 0:1],
                                                    ln1f[1 - par][:, :, 128:129]),
                   reads=[r_ln1f[1 - par]], writes=[r_ln1f[par]])
        S.task("scalar",
               lambda e, par=par: e.copy(ln1f[par][:, :, 1:129],
                                         pbt[:, :].rearrange("p (a b) -> p a b", a=CC)),
               reads=[rbt], writes=[r_ln1f[par]])
        lcur = ln1f[par][:, :, 1:129]
        lprev = ln1f[par][:, :, 0:128]
        S.task("vector", lambda e: e.tensor_tensor(diff[:, :, :], lcur, lprev,
                                                   ALU.subtract),
               reads=[r_ln1f[par]], writes=[r_diff])
        for mi in (range(4) if own else (1, 2)):
            dst = mixg[par] if mi == 3 else mixo[mi]
            rdst = r_mixg[par] if mi == 3 else r_mixo[mi]
            def domix(e, mi=mi, dst=dst):
                e.tensor_tensor(dst[:, :, :], diff[:, :, :],
                                mixb[:, mi, :].rearrange("p (a b) -> p a b", a=CC),
                                ALU.mult)
                e.engine_nop()
                return e.tensor_tensor(dst[:, :, :], dst[:, :, :], lprev, ALU.add)
            S.task("vector", domix, reads=[r_diff, r_mixb, r_ln1f[par]],
                   writes=[rdst])

    def attn_backSproj(kind, ci):
        own = (kind == "own")
        def proj_tm(mi, wten, b0, b1):
            def fn(e):
                last = None
                for K in range(CC):
                    for nh in range(2):
                        last = e.matmul((pb[b0], pb[b1])[nh][:, :],
                                        mixo[mi][:, K, :],
                                        wten[:, K, nh * 512:(nh + 1) * 512],
                                        start=(K == 0), stop=(K == CC - 1))
                return last
            return fn
        S.task("tensor", proj_tm(1, wk_s, 0, 1), reads=[r_mixo[1], r_wk],
               writes=[rb[0], rb[1]])
        S.task("tensor", proj_tm(2, wv_s, 2, 3), reads=[r_mixo[2], r_wv],
               writes=[rb[2], rb[3]])
        if own:
            S.task("tensor", proj_tm(0, wq_s, 4, 5), reads=[r_mixo[0], r_wq],
                   writes=[rb[4], rb[5]])

    def attn_backSrest(kind, ci):
        par, first = chctx[(kind, ci)]
        own = (kind == "own")

        def rms_sq(pbank0, pbank1, rbank0, rbank1, stcol, r_stc, sink):
            def s1(e):
                e.activation(sink[:, 0:512], pbank0[:, :], AF.Square,
                             accum_out=stcol[:, 0:1])
                return e.activation(sink[:, 512:1024], pbank1[:, :], AF.Square,
                                    accum_out=stcol[:, 1:2])
            S.task("scalar", s1, reads=[rbank0, rbank1], writes=[r_scr1, r_stc])
        rms_sq(pb[0], pb[1], rb[0], rb[1], st[1], r_st[1], scr1)
        if own:
            rms_sq(pb[4], pb[5], rb[4], rb[5], st[2], r_st[2], scr1)
        def rms_add(stcol, r_stc):
            def fn(e):
                e.tensor_tensor(stcol[:, 2:3], stcol[:, 0:1],
                                stcol[:, 1:2], ALU.add)
                e.engine_nop()
                return e.tensor_scalar(stcol[:, 2:3], stcol[:, 2:3], 1e-13,
                                       None, ALU.add)
            S.task("vector", fn, reads=[r_stc], writes=[r_stc])
        rms_add(st[1], r_st[1])
        if own:
            rms_add(st[2], r_st[2])
        def rms_sqrt(stcol, r_stc):
            S.task("scalar",
                   lambda e: e.activation(stcol[:, 3:4], stcol[:, 2:3], AF.Ln,
                                          scale=1.0 / C),
                   reads=[r_stc], writes=[r_stc])
            S.task("scalar",
                   lambda e: e.activation(stcol[:, 4:5], stcol[:, 3:4], AF.Exp,
                                          scale=-0.5),
                   reads=[r_stc], writes=[r_stc])
        rms_sqrt(st[1], r_st[1])
        if own:
            rms_sqrt(st[2], r_st[2])
        def rms_fin(stcol, r_stc, pbank0, pbank1, rbank0, rbank1, dst):
            def ap1(e):
                e.tensor_tensor(dst[:, 0:512], pbank0[:, :], nsc_s[:, 0:512], ALU.mult)
                return e.tensor_tensor(dst[:, 512:1024], pbank1[:, :],
                                       nsc_s[:, 512:1024], ALU.mult)
            S.task("vector", ap1, reads=[rbank0, rbank1, r_nsc], writes=[r_scr3])
        rms_fin(st[1], r_st[1], pb[0], pb[1], rb[0], rb[1], scr3)
        S.task("scalar",
               lambda e: e.activation(kex_tm[:, :], scr3[:, :], AF.Exp,
                                      scale=st[1][:, 4:5]),
               reads=[r_scr3, r_st[1]], writes=[r_kex_tm])
        def vev(e):
            e.copy(v_tm[:, 0:512], pb[2][:, :])
            return e.copy(v_tm[:, 512:1024], pb[3][:, :])
        S.task("scalar", vev, reads=[rb[2], rb[3]], writes=[r_v_tm])
        if own:
            rms_fin(st[2], r_st[2], pb[4], pb[5], rb[4], rb[5], scr1)
            S.task("scalar",
                   lambda e: e.activation(qex_tm[:, :], scr1[:, :], AF.Exp,
                                          scale=st[2][:, 4:5]),
                   reads=[r_scr1, r_st[2]], writes=[r_qex_tm])

        S.task("vector",
               lambda e: e.tensor_tensor(khst[:, :], kex_tm[:, :], TAB[:, TI["kst"], :],
                                         ALU.mult),
               reads=[r_kex_tm, r_TAB], writes=[r_khst])

        if own:
            def proj_g(e):
                last = None
                for Mc in range(CC):
                    bank = pb[6] if Mc < 4 else pb[0]
                    col = (Mc % 4) * 128
                    for K in range(CC):
                        last = e.matmul(bank[:, col:col + 128],
                                        wg_s[:, K, Mc * 128:(Mc + 1) * 128],
                                        mixg[par][:, K, :],
                                        start=(K == 0 and Mc % 4 == 0),
                                        stop=(K == CC - 1 and Mc % 4 == 3))
                return last
            S.task("tensor", proj_g, reads=[r_mixg[par], r_wg], writes=[rb[6], rb[0]])
            def gev(e):
                e.activation(scr3[:, 0:512], pb[6][:, :], AF.Tanh, scale=0.5)
                return e.activation(scr3[:, 512:1024], pb[0][:, :], AF.Tanh, scale=0.5)
            S.task("scalar", gev, reads=[rb[6], rb[0]], writes=[r_scr3])
            S.task("vector",
                   lambda e: e.tensor_scalar(scr3[:, :], scr3[:, :], 0.5, 0.5,
                                             ALU.mult, ALU.add),
                   reads=[r_scr3], writes=[r_scr3])

            def trT(src):
                def fn(e):
                    last = None
                    for cc in range(CC):
                        last = e.matmul(pbt[:, cc * 128:(cc + 1) * 128],
                                        src[:, cc * 128:(cc + 1) * 128], ident[:, :],
                                        is_transpose=True, start=(cc == 0),
                                        stop=(cc == CC - 1))
                    return last
                return fn
            S.task("tensor", trT(qex_tm), reads=[r_qex_tm, r_ident], writes=[rbt])
            S.task("scalar", lambda e: e.copy(qex_fm[:, :], pbt[:, :]),
                   reads=[rbt], writes=[r_qex_fm])
            S.task("tensor", trT(kex_tm), reads=[r_kex_tm, r_ident], writes=[rbt])
            S.task("scalar", lambda e: e.copy(kex_fm[:, :], pbt[:, :]),
                   reads=[rbt], writes=[r_kex_fm])
            S.task("vector",
                   lambda e: e.tensor_tensor(qtint[:, :], qex_fm[:, :],
                                             TAB[:, TI["qint"], :], ALU.mult),
                   reads=[r_qex_fm, r_TAB], writes=[r_qtint])

            mkp = {"i": 0}
            def makes(qn, kn):
                mp = mkp["i"] & 1
                mkp["i"] += 1
                def fn(e, qn=qn, kn=kn, mp=mp):
                    e.tensor_tensor(mkq[mp][:, :], qex_fm[:, :], TAB[:, TI[qn], :],
                                    ALU.mult)
                    return e.tensor_tensor(mkk[mp][:, :], kex_fm[:, :],
                                           TAB[:, TI[kn], :], ALU.mult)
                S.task("vector", fn, reads=[r_qex_fm, r_kex_fm, r_TAB],
                       writes=[r_mkq[mp], r_mkk[mp]])
                return mp
            for (qn, kn, bank) in (("q8", "k8", 2), ("q8o", "k8o", 4),
                                   ("q16", "k16", 5), ("q32", "k32", 3),
                                   ("q64", "k64", 6)):
                mp = makes(qn, kn)
                def pmm(e, bank=bank, mp=mp):
                    last = None
                    for h in range(H):
                        for dc in range(2):
                            cc = 2 * h + dc
                            last = e.matmul(pb[bank][:, h * 128:(h + 1) * 128],
                                            mkk[mp][:, cc * 128:(cc + 1) * 128],
                                            mkq[mp][:, cc * 128:(cc + 1) * 128],
                                            start=(h == 0 and dc == 0),
                                            stop=(h == H - 1 and dc == 1))
                    return last
                S.task("tensor", pmm, reads=[r_mkq[mp], r_mkk[mp]], writes=[rb[bank]])

            def mk_m(m):
                return masks[:, m:m + 1, :].broadcast_to((128, 4, 128))
            def r3(t):
                return t.rearrange("p (a b) -> p a b", a=4)
            def pev1(e):
                sa, sb_ = scr1[:, 0:512], scr1[:, 512:1024]
                e.tensor_tensor(r3(sa), r3(pb[2][:, :]), mk_m(0), ALU.mult)
                e.tensor_tensor(r3(sb_), r3(pb[4][:, :]), mk_m(1), ALU.mult)
                e.engine_nop()
                e.tensor_tensor(sa, sa, sb_, ALU.add)
                e.tensor_tensor(r3(sb_), r3(pb[5][:, :]), mk_m(2), ALU.mult)
                e.engine_nop()
                e.tensor_tensor(sa, sa, sb_, ALU.add)
                e.tensor_tensor(r3(sb_), r3(pb[3][:, :]), mk_m(3), ALU.mult)
                e.engine_nop()
                e.tensor_tensor(sa, sa, sb_, ALU.add)
                e.tensor_tensor(r3(sb_), r3(pb[6][:, :]), mk_m(4), ALU.mult)
                e.engine_nop()
                return e.tensor_tensor(Psb[:, :], sa, sb_, ALU.add)
            S.task("vector", pev1, reads=[rb[2], rb[3], rb[4], rb[5], rb[6], r_masks],
                   writes=[r_Psb, r_scr1])

            def smm(bank, ccs):
                def fn(e, bank=bank, ccs=ccs):
                    last = None
                    n = len(ccs) * 3
                    i = 0
                    for cc in ccs:
                        h, ec = cc // 2, cc % 2
                        col = (cc % 4) * 128
                        last = e.matmul(pb[bank][:, col:col + 128],
                                        v_tm[:, cc * 128:(cc + 1) * 128],
                                        Psb[:, h * 128:(h + 1) * 128],
                                        start=(i == 0), stop=False)
                        i += 1
                        for dc in range(2):
                            ci2 = 2 * h + dc
                            last = e.matmul(pb[bank][:, col:col + 128],
                                            Abf[:, ci2, ec * 128:(ec + 1) * 128],
                                            qtint[:, ci2 * 128:(ci2 + 1) * 128],
                                            start=False, stop=(i == n - 1))
                            i += 1
                    return last
                return fn
            S.task("tensor", smm(4, [0, 1, 2, 3]),
                   reads=[r_v_tm, r_Psb, r_Abf, r_qtint], writes=[rb[4]])
            S.task("tensor", smm(5, [4, 5, 6, 7]),
                   reads=[r_v_tm, r_Psb, r_Abf, r_qtint], writes=[rb[5]])
            def zmm(e):
                last = None
                n = H * 3
                i = 0
                for h in range(H):
                    last = e.matmul(pb[6][0:1, h * 128:(h + 1) * 128],
                                    onesc_bf[:, :], Psb[:, h * 128:(h + 1) * 128],
                                    start=(i == 0), stop=False)
                    i += 1
                    for dc in range(2):
                        ci2 = 2 * h + dc
                        last = e.matmul(pb[6][0:1, h * 128:(h + 1) * 128],
                                        Abf[:, ci2, 256:257],
                                        qtint[:, ci2 * 128:(ci2 + 1) * 128],
                                        start=False, stop=(i == n - 1))
                        i += 1
                return last
            S.task("tensor", zmm, reads=[r_onesc, r_Psb, r_Abf, r_qtint],
                   writes=[rb[6]])


            def epi1(e):
                last = None
                for cc in range(CC):
                    bank = pb[4] if cc < 4 else pb[5]
                    col = (cc % 4) * 128
                    last = e.scalar_tensor_tensor(
                        attf[:, cc, :], bank[:, col:col + 128], 1.0,
                        scr3[:, cc * 128:(cc + 1) * 128], ALU.mult, ALU.mult)
                return last
            S.task("vector", epi1, reads=[rb[4], rb[5], r_scr3], writes=[r_attf])
            S.task("scalar", lambda e: e.copy(rzrow[:, :], pb[6][0:1, 0:512]),
                   reads=[rb[6]], writes=[r_rzrow])

        def damm(e):
            last = None
            for b in range(CC):
                h = b // 2
                last = e.matmul(pb[b // 2][:, (b % 2) * 256:(b % 2) * 256 + 256],
                                khst[:, b * 128:(b + 1) * 128],
                                v_tm[:, h * 256:(h + 1) * 256],
                                start=(b % 2 == 0), stop=(b % 2 == 1))
            return last
        S.task("tensor", damm, reads=[r_khst, r_v_tm],
               writes=[rb[0], rb[1], rb[2], rb[3]])
        def dzmm(e):
            last = None
            for b in range(CC):
                last = e.matmul(pb[6][:, b:b + 1], khst[:, b * 128:(b + 1) * 128],
                                onesc_bf[:, :], start=(b == 0), stop=(b == CC - 1))
            return last
        S.task("tensor", dzmm, reads=[r_khst, r_onesc], writes=[rb[6]])
        if own:
            S.task("tensor",
                   lambda e: e.matmul(pb[4][:, :], ones1[:, :], rzrow[:, :],
                                      start=True, stop=True),
                   reads=[r_rzrow, r_ones1], writes=[rb[4]])
            S.task("vector", lambda e: e.reciprocal(rzbc[:, :], pb[4][:, :]),
                   reads=[rb[4]], writes=[r_rzbc])
        def upd(e):
            last = None
            for b in range(CC):
                last = e.scalar_tensor_tensor(
                    Ast[:, b, 0:256], Ast[:, b, 0:256], wL[:, b:b + 1],
                    pb[b // 2][:, (b % 2) * 256:(b % 2) * 256 + 256],
                    ALU.mult, ALU.add)
                last = e.scalar_tensor_tensor(
                    Ast[:, b, 256:257], Ast[:, b, 256:257], wL[:, b:b + 1],
                    pb[6][:, b:b + 1], ALU.mult, ALU.add)
            return last
        if own:
            def epi2(e):
                rz3 = rzbc[:, :].rearrange("p (h i) -> p h i", h=H)
                rzb = rz3.unsqueeze(2).broadcast_to((128, H, 2, 128))
                a4 = attf[:, :, :].rearrange("p (h d) i -> p h d i", h=H)
                return e.tensor_tensor(a4, a4, rzb, ALU.mult)
            S.task("vector", epi2, reads=[r_rzbc, r_attf], writes=[r_attf])
            def womm(e):
                last = None
                for K in range(CC):
                    for nh in range(2):
                        last = e.matmul((pb[4], pb[5])[nh][:, :], attf[:, K, :],
                                        wo_s[:, K, nh * 512:(nh + 1) * 512],
                                        start=(K == 0), stop=(K == CC - 1))
                return last
            S.task("tensor", womm, reads=[r_attf, r_wo], writes=[rb[4], rb[5]])
        S.task("vector", upd, reads=[rb[0], rb[1], rb[2], rb[3], rb[6], r_wL],
               writes=[r_Ast])
        S.task("scalar", lambda e: e.copy(Abf[:, :, :], Ast[:, :, :]),
               reads=[r_Ast], writes=[r_Abf])

    def attn_backE_B(kind, ci):
        par, first = chctx[(kind, ci)][:2]
        if True:
            stv2, r_stv2 = stL, r_stL
            def l2sq(e):
                e.activation(scr1[:, 0:512], pb[4][:, :], AF.Square,
                             accum_out=stv2[:, 0:1])
                return e.activation(scr1[:, 512:1024], pb[5][:, :], AF.Square,
                                    accum_out=stv2[:, 1:2])
            S.task("scalar", l2sq, reads=[rb[4], rb[5]], writes=[r_scr1, r_stv2])
            def l2s(e):
                e.tensor_scalar(scr3[:, 0:512], pb[4][:, :], 1.0, 0.0, ALU.mult,
                                ALU.add, accum_out=stv2[:, 2:3])
                return e.tensor_scalar(scr3[:, 512:1024], pb[5][:, :], 1.0, 0.0,
                                       ALU.mult, ALU.add, accum_out=stv2[:, 3:4])
            S.task("vector", l2s, reads=[rb[4], rb[5], r_stv2],
                   writes=[r_scr3, r_stv2])
            def l2s2(e):
                e.tensor_tensor(stv2[:, 0:1], stv2[:, 0:1], stv2[:, 1:2], ALU.add)
                e.tensor_tensor(stv2[:, 2:3], stv2[:, 2:3], stv2[:, 3:4], ALU.add)
                e.engine_nop()
                e.tensor_scalar(stv2[:, 4:5], stv2[:, 2:3], 1.0 / C, None, ALU.mult)
                e.engine_nop()
                e.tensor_tensor(stv2[:, 5:6], stv2[:, 4:5], stv2[:, 4:5], ALU.mult)
                e.engine_nop()
                e.scalar_tensor_tensor(stv2[:, 6:7], stv2[:, 0:1], 1.0 / C,
                                       stv2[:, 5:6], ALU.mult, ALU.subtract)
                e.engine_nop()
                return e.tensor_scalar(stv2[:, 6:7], stv2[:, 6:7], 1e-5, None, ALU.add)
            S.task("vector", l2s2, reads=[r_stv2], writes=[r_stv2])
            S.task("scalar",
                   lambda e: e.activation(stv2[:, 7:8], stv2[:, 6:7], AF.Ln),
                   reads=[r_stv2], writes=[r_stv2])
            S.task("scalar",
                   lambda e: e.activation(stv2[:, 6:7], stv2[:, 7:8], AF.Exp,
                                          scale=-0.5),
                   reads=[r_stv2], writes=[r_stv2])
            def l2s3(e):
                e.tensor_tensor(stv2[:, 5:6], stv2[:, 4:5], stv2[:, 6:7], ALU.mult)
                e.engine_nop()
                return e.tensor_scalar(stv2[:, 5:6], stv2[:, 5:6], -1.0, None,
                                       ALU.mult)
            S.task("vector", l2s3, reads=[r_stv2], writes=[r_stv2])
            S.task("vector",
                   lambda e, par=par: e.tensor_scalar(sx[par][:, :], sx[par][:, :],
                                                      stv2[:, 5:6], None, ALU.add),
                   reads=[r_stv2], writes=[r_sx[par]])
            def l2b(e, par=par):
                e.scalar_tensor_tensor(sx[par][:, 0:512], pb[4][:, :], stv2[:, 6:7],
                                       sx[par][:, 0:512], ALU.mult, ALU.add)
                return e.scalar_tensor_tensor(sx[par][:, 512:1024], pb[5][:, :],
                                              stv2[:, 6:7], sx[par][:, 512:1024],
                                              ALU.mult, ALU.add)
            S.task("vector", l2b, reads=[rb[4], rb[5], r_stv2, r_sx[par]],
                   writes=[r_sx[par]])
            S.task("sync",
                   lambda e, ci=ci, par=par: e.dma_start(
                       xmid_d[ci * L:(ci + 1) * L, :], sx[par][:, :]),
                   reads=[r_sx[par]], writes=[r_xmidd], dma=True)


    def emit_state_mask():
        S.task("vector",
               lambda e: e.tensor_scalar(Ast[:, :, :], Ast[:, :, :], flag_s[:, 0:1],
                                         None, ALU.mult),
               reads=[r_flag], writes=[r_Ast])
        S.task("scalar", lambda e: e.copy(Abf[:, :, :], Ast[:, :, :]),
               reads=[r_Ast], writes=[r_Abf])

    chunks = [("pre", i) for i in range(n_pre)] + [("own", i) for i in range(n_own)]
    prep_done = prep_point()
    attn_frontA(*chunks[0])
    attn_frontB(*chunks[0])
    prev = None
    for idx, ch in enumerate(chunks):
        if prev is not None and prev[0] == "own":
            attn_backE_B(*prev)
        if idx == n_pre and n_pre > 0:
            emit_state_mask()
        prep_done = prep_point(act=True)
        if idx + 1 < len(chunks):
            attn_frontA(*chunks[idx + 1])
        prep_done = prep_point()
        attn_backSproj(*ch)
        if idx + 1 < len(chunks):
            attn_frontB(*chunks[idx + 1])
        prep_done = prep_point()
        attn_backSrest(*ch)
        prep_done = prep_point(act=True)
        prev = ch
    if prev[0] == "own":
        attn_backE_B(*prev)
    while not prep_done:
        prep_done = prep_point()

    tap_reg.update({
        "sx0": (sx[cstate["par"] ^ 1][:, :], r_sx[cstate["par"] ^ 1], "attn"),
        "stv0": (st[0][:, :], r_st[0], "attn"),
        "ln1tm": (ln1tm_t[:, :], r_ln1tm, "attn"),
        "scr1t": (scr1[:, :], r_scr1, "attn"),
        "TAB": (TAB[:, :, :], r_TAB, "attn"),
        "wLs": (wL[:, :], r_wL, "attn"),
        "ln1f": (ln1f[cstate["par"] ^ 1][:, :, :], r_ln1f[cstate["par"] ^ 1], "attn"),
        "mixo0": (mixo[0][:, :, :], r_mixo[0], "attn"),
        "mixo1": (mixo[1][:, :, :], r_mixo[1], "attn"),
        "kex": (kex_tm[:, :], r_kex_tm, "attn"),
        "qex": (qex_tm[:, :], r_qex_tm, "attn"),
        "vtm": (v_tm[:, :], r_v_tm, "attn"),
        "khst": (khst[:, :], r_khst, "attn"),
        "qexf": (qex_fm[:, :], r_qex_fm, "attn"),
        "kexf": (kex_fm[:, :], r_kex_fm, "attn"),
        "qti": (qtint[:, :], r_qtint, "attn"),
        "Psb": (Psb[:, :], r_Psb, "attn"),
        "attf": (attf[:, :, :], r_attf, "attn"),
        "Ast": (Ast[:, :, :], r_Ast, "attn"),
        "gfm": (scr3[:, :], r_scr3, "attn"),
        "nscs": (nsc_s[:, :], r_nsc, "attn"),
        "mixb": (mixb[:, :, :], r_mixb, "attn"),
    })
    emit_taps("attn")

    S.barrier()
    ctx_attn.close()
    ctx_a = ExitStack()
    wgu, r_wgu = mksb(ctx_a, "wgu", [128, 2, CC, INNER], BF16)
    hq, r_hq = mksb(ctx_a, "hq", [128, INNER], F32)
    hqb, r_hqb = zip(*[mksb(ctx_a, f"hqb{i}", [128, INNER], BF16) for i in range(2)])
    aq1, r_aq1 = mksb(ctx_a, "aq1", [128, C], BF16)
    aq1f, r_aq1f = zip(*[mksb(ctx_a, f"aq1f{i}", [128, C], BF16) for i in range(2)])
    sxm, r_sxm = zip(*[mksb(ctx_a, f"sxm{i}", [128, C], F32) for i in range(2)])
    fsc1, r_fsc1 = mksb(ctx_a, "fsc1", [128, C], F32)
    fsc2, r_fsc2 = mksb(ctx_a, "fsc2", [128, C], F32)
    fst2, r_fst2 = zip(*[mksb(ctx_a, f"fst{i}", [128, 12], F32) for i in range(2)])
    sil, r_sil = zip(*[mksb(ctx_a, f"sil{i}", [128, 512], F32) for i in range(2)])
    hsp, r_hsp = zip(*[mksb(ctx_a, f"hsp{i}", [128, 1024], BF16) for i in range(2)])

    r_wgu_s = [Res(f"wgu_s{s}") for s in range(NSL)]
    for s in range(NSL):
        for m in range(2):
            S.task("sync",
                   lambda e, m=m, s=s: e.dma_start(
                       wgu[:, m, :, s * 512:(s + 1) * 512],
                       gq_d[m, :, :, s * 512:(s + 1) * 512].transpose([1, 0, 2])),
                   reads=[r_gqd], writes=[r_wgu_s[s]], dma=True)

    def ffnA_frontA(ci):
        par = ci % 2
        fst, r_fst = fst2[par], r_fst2[par]
        S.task("scalar",
               lambda e, par=par, ci=ci: e.dma_start(sxm[par][:, :],
                                                     xmid_d[ci * L:(ci + 1) * L, :]),
               reads=[r_xmidd], writes=[r_sxm[par]], dma=True)
        def f1(e, par=par):
            e.scalar_tensor_tensor(fsc1[:, :], sxm[par][:, :], 1.0, sxm[par][:, :],
                                   ALU.mult, ALU.mult, accum_out=fst[:, 0:1])
            return e.tensor_scalar(fsc2[:, :], sxm[par][:, :], 1.0, 0.0, ALU.mult,
                                   ALU.add, accum_out=fst[:, 1:2])
        S.task("vector", f1, reads=[r_sxm[par]], writes=[r_fsc1, r_fsc2, r_fst])
        def f2(e):
            e.tensor_scalar(fst[:, 2:3], fst[:, 1:2], 1.0 / C, None, ALU.mult)
            e.engine_nop()
            e.tensor_tensor(fst[:, 3:4], fst[:, 2:3], fst[:, 2:3], ALU.mult)
            e.engine_nop()
            e.scalar_tensor_tensor(fst[:, 4:5], fst[:, 0:1], 1.0 / C, fst[:, 3:4],
                                   ALU.mult, ALU.subtract)
            e.engine_nop()
            return e.tensor_scalar(fst[:, 4:5], fst[:, 4:5], 1e-5, None, ALU.add)
        S.task("vector", f2, reads=[r_fst], writes=[r_fst])
        S.task("scalar", lambda e: e.activation(fst[:, 5:6], fst[:, 4:5], AF.Ln),
               reads=[r_fst], writes=[r_fst])
        S.task("scalar", lambda e: e.activation(fst[:, 6:7], fst[:, 5:6], AF.Exp,
                                                scale=-0.5),
               reads=[r_fst], writes=[r_fst])
        def f3(e):
            e.tensor_tensor(fst[:, 7:8], fst[:, 2:3], fst[:, 6:7], ALU.mult)
            e.engine_nop()
            return e.tensor_scalar(fst[:, 7:8], fst[:, 7:8], -1.0, None, ALU.mult)
        S.task("vector", f3, reads=[r_fst], writes=[r_fst])
        S.task("scalar",
               lambda e, par=par: e.activation(fsc1[:, :], sxm[par][:, :], AF.Identity,
                                               bias=fst[:, 7:8], scale=fst[:, 6:7]),
               reads=[r_sxm[par], r_fst], writes=[r_fsc1])
        def f4(e):
            return e.reduce_max(fst[:, 8:9], fsc1[:, :], AX.X,
                                apply_absolute_value=True)
        S.task("vector", f4, reads=[r_fsc1], writes=[r_fst])
        def f5(e):
            e.tensor_scalar(fst[:, 8:9], fst[:, 8:9], 1e-5, None, ALU.max)
            e.engine_nop()
            e.reciprocal(fst[:, 9:10], fst[:, 8:9])
            e.engine_nop()
            return e.tensor_scalar(fst[:, 9:10], fst[:, 9:10], 127.0, None, ALU.mult)
        S.task("vector", f5, reads=[r_fst], writes=[r_fst])
        def f6(e):
            e.tensor_scalar(fsc2[:, :], fsc1[:, :], fst[:, 9:10], MAGIC, ALU.mult,
                            ALU.add)
            e.engine_nop()
            return e.tensor_scalar(aq1[:, :], fsc2[:, :], MAGIC, None, ALU.subtract)
        S.task("vector", f6, reads=[r_fsc1, r_fst], writes=[r_fsc2, r_aq1])
        def f7(e):
            e.scalar_tensor_tensor(fst[:, 10:11], mgv_m[:, 0:1], 1.0 / 127.0,
                                   fst[:, 8:9], ALU.mult, ALU.mult)
            e.scalar_tensor_tensor(fst[:, 11:12], mgv_m[:, 1:2], 1.0 / 127.0,
                                   fst[:, 8:9], ALU.mult, ALU.mult)
            return e.memset(fst[:, 0:1], 0.0)
        S.task("vector", f7, reads=[r_mgv_m, r_fst], writes=[r_fst])

    def ffnA_frontB(ci):
        par = ci % 2
        def aqT(e):
            last = None
            for cc in range(CC):
                last = e.matmul(pbt[:, cc * 128:(cc + 1) * 128],
                                aq1[:, cc * 128:(cc + 1) * 128], ident[:, :],
                                is_transpose=True, start=(cc == 0), stop=(cc == CC - 1))
            return last
        S.task("tensor", aqT, reads=[r_aq1, r_ident], writes=[rbt])
        S.task("scalar", lambda e, par=par: e.copy(aq1f[par][:, :], pbt[:, :]),
               reads=[rbt], writes=[r_aq1f[par]])

    def ffnA_backG(ci):
        par = ci % 2
        fst, r_fst = fst2[par], r_fst2[par]
        for s in range(NSL):
            bg, bu = (0, 1) if s % 2 == 0 else (2, 3)
            def gumm(e, s=s, bg=bg, bu=bu, par=par):
                last = None
                for K in range(CC):
                    last = e.matmul(pb[bg][:, :], aq1f[par][:, K * 128:(K + 1) * 128],
                                    wgu[:, 0, K, s * 512:(s + 1) * 512],
                                    start=(K == 0), stop=(K == CC - 1))
                for K in range(CC):
                    last = e.matmul(pb[bu][:, :], aq1f[par][:, K * 128:(K + 1) * 128],
                                    wgu[:, 1, K, s * 512:(s + 1) * 512],
                                    start=(K == 0), stop=(K == CC - 1))
                return last
            S.task("tensor", gumm, reads=[r_aq1f[par], r_wgu_s[s]],
                   writes=[rb[bg], rb[bu]])
            sp = s % 2
            S.task("scalar",
                   lambda e, bg=bg, sp=sp: e.activation(sil[sp][:, :], pb[bg][:, :],
                                                        AF.Silu, scale=fst[:, 10:11]),
                   reads=[rb[bg], r_fst], writes=[r_sil[sp]])
            S.task("vector",
                   lambda e, bu=bu, sp=sp, s=s: e.scalar_tensor_tensor(
                       hq[:, s * 512:(s + 1) * 512], pb[bu][:, :], 1.0, sil[sp][:, :],
                       ALU.mult, ALU.mult),
                   reads=[rb[bu], r_sil[sp]], writes=[r_hq])
            def hmax(e, s=s):
                e.reduce_max(fst[:, 1:2], hq[:, s * 512:(s + 1) * 512], AX.X,
                             apply_absolute_value=True)
                e.engine_nop()
                return e.tensor_tensor(fst[:, 0:1], fst[:, 0:1], fst[:, 1:2], ALU.max)
            S.task("vector", hmax, reads=[r_hq, r_fst], writes=[r_fst])
        def fh(e, ci=ci):
            e.tensor_tensor(fst[:, 2:3], fst[:, 11:12], fst[:, 0:1], ALU.mult)
            e.engine_nop()
            e.tensor_scalar(fst[:, 2:3], fst[:, 2:3], 1e-5, None, ALU.max)
            e.engine_nop()
            e.reciprocal(fst[:, 3:4], fst[:, 2:3])
            e.engine_nop()
            e.tensor_scalar(fst[:, 3:4], fst[:, 3:4], 127.0, None, ALU.mult)
            e.engine_nop()
            e.tensor_tensor(fst[:, 4:5], fst[:, 11:12], fst[:, 3:4], ALU.mult)
            e.engine_nop()
            return e.scalar_tensor_tensor(alpha[:, ci:ci + 1], mgv_m[:, 2:3],
                                          1.0 / 127.0, fst[:, 2:3], ALU.mult, ALU.mult)
        S.task("vector", fh, reads=[r_fst, r_mgv_m], writes=[r_fst, r_alpha])
        def qh(e, par=par):
            e.tensor_scalar(hq[:, :], hq[:, :], fst[:, 4:5], MAGIC, ALU.mult, ALU.add)
            e.engine_nop()
            return e.tensor_scalar(hqb[par][:, :], hq[:, :], MAGIC, None, ALU.subtract)
        S.task("vector", qh, reads=[r_hq, r_fst], writes=[r_hq, r_hqb[par]])

    def ffnA_tail(ci):
        par = ci % 2
        for grp in range(4):
            gp = grp % 2
            def hT(e, grp=grp, par=par):
                last = None
                for tile in range(8):
                    ic = grp * 8 + tile
                    last = e.matmul(pbt[:, tile * 128:(tile + 1) * 128],
                                    hqb[par][:, ic * 128:(ic + 1) * 128], ident[:, :],
                                    is_transpose=True, start=(tile == 0),
                                    stop=(tile == 7))
                return last
            S.task("tensor", hT, reads=[r_hqb[par], r_ident], writes=[rbt])
            S.task("scalar", lambda e, gp=gp: e.copy(hsp[gp][:, :], pbt[:, :]),
                   reads=[rbt], writes=[r_hsp[gp]])
            S.task("sync",
                   lambda e, ci=ci, grp=grp, gp=gp: e.dma_start(hq_d[ci, grp, :, :],
                                                                hsp[gp][:, :]),
                   reads=[r_hsp[gp]], writes=[r_hqd], dma=True)

    ffnA_frontA(0)
    ffnA_frontB(0)
    for ci in range(n_own):
        if ci + 1 < n_own:
            ffnA_frontA(ci + 1)
        ffnA_backG(ci)
        if ci + 1 < n_own:
            ffnA_frontB(ci + 1)
        if ci > 0:
            ffnA_tail(ci - 1)
    ffnA_tail(n_own - 1)

    S.barrier()
    ctx_a.close()
    ctx_b = ExitStack()
    wdq, r_wdq = mksb(ctx_b, "wdq", [128, NIC, C], BF16)
    hqt, r_hqt = zip(*[mksb(ctx_b, f"hqt{i}", [128, INNER], BF16) for i in range(2)])
    sxb, r_sxb = zip(*[mksb(ctx_b, f"sxb{i}", [128, C], F32) for i in range(2)])
    dsc, r_dsc = mksb(ctx_b, "dsc", [128, C], F32)
    bsc, r_bsc = mksb(ctx_b, "bsc", [128, C], F32)
    yb, r_yb = zip(*[mksb(ctx_b, f"yb{i}", [128, C], F32) for i in range(2)])
    bst, r_bst = mksb(ctx_b, "bst", [128, 10], F32)

    r_wdq_g = [Res(f"wdq_g{g}") for g in range(4)]
    for g in range(4):
        S.task("sync",
               lambda e, g=g: e.dma_start(
                   wdq[:, g * 8:(g + 1) * 8, :],
                   dq_d[g * 8:(g + 1) * 8, :, :].transpose([1, 0, 2])),
               reads=[r_dqd], writes=[r_wdq_g[g]], dma=True)

    def emit_ffnB_chunk(ci):
        par = ci % 2
        for g in range(4):
            def dmm(e, par=par, g=g):
                last = None
                for K in range(g * 8, (g + 1) * 8):
                    for nh in range(2):
                        last = e.matmul((pb[0], pb[1])[nh][:, :],
                                        hqt[par][:, K * 128:(K + 1) * 128],
                                        wdq[:, K, nh * 512:(nh + 1) * 512],
                                        start=(K == 0),
                                        stop=(K == NIC - 1))
                return last
            S.task("tensor", dmm, reads=[r_hqt[par], r_wdq_g[g]],
                   writes=[rb[0], rb[1]])
        def b1(e, ci=ci):
            e.tensor_scalar(dsc[:, 0:512], pb[0][:, :], alpha[:, ci:ci + 1], 0.0,
                            ALU.mult, ALU.add, accum_out=bst[:, 0:1])
            return e.tensor_scalar(dsc[:, 512:1024], pb[1][:, :], alpha[:, ci:ci + 1],
                                   0.0, ALU.mult, ALU.add, accum_out=bst[:, 1:2])
        S.task("vector", b1, reads=[rb[0], rb[1], r_alpha], writes=[r_dsc, r_bst])
        S.task("scalar",
               lambda e: e.activation(bsc[:, :], dsc[:, :], AF.Square,
                                      accum_out=bst[:, 2:3]),
               reads=[r_dsc], writes=[r_bsc, r_bst])
        def b3(e):
            e.tensor_tensor(bst[:, 0:1], bst[:, 0:1], bst[:, 1:2], ALU.add)
            e.engine_nop()
            e.tensor_scalar(bst[:, 3:4], bst[:, 0:1], 1.0 / C, None, ALU.mult)
            e.engine_nop()
            e.tensor_tensor(bst[:, 4:5], bst[:, 3:4], bst[:, 3:4], ALU.mult)
            e.engine_nop()
            e.scalar_tensor_tensor(bst[:, 5:6], bst[:, 2:3], 1.0 / C, bst[:, 4:5],
                                   ALU.mult, ALU.subtract)
            e.engine_nop()
            return e.tensor_scalar(bst[:, 5:6], bst[:, 5:6], 1e-5, None, ALU.add)
        S.task("vector", b3, reads=[r_bst], writes=[r_bst])
        S.task("scalar", lambda e: e.activation(bst[:, 6:7], bst[:, 5:6], AF.Ln),
               reads=[r_bst], writes=[r_bst])
        S.task("scalar", lambda e: e.activation(bst[:, 7:8], bst[:, 6:7], AF.Exp,
                                                scale=-0.5),
               reads=[r_bst], writes=[r_bst])
        def b4(e):
            e.tensor_tensor(bst[:, 8:9], bst[:, 3:4], bst[:, 7:8], ALU.mult)
            e.engine_nop()
            return e.tensor_scalar(bst[:, 8:9], bst[:, 8:9], -1.0, None, ALU.mult)
        S.task("vector", b4, reads=[r_bst], writes=[r_bst])
        S.task("scalar",
               lambda e, par=par: e.activation(yb[par][:, :], sxb[par][:, :], AF.Identity,
                                               bias=bst[:, 8:9]),
               reads=[r_sxb[par], r_bst], writes=[r_yb[par]])
        S.task("vector",
               lambda e, par=par: e.scalar_tensor_tensor(yb[par][:, :], dsc[:, :],
                                                         bst[:, 7:8], yb[par][:, :],
                                                         ALU.mult, ALU.add),
               reads=[r_dsc, r_bst, r_yb[par]], writes=[r_yb[par]])
        S.task("sync",
               lambda e, par=par, ci=ci: e.dma_start(y[ci * L:(ci + 1) * L, :],
                                                     yb[par][:, :]),
               reads=[r_yb[par]], dma=True)

    def ffnB_front(ci):
        par = ci % 2
        S.task("scalar",
               lambda e, par=par, ci=ci: e.dma_start(
                   hqt[par][:, :].rearrange("p (g q) -> p g q", g=4),
                   hq_d[ci, :, :, :].transpose([1, 0, 2])),
               reads=[r_hqd], writes=[r_hqt[par]], dma=True)
        S.task("scalar",
               lambda e, par=par, ci=ci: e.dma_start(sxb[par][:, :],
                                                     xmid_d[ci * L:(ci + 1) * L, :]),
               reads=[r_xmidd], writes=[r_sxb[par]], dma=True)

    ffnB_front(0)
    for ci in range(n_own):
        if ci + 1 < n_own:
            ffnB_front(ci + 1)
        emit_ffnB_chunk(ci)

    with nc.Block() as block:
        S.emit(nc, block)
    ctx_b.close()
    ctx_common.close()
    return nc


def _host_constants():
    p = np.arange(128, dtype=np.float32)
    pos = {
        "q8": (p % 8) - 3.5, "k8": 3.5 - (p % 8),
        "q8o": (p % 16) - 7.5, "k8o": 7.5 - (p % 16),
        "q16": (p % 32) - 15.5, "k16": 15.5 - (p % 32),
        "q32": (p % 32) + 0.5, "k32": 31.5 - (p % 32),
        "q64": (p % 64) + 0.5, "k64": 63.5 - p,
        "qint": p + 1.0, "kst": 127.0 - p,
    }
    posv = np.stack([pos[k] for k in POS_ORDER])[None, :, :].astype(np.float32)
    j = np.arange(128)[:, None]
    i = np.arange(128)[None, :]
    m8 = ((j // 8 == i // 8) & (j <= i)).astype(np.float32)
    m8o = ((j // 16 == i // 16) & ((i // 8) % 2 == 1)
           & ((j // 8) % 2 == 0)).astype(np.float32)
    m16 = ((j // 32 == i // 32) & ((i // 16) % 2 == 1)
           & ((j // 16) % 2 == 0)).astype(np.float32)
    m32 = ((j // 64 == i // 64) & ((i // 32) % 2 == 1)
           & ((j // 32) % 2 == 0)).astype(np.float32)
    m64 = ((j < 64) & (i >= 64)).astype(np.float32)
    masks = np.stack([m8, m8o, m16, m32, m64], axis=1).astype(np.float32)
    ident = np.eye(128, dtype=BF)
    return posv, masks, ident


_CACHE = {}


def _get_program(n_own=NOWN, n_pre=NPRE, debug_taps=()):
    key = (n_own, n_pre, tuple(debug_taps))
    if key not in _CACHE:
        _CACHE[key] = build_program(n_own, n_pre, debug_taps)
    return _CACHE[key]


def make_in_maps(inputs, n_own=NOWN, n_pre=NPRE, n_cores=8):
    x = np.asarray(inputs["L0_attn_x"], dtype=np.float32)
    td = np.asarray(inputs["L0_attn_time_decay"], dtype=np.float32).reshape(1, C)
    nsc = np.asarray(inputs["L0_attn_norm_scale"], dtype=np.float32)
    mixes = [np.asarray(inputs["L0_attn_mix_" + n], dtype=np.float32).reshape(C)
             for n in ("q", "k", "v", "g")]
    mix_bc = np.stack([np.repeat(m.reshape(CC, 128).T[:, :, None], 128, axis=2)
                       .reshape(128, C) for m in mixes], axis=1).astype(np.float32)
    nsc_bc = np.ascontiguousarray(np.broadcast_to(nsc.reshape(1, C), (128, C)))
    posv, masks, ident = _host_constants()

    wT = {k: np.ascontiguousarray(np.asarray(inputs[f"L0_attn_W{k}"],
                                             dtype=np.float32).T)
          for k in ("q", "k", "v", "g", "o")}
    wgateT = np.ascontiguousarray(np.asarray(inputs["L0_ffn_w_gate"], np.float32).T)
    wupT = np.ascontiguousarray(np.asarray(inputs["L0_ffn_w_up"], np.float32).T)
    wdownT = np.ascontiguousarray(np.asarray(inputs["L0_ffn_w_down"], np.float32).T)

    NTOK = n_own * L
    NPRET = max(n_pre, 1) * L
    in_maps = []
    for core in range(n_cores):
        b, half = core // 2, core % 2
        start = half * (T // 2)
        xin = np.ascontiguousarray(x[b, start:start + NTOK])
        if half == 0 or n_pre == 0:
            xpre = np.zeros((NPRET, C), np.float32)
            fl = np.zeros((128, 1), np.float32)
        else:
            xpre = np.ascontiguousarray(x[b, start - NPRET:start])
            fl = np.ones((128, 1), np.float32)
        in_maps.append(dict(
            xin=xin, xpre=xpre, flag=fl, td=td, nsc_bc=nsc_bc, mix_bc=mix_bc,
            wqT=wT["q"], wkT=wT["k"], wvT=wT["v"], wgT=wT["g"], woT=wT["o"],
            wgateT=wgateT, wupT=wupT, wdownT=wdownT,
            ident_i=ident, masks_i=masks, posv=posv,
        ))
    return in_maps


def kernel(**inputs):
    nc = _get_program()
    in_maps = make_in_maps(inputs)
    res = run_bass_kernel_spmd(nc, in_maps, core_ids=list(range(8)))
    out = np.empty((B, T, C), np.float32)
    for core in range(8):
        b, half = core // 2, core % 2
        out[b, half * (T // 2):(half + 1) * (T // 2)] = res.results[core]["y"]
    return out


# version 19
# speedup vs baseline: 1.0037x; 1.0037x over previous
import numpy as np
import ml_dtypes
from contextlib import ExitStack

import concourse.bass as bass
import concourse.mybir as mybir
from concourse.bass_utils import run_bass_kernel_spmd

F32 = mybir.dt.float32
BF16 = mybir.dt.bfloat16
AF = mybir.ActivationFunctionType
ALU = mybir.AluOpType
AX = mybir.AxisListType

MAGIC = float(np.float32(1.5 * 2 ** 23))
BF = ml_dtypes.bfloat16

B, T, C, H, D = 4, 4096, 1024, 4, 256
L = 128
CC = 8
NOWN = 16
NPRE = 8
INNER = 4096
NIC = 32
NSL = 8

POS_ORDER = ["q8", "k8", "q8o", "k8o", "q16", "k16", "q32", "k32",
             "q64", "k64", "qint", "kst"]
TI = {n: i for i, n in enumerate(POS_ORDER)}


class Res:
    __slots__ = ("name", "writer", "readers")

    def __init__(self, name):
        self.name, self.writer, self.readers = name, None, []


class Task:
    __slots__ = ("tid", "engine", "stream", "fn", "deps", "inc", "cum", "needed")


class Sched:
    ENGINES = ("tensor", "vector", "scalar", "sync", "gpsimd")

    def __init__(self):
        self.tasks = []

    def task(self, engine, fn, reads=(), writes=(), deps=(), dma=False):
        t = Task()
        t.tid = len(self.tasks)
        t.engine = engine
        t.stream = ("dma_" + engine) if dma else engine
        t.fn = fn
        t.inc = 16 if dma else 1
        dep = set()
        for r in reads:
            if r.writer is not None:
                dep.add(r.writer)
        for w in writes:
            if w.writer is not None:
                dep.add(w.writer)
            for rd in w.readers:
                dep.add(rd)
        for d in deps:
            if d is not None:
                dep.add(d)
        t.deps = sorted(dep, key=lambda d: d.tid)
        for d in t.deps:
            assert d.tid < t.tid
        for r in reads:
            r.readers.append(t)
        for w in writes:
            w.writer = t
            w.readers = []
        self.tasks.append(t)
        return t

    def barrier(self):
        last = {}
        for t in self.tasks:
            last[t.stream] = t
        deps = list(last.values())
        for e in self.ENGINES:
            self.task(e, lambda eng: eng.nop(), deps=deps)

    def emit(self, nc, block):
        NSEM = 6
        streams = {}
        for t in self.tasks:
            streams.setdefault(t.stream, []).append(t)
        sems, guards = {}, []
        for s, ts in streams.items():
            if ts and ts[0].inc == 16:
                hs = []
                for i in range(NSEM):
                    g = nc.semaphore(f"sem_{s}_{i}")
                    hs.append(g.__enter__())
                    guards.append(g)
                sems[s] = hs
            else:
                g = nc.semaphore("sem_" + s)
                sems[s] = g.__enter__()
                guards.append(g)

        def skip_edge(d, t):
            return False

        for t in self.tasks:
            t.needed = (t.inc == 16)
        for t in self.tasks:
            for d in t.deps:
                if not skip_edge(d, t):
                    d.needed = True

        consumers = {}
        for t in self.tasks:
            for d in t.deps:
                if d.inc == 16:
                    consumers.setdefault(d.tid, []).append(t.tid)

        wave_wait = {}
        totals = {}
        for s, ts in streams.items():
            if not ts or ts[0].inc != 16:
                c = 0
                for t in ts:
                    if t.needed:
                        c += t.inc
                    t.cum = c
                totals[s] = c
                continue
            cums = [0] * NSEM
            hist = []
            for w, t in enumerate(ts):
                si = w % NSEM
                cums[si] += 16
                t.cum = (si, cums[si])
                hist.append((si, cums[si]))
                if w >= NSEM:
                    wave_wait[t.tid] = hist[w - NSEM]
            totals[s] = cums

        def body_for(engine):
            mine = [t for t in self.tasks if t.engine == engine]

            def body(eng):
                marks = {}
                for t in mine:
                    want = {}
                    if t.inc == 16 and t.tid in wave_wait:
                        si, v = wave_wait[t.tid]
                        want[(t.stream, si)] = v
                    for d in t.deps:
                        if skip_edge(d, t):
                            continue
                        assert d.needed
                        if d.inc == 16:
                            si, v = d.cum
                            key = (d.stream, si)
                        else:
                            key, v = (d.stream, 0), d.cum
                        want[key] = max(want.get(key, 0), v)
                    for (s, si), v in sorted(want.items()):
                        if marks.get((s, si), 0) < v:
                            h = sems[s][si] if isinstance(sems[s], list) else sems[s]
                            eng.wait_ge(h, v)
                            marks[(s, si)] = v
                    inst = t.fn(eng)
                    if t.needed:
                        assert inst is not None, f"task {t.tid} ({engine}) returned None"
                        if t.inc == 16:
                            inst.then_inc(sems[t.stream][t.cum[0]], 16)
                        else:
                            inst.then_inc(sems[t.stream], 1)
                if engine == "sync" and "dma_sync" in totals:
                    for i, v in enumerate(totals["dma_sync"]):
                        if v:
                            eng.wait_ge(sems["dma_sync"][i], v)
            return body

        for e in self.ENGINES:
            getattr(block, e)(body_for(e))
        return guards


def build_program(n_own=NOWN, n_pre=NPRE, debug_taps=(), debug_spill=False,
                  detect_races=False):
    nc = bass.Bass("TRN2", target_bir_lowering=False,
                   detect_race_conditions=detect_races)
    _lp = nc.allow_low_precision("bf16 matmul operands by design; fp32 accumulate")
    _lp.__enter__()
    nc._lp_guard = _lp
    S = Sched()
    NTOK = n_own * L
    NPRET = max(n_pre, 1) * L

    def din(name, shape, dt=F32):
        return nc.dram_tensor(name, shape, dt, kind="ExternalInput").ap()

    xin = din("xin", [NTOK, C])
    xpre = din("xpre", [NPRET, C])
    flag = din("flag", [128, 1])
    td = din("td", [1, C])
    nsc_bc = din("nsc_bc", [128, C])
    mix_bc = din("mix_bc", [128, 4, C])
    wqT = din("wqT", [C, C]); wkT = din("wkT", [C, C]); wvT = din("wvT", [C, C])
    wgT = din("wgT", [C, C]); woT = din("woT", [C, C])
    wgateT = din("wgateT", [C, INNER]); wupT = din("wupT", [C, INNER])
    wdownT = din("wdownT", [INNER, C])
    ident_in = din("ident_i", [128, 128], BF16)
    masks_in = din("masks_i", [128, 5, 128])
    posv_in = din("posv", [1, 12, 128])
    y = nc.dram_tensor("y", [NTOK, C], F32, kind="ExternalOutput").ap()

    tap_reg = {}
    def emit_taps(point):
        for tn in list(debug_taps):
            if tn in tap_reg:
                ap, res, when = tap_reg[tn]
                if when != point:
                    continue
                o = nc.dram_tensor("tap_" + tn, list(ap.shape), ap.dtype,
                                   kind="ExternalOutput").ap()
                S.task("sync", lambda e, o=o, s=ap: e.dma_start(o, s),
                       reads=[res], dma=True)

    _ki = "ExternalOutput" if debug_spill else "Internal"
    xmid_d = nc.dram_tensor("xmid_d", [NTOK, C], F32, kind=_ki).ap()
    hq_d = nc.dram_tensor("hq_d", [n_own, 4, 128, 1024], BF16, kind=_ki).ap()
    gq_d = nc.dram_tensor("gq_d", [2, CC, 128, INNER], BF16, kind=_ki).ap()
    dq_d = nc.dram_tensor("dq_d", [NIC, 128, C], BF16, kind=_ki).ap()
    r_xmidd, r_hqd, r_gqd, r_dqd = Res("xmid_d"), Res("hq_d"), Res("gq_d"), Res("dq_d")

    ctx_common = ExitStack()
    ctx_attn = ExitStack()
    ctx_pre = ExitStack()

    def mksb(ctx, name, shape, dt):
        t = ctx.enter_context(nc.sbuf_tensor(name, shape, dt))
        return t, Res(name)

    pb = [ctx_common.enter_context(nc.psum_tensor(f"pb{i}", [128, 512], F32))
          for i in range(7)]
    pbt = ctx_common.enter_context(nc.psum_tensor("pbt", [128, 1024], BF16))
    rb = [Res(f"pb{i}") for i in range(7)]
    rbt = Res("pbt")

    ident, r_ident = mksb(ctx_common, "ident", [128, 128], BF16)
    mgv_m, r_mgv_m = mksb(ctx_common, "mgv_m", [128, 3], F32)
    alpha, r_alpha = mksb(ctx_common, "alpha", [128, max(n_own, 1)], F32)

    A = ctx_attn

    ones1, r_ones1 = mksb(A, "ones1", [1, 128], F32)
    onesc_bf, r_onesc = mksb(A, "onesc_bf", [128, 1], BF16)
    onesf_col, r_onesf = mksb(A, "onesf_col", [128, 1], F32)
    flag_s, r_flag = mksb(A, "flag_s", [128, 1], F32)
    nsc_s, r_nsc = mksb(A, "nsc_s", [128, C], F32)
    mixb, r_mixb = mksb(A, "mixb", [128, 4, C], BF16)
    wL, r_wL = mksb(A, "wL", [128, CC], F32)
    TAB, r_TAB = mksb(A, "TAB", [128, 12, C], BF16)
    mgv, r_mgv = mksb(A, "mgv", [128, 3], F32)
    asum, r_asum = mksb(A, "asum", [128, 3], F32)
    Ast, r_Ast = mksb(A, "Ast", [128, CC, 257], F32)
    Abf, r_Abf = mksb(A, "Abf", [128, CC, 257], BF16)
    sx, r_sx = zip(*[mksb(A, f"sx{i}", [128, C], F32) for i in range(2)])
    ln1f, r_ln1f = zip(*[mksb(A, f"ln1f{i}", [128, CC, 132], BF16) for i in range(2)])
    mixo, r_mixo = zip(*[mksb(A, n, [128, CC, 128], BF16)
                         for n in ("xq", "xk", "xv")])
    mixg, r_mixg = zip(*[mksb(A, f"xg{i}", [128, CC, 128], BF16) for i in range(2)])
    diff, r_diff = mksb(A, "diff", [128, CC, 128], BF16)
    qex_tm, r_qex_tm = mksb(A, "qex_tm", [128, C], BF16)
    kex_tm, r_kex_tm = mksb(A, "kex_tm", [128, C], BF16)
    qex_fm, r_qex_fm = mksb(A, "qex_fm", [128, C], BF16)
    kex_fm, r_kex_fm = mksb(A, "kex_fm", [128, C], BF16)
    v_tm, r_v_tm = mksb(A, "v_tm", [128, C], BF16)
    khst, r_khst = mksb(A, "khst", [128, C], BF16)
    qtint, r_qtint = mksb(A, "qtint", [128, C], BF16)
    mkq1, r_mkq1 = mksb(A, "mkq", [128, C], BF16)
    mkk1, r_mkk1 = mksb(A, "mkk", [128, C], BF16)
    mkq, r_mkq = (mkq1, mkq1), (r_mkq1, r_mkq1)
    mkk, r_mkk = (mkk1, mkk1), (r_mkk1, r_mkk1)
    Psb, r_Psb = mksb(A, "Psb", [128, 512], BF16)
    attf, r_attf = mksb(A, "attf", [128, CC, 128], BF16)
    rzrow, r_rzrow = mksb(A, "rzrow", [1, 512], F32)
    rzbc, r_rzbc = mksb(A, "rzbc", [128, 512], BF16)
    scr1, r_scr1 = mksb(A, "scr1", [128, C], F32)
    ln1tm_t, r_ln1tm = mksb(A, "ln1tm", [128, C], BF16)
    scr3, r_scr3 = mksb(A, "scr3", [128, C], F32)
    st, r_st = zip(*[mksb(A, f"st{i}", [128, 8], F32) for i in range(4)])
    stL, r_stL = mksb(A, "stL", [128, 10], F32)
    wq_s, r_wq = mksb(A, "wq_s", [128, CC, C], BF16)
    wk_s, r_wk = mksb(A, "wk_s", [128, CC, C], BF16)
    wv_s, r_wv = mksb(A, "wv_s", [128, CC, C], BF16)
    wg_s, r_wg = mksb(A, "wg_s", [128, CC, C], BF16)
    wo_s, r_wo = mksb(A, "wo_s", [128, CC, C], BF16)

    c128, r_c128 = mksb(ctx_pre, "c128", [1, 1], F32)
    lw, r_lw = mksb(ctx_pre, "lw", [1, C], F32)
    posv_s, r_posv = mksb(ctx_pre, "posv_s", [1, 12, 128], F32)

    def dma_in(dst_ap, src_ap, wres, engine="sync", rres=()):
        return S.task(engine, lambda e, d=dst_ap, s=src_ap: e.dma_start(d, s),
                      reads=rres, writes=[wres], dma=True)

    dma_in(ident[:, :], ident_in, r_ident)
    dma_in(posv_s[:, :, :], posv_in, r_posv)
    dma_in(flag_s[:, :], flag, r_flag)
    dma_in(lw[:, :], td, r_lw)
    dma_in(nsc_s[:, :], nsc_bc, r_nsc)

    for wdst, rres, wsrc in ((wq_s, r_wq, wqT), (wk_s, r_wk, wkT), (wv_s, r_wv, wvT),
                             (wg_s, r_wg, wgT), (wo_s, r_wo, woT)):
        src = wsrc.rearrange("(cc p) o -> p cc o", p=128)
        S.task("gpsimd", lambda e, d=wdst, s=src: e.dma_start(d[:, :, :], s),
               writes=[rres], dma=True)

    S.task("vector", lambda e: e.memset(ones1[:, :], 1.0), writes=[r_ones1])
    S.task("vector", lambda e: e.memset(onesc_bf[:, :], 1.0), writes=[r_onesc])
    S.task("vector", lambda e: e.memset(onesf_col[:, :], 1.0), writes=[r_onesf])
    S.task("vector", lambda e: e.memset(c128[:, :], 128.0), writes=[r_c128])
    S.task("vector", lambda e: e.memset(Ast[:, :, :], 0.0), writes=[r_Ast])
    S.task("vector", lambda e: e.memset(Abf[:, :, :], 0.0), writes=[r_Abf])
    S.task("vector", lambda e: e.memset(asum[:, :], 0.0), writes=[r_asum])
    S.task("gpsimd", lambda e: e.dma_start(mixb[:, :, :], mix_bc),
           writes=[r_mixb], dma=True)
    S.task("scalar", lambda e: e.activation(lw[:, :], lw[:, :], AF.Exp),
           reads=[r_lw], writes=[r_lw])
    S.task("vector", lambda e: e.tensor_scalar(lw[:, :], lw[:, :], -1.0, None, ALU.mult),
           reads=[r_lw], writes=[r_lw])

    for ti, pname in enumerate(POS_ORDER):
        pvec = posv_s[:, ti, :]
        if pname == "kst":
            def mk_out(e, pv=pvec):
                e.matmul(pb[0][:, :], pv, lw[:, 0:512], start=True, stop=True)
                return e.matmul(pb[1][:, :], pv, lw[:, 512:1024], start=True, stop=True)
        else:
            def mk_out(e, pv=pvec):
                last = None
                for cc in range(CC):
                    bank = pb[cc // 4]
                    col = (cc % 4) * 128
                    last = e.matmul(bank[:, col:col + 128],
                                    lw[:, cc * 128:(cc + 1) * 128], pv,
                                    start=(cc % 4 == 0), stop=(cc % 4 == 3))
                return last
        S.task("tensor", mk_out, reads=[r_lw, r_posv], writes=[rb[0], rb[1]])
        def mk_clamp(e):
            e.tensor_scalar(scr1[:, 0:512], pb[0][:, :], 30.0, None, ALU.min)
            return e.tensor_scalar(scr1[:, 512:1024], pb[1][:, :], 30.0, None, ALU.min)
        S.task("vector", mk_clamp, reads=[rb[0], rb[1]], writes=[r_scr1])
        S.task("scalar", lambda e, ti=ti: e.activation(TAB[:, ti, :], scr1[:, :], AF.Exp),
               reads=[r_scr1], writes=[r_TAB])

    def mk_wl(e):
        last = None
        for cc in range(CC):
            last = e.matmul(pb[2][:, cc:cc + 1], lw[:, cc * 128:(cc + 1) * 128],
                            c128[:, :], start=(cc == 0), stop=(cc == CC - 1))
        return last
    S.task("tensor", mk_wl, reads=[r_lw, r_c128], writes=[rb[2]])
    S.task("scalar", lambda e: e.activation(wL[:, :], pb[2][:, 0:CC], AF.Exp),
           reads=[rb[2]], writes=[r_wL])
    S.barrier()
    ctx_pre.close()
    masks, r_masks = mksb(A, "masks", [128, 5, 128], F32)
    dma_in(masks[:, :, :], masks_in, r_masks)
    stg, r_stg = zip(*[mksb(A, f"stg{i}", [128, 1024], F32) for i in range(2)])
    stgs, r_stgs = mksb(A, "stgs", [128, 1024], F32)

    def prep_jobs(pass2):
        jobs = []
        specs = ((0, wgateT, CC, INNER, 1024), (1, wupT, CC, INNER, 1024),
                 (2, wdownT, NIC, C, 1024))
        for m, src, rows, cols, w in specs:
            per_row = cols // w
            for r0 in range(rows):
                for h in range(per_row):
                    c0 = h * w
                    sl = src[r0 * 128:(r0 + 1) * 128, c0:c0 + w]
                    if not pass2:
                        jobs.append((m, sl, None, w))
                    else:
                        if m < 2:
                            dst = gq_d[m, r0, :, c0:c0 + w]
                        else:
                            dst = dq_d[r0, :, c0:c0 + w]
                        jobs.append((m, sl, dst, w))
        return jobs

    prep_state = {"phase": 1, "jobs": prep_jobs(False), "i": 0,
                  "pending": [], "done": False}

    def finish_pass1():
        def sums(e):
            last = None
            for m in range(3):
                last = e.matmul(pb[2][0:1, m:m + 1], asum[:, m:m + 1],
                                onesf_col[:, 0:1], start=(m == 0), stop=(m == 2))
            return last
        S.task("tensor", sums, reads=[r_asum, r_onesf], writes=[rb[2]])
        def mw(e):
            n0 = float(C * INNER)
            e.tensor_scalar(st[3][0:1, 0:3], pb[2][0:1, 0:3], 1.0 / n0, 1e-5,
                            ALU.mult, ALU.max)
            e.engine_nop()
            return e.reciprocal(st[3][0:1, 3:6], st[3][0:1, 0:3])
        S.task("vector", mw, reads=[rb[2]], writes=[r_st[3]])
        S.task("tensor",
               lambda e: e.matmul(pb[2][:, 0:6], ones1[:, :], st[3][0:1, 0:6],
                                  start=True, stop=True),
               reads=[r_st[3], r_ones1], writes=[rb[2]])
        def split(e):
            e.tensor_copy(mgv_m[:, :], pb[2][:, 0:3])
            return e.tensor_copy(mgv[:, :], pb[2][:, 3:6])
        S.task("vector", split, reads=[rb[2]], writes=[r_mgv, r_mgv_m])

    def prep_compute(bi, m, dst, w, phase):
        if phase == 1:
            S.task("scalar",
                   lambda e, bi=bi, w=w: e.activation(
                       stgs[:, 0:w], stg[bi][:, 0:w], AF.Abs,
                       accum_out=st[3][:, 6:7]),
                   reads=[r_stg[bi]], writes=[r_stgs, r_st[3]])
            S.task("vector",
                   lambda e, m=m: e.tensor_tensor(
                       asum[:, m:m + 1], asum[:, m:m + 1], st[3][:, 6:7],
                       ALU.add),
                   reads=[r_st[3], r_asum], writes=[r_asum])
        else:
            def q1(e, bi=bi, m=m, w=w):
                e.tensor_scalar(stgs[:, 0:w], stg[bi][:, 0:w], mgv[:, m:m + 1],
                                MAGIC, ALU.mult, ALU.add)
                e.engine_nop()
                return e.tensor_scalar(stgs[:, 0:w], stgs[:, 0:w], MAGIC, -1.0,
                                       ALU.subtract, ALU.max)
            S.task("vector", q1, reads=[r_stg[bi], r_mgv], writes=[r_stgs])
            S.task("vector",
                   lambda e, bi=bi, w=w: e.tensor_scalar(
                       stg[bi][:, :].bitcast(BF16)[:, 0:w], stgs[:, 0:w], 1.0,
                       None, ALU.min),
                   reads=[r_stgs], writes=[r_stg[bi]])
            S.task("sync",
                   lambda e, bi=bi, dst=dst, w=w: e.dma_start(
                       dst, stg[bi][:, :].bitcast(BF16)[:, 0:w]),
                   reads=[r_stg[bi]], writes=[r_gqd if m < 2 else r_dqd],
                   dma=True)

    def prep_point():
        ps = prep_state
        if ps["done"]:
            return True
        for bi, m, dst, w, phase in ps["pending"]:
            prep_compute(bi, m, dst, w, phase)
        ps["pending"] = []
        if ps["i"] >= len(ps["jobs"]):
            if ps["phase"] == 1:
                finish_pass1()
                ps["phase"], ps["jobs"], ps["i"] = 2, prep_jobs(True), 0
            else:
                ps["done"] = True
                return True
        for bi in range(2):
            if ps["i"] < len(ps["jobs"]):
                m, src_, dst, w = ps["jobs"][ps["i"]]
                ps["i"] += 1
                dma_in(stg[bi][:, 0:w], src_, r_stg[bi], engine="gpsimd")
                ps["pending"].append((bi, m, dst, w, ps["phase"]))
        return False

    cstate = {"par": 0, "first": True}
    chctx = {}

    def attn_frontA(kind, ci):
        par = cstate["par"]; cstate["par"] ^= 1
        first = cstate["first"]; cstate["first"] = False
        chctx[(kind, ci)] = (par, first)
        own = (kind == "own")
        xsrc = (xin if own else xpre)[ci * L:(ci + 1) * L, :]
        dma_in(sx[par][:, :], xsrc, r_sx[par], engine="scalar")

        stv, r_stv = st[0], r_st[0]
        dsink = diff[:, :, :]
        def stats(e, par=par):
            e.scalar_tensor_tensor(dsink, sx[par][:, :].rearrange("p (a b) -> p a b", a=CC),
                                   1.0, sx[par][:, :].rearrange("p (a b) -> p a b", a=CC),
                                   ALU.mult, ALU.mult, accum_out=stv[:, 0:1])
            return e.tensor_scalar(dsink, sx[par][:, :].rearrange("p (a b) -> p a b", a=CC),
                                   1.0, 0.0, ALU.mult, ALU.add, accum_out=stv[:, 1:2])
        S.task("vector", stats, reads=[r_sx[par]], writes=[r_diff, r_stv])
        def stats2(e):
            e.tensor_scalar(stv[:, 2:3], stv[:, 1:2], 1.0 / C, None, ALU.mult)
            e.engine_nop()
            e.tensor_tensor(stv[:, 3:4], stv[:, 2:3], stv[:, 2:3], ALU.mult)
            e.engine_nop()
            e.scalar_tensor_tensor(stv[:, 4:5], stv[:, 0:1], 1.0 / C, stv[:, 3:4],
                                   ALU.mult, ALU.subtract)
            e.engine_nop()
            return e.tensor_scalar(stv[:, 4:5], stv[:, 4:5], 1e-5, None, ALU.add)
        S.task("vector", stats2, reads=[r_stv], writes=[r_stv])
        S.task("scalar", lambda e: e.activation(stv[:, 5:6], stv[:, 4:5], AF.Ln),
               reads=[r_stv], writes=[r_stv])
        S.task("scalar", lambda e: e.activation(stv[:, 6:7], stv[:, 5:6], AF.Exp,
                                                scale=-0.5),
               reads=[r_stv], writes=[r_stv])
        def stats4(e):
            e.tensor_tensor(stv[:, 7:8], stv[:, 2:3], stv[:, 6:7], ALU.mult)
            e.engine_nop()
            return e.tensor_scalar(stv[:, 7:8], stv[:, 7:8], -1.0, None, ALU.mult)
        S.task("vector", stats4, reads=[r_stv], writes=[r_stv])
        S.task("scalar",
               lambda e, par=par: e.activation(ln1tm_t[:, 0:C], sx[par][:, :],
                                               AF.Identity, bias=stv[:, 7:8],
                                               scale=stv[:, 6:7]),
               reads=[r_sx[par], r_stv], writes=[r_ln1tm])

    def attn_frontB(kind, ci):
        par, first = chctx[(kind, ci)]
        own = (kind == "own")
        def ln1T(e):
            last = None
            for cc in range(CC):
                last = e.matmul(pbt[:, cc * 128:(cc + 1) * 128],
                                ln1tm_t[:, cc * 128:(cc + 1) * 128], ident[:, :],
                                is_transpose=True, start=(cc == 0), stop=(cc == CC - 1))
            return last
        S.task("tensor", ln1T, reads=[r_ln1tm, r_ident], writes=[rbt])
        if first:
            S.task("vector", lambda e, par=par: e.memset(ln1f[par][:, :, 0:1], 0.0),
                   writes=[r_ln1f[par]])
        else:
            S.task("vector",
                   lambda e, par=par: e.tensor_copy(ln1f[par][:, :, 0:1],
                                                    ln1f[1 - par][:, :, 128:129]),
                   reads=[r_ln1f[1 - par]], writes=[r_ln1f[par]])
        S.task("scalar",
               lambda e, par=par: e.copy(ln1f[par][:, :, 1:129],
                                         pbt[:, :].rearrange("p (a b) -> p a b", a=CC)),
               reads=[rbt], writes=[r_ln1f[par]])
        lcur = ln1f[par][:, :, 1:129]
        lprev = ln1f[par][:, :, 0:128]
        S.task("vector", lambda e: e.tensor_tensor(diff[:, :, :], lcur, lprev,
                                                   ALU.subtract),
               reads=[r_ln1f[par]], writes=[r_diff])
        for mi in (range(4) if own else (1, 2)):
            dst = mixg[par] if mi == 3 else mixo[mi]
            rdst = r_mixg[par] if mi == 3 else r_mixo[mi]
            def domix(e, mi=mi, dst=dst):
                e.tensor_tensor(dst[:, :, :], diff[:, :, :],
                                mixb[:, mi, :].rearrange("p (a b) -> p a b", a=CC),
                                ALU.mult)
                e.engine_nop()
                return e.tensor_tensor(dst[:, :, :], dst[:, :, :], lprev, ALU.add)
            S.task("vector", domix, reads=[r_diff, r_mixb, r_ln1f[par]],
                   writes=[rdst])

    def attn_backSproj(kind, ci):
        own = (kind == "own")
        def proj_tm(mi, wten, b0, b1):
            def fn(e):
                last = None
                for K in range(CC):
                    for nh in range(2):
                        last = e.matmul((pb[b0], pb[b1])[nh][:, :],
                                        mixo[mi][:, K, :],
                                        wten[:, K, nh * 512:(nh + 1) * 512],
                                        start=(K == 0), stop=(K == CC - 1))
                return last
            return fn
        S.task("tensor", proj_tm(1, wk_s, 0, 1), reads=[r_mixo[1], r_wk],
               writes=[rb[0], rb[1]])
        S.task("tensor", proj_tm(2, wv_s, 2, 3), reads=[r_mixo[2], r_wv],
               writes=[rb[2], rb[3]])
        if own:
            S.task("tensor", proj_tm(0, wq_s, 4, 5), reads=[r_mixo[0], r_wq],
                   writes=[rb[4], rb[5]])

    def attn_backSrest(kind, ci):
        par, first = chctx[(kind, ci)]
        own = (kind == "own")

        def rms_sq(pbank0, pbank1, rbank0, rbank1, stcol, r_stc, sink):
            def s1(e):
                e.activation(sink[:, 0:512], pbank0[:, :], AF.Square,
                             accum_out=stcol[:, 0:1])
                return e.activation(sink[:, 512:1024], pbank1[:, :], AF.Square,
                                    accum_out=stcol[:, 1:2])
            S.task("scalar", s1, reads=[rbank0, rbank1], writes=[r_scr1, r_stc])
        rms_sq(pb[0], pb[1], rb[0], rb[1], st[1], r_st[1], scr1)
        if own:
            rms_sq(pb[4], pb[5], rb[4], rb[5], st[2], r_st[2], scr1)
        def rms_add(stcol, r_stc):
            def fn(e):
                e.tensor_tensor(stcol[:, 2:3], stcol[:, 0:1],
                                stcol[:, 1:2], ALU.add)
                e.engine_nop()
                return e.tensor_scalar(stcol[:, 2:3], stcol[:, 2:3], 1e-13,
                                       None, ALU.add)
            S.task("vector", fn, reads=[r_stc], writes=[r_stc])
        rms_add(st[1], r_st[1])
        if own:
            rms_add(st[2], r_st[2])
        def rms_sqrt(stcol, r_stc):
            S.task("scalar",
                   lambda e: e.activation(stcol[:, 3:4], stcol[:, 2:3], AF.Ln,
                                          scale=1.0 / C),
                   reads=[r_stc], writes=[r_stc])
            S.task("scalar",
                   lambda e: e.activation(stcol[:, 4:5], stcol[:, 3:4], AF.Exp,
                                          scale=-0.5),
                   reads=[r_stc], writes=[r_stc])
        rms_sqrt(st[1], r_st[1])
        if own:
            rms_sqrt(st[2], r_st[2])
        def rms_fin(stcol, r_stc, pbank0, pbank1, rbank0, rbank1, dst):
            def ap1(e):
                e.tensor_tensor(dst[:, 0:512], pbank0[:, :], nsc_s[:, 0:512], ALU.mult)
                return e.tensor_tensor(dst[:, 512:1024], pbank1[:, :],
                                       nsc_s[:, 512:1024], ALU.mult)
            S.task("vector", ap1, reads=[rbank0, rbank1, r_nsc], writes=[r_scr3])
        rms_fin(st[1], r_st[1], pb[0], pb[1], rb[0], rb[1], scr3)
        S.task("scalar",
               lambda e: e.activation(kex_tm[:, :], scr3[:, :], AF.Exp,
                                      scale=st[1][:, 4:5]),
               reads=[r_scr3, r_st[1]], writes=[r_kex_tm])
        def vev(e):
            e.copy(v_tm[:, 0:512], pb[2][:, :])
            return e.copy(v_tm[:, 512:1024], pb[3][:, :])
        S.task("scalar", vev, reads=[rb[2], rb[3]], writes=[r_v_tm])
        if own:
            rms_fin(st[2], r_st[2], pb[4], pb[5], rb[4], rb[5], scr1)
            S.task("scalar",
                   lambda e: e.activation(qex_tm[:, :], scr1[:, :], AF.Exp,
                                          scale=st[2][:, 4:5]),
                   reads=[r_scr1, r_st[2]], writes=[r_qex_tm])

        S.task("vector",
               lambda e: e.tensor_tensor(khst[:, :], kex_tm[:, :], TAB[:, TI["kst"], :],
                                         ALU.mult),
               reads=[r_kex_tm, r_TAB], writes=[r_khst])

        if own:
            def proj_g(e):
                last = None
                for Mc in range(CC):
                    bank = pb[6] if Mc < 4 else pb[0]
                    col = (Mc % 4) * 128
                    for K in range(CC):
                        last = e.matmul(bank[:, col:col + 128],
                                        wg_s[:, K, Mc * 128:(Mc + 1) * 128],
                                        mixg[par][:, K, :],
                                        start=(K == 0 and Mc % 4 == 0),
                                        stop=(K == CC - 1 and Mc % 4 == 3))
                return last
            S.task("tensor", proj_g, reads=[r_mixg[par], r_wg], writes=[rb[6], rb[0]])
            def gev(e):
                e.activation(scr3[:, 0:512], pb[6][:, :], AF.Tanh, scale=0.5)
                return e.activation(scr3[:, 512:1024], pb[0][:, :], AF.Tanh, scale=0.5)
            S.task("scalar", gev, reads=[rb[6], rb[0]], writes=[r_scr3])
            S.task("vector",
                   lambda e: e.tensor_scalar(scr3[:, :], scr3[:, :], 0.5, 0.5,
                                             ALU.mult, ALU.add),
                   reads=[r_scr3], writes=[r_scr3])

            def trT(src):
                def fn(e):
                    last = None
                    for cc in range(CC):
                        last = e.matmul(pbt[:, cc * 128:(cc + 1) * 128],
                                        src[:, cc * 128:(cc + 1) * 128], ident[:, :],
                                        is_transpose=True, start=(cc == 0),
                                        stop=(cc == CC - 1))
                    return last
                return fn
            S.task("tensor", trT(qex_tm), reads=[r_qex_tm, r_ident], writes=[rbt])
            S.task("scalar", lambda e: e.copy(qex_fm[:, :], pbt[:, :]),
                   reads=[rbt], writes=[r_qex_fm])
            S.task("tensor", trT(kex_tm), reads=[r_kex_tm, r_ident], writes=[rbt])
            S.task("scalar", lambda e: e.copy(kex_fm[:, :], pbt[:, :]),
                   reads=[rbt], writes=[r_kex_fm])
            S.task("vector",
                   lambda e: e.tensor_tensor(qtint[:, :], qex_fm[:, :],
                                             TAB[:, TI["qint"], :], ALU.mult),
                   reads=[r_qex_fm, r_TAB], writes=[r_qtint])

            mkp = {"i": 0}
            def makes(qn, kn):
                mp = mkp["i"] & 1
                mkp["i"] += 1
                def fn(e, qn=qn, kn=kn, mp=mp):
                    e.tensor_tensor(mkq[mp][:, :], qex_fm[:, :], TAB[:, TI[qn], :],
                                    ALU.mult)
                    return e.tensor_tensor(mkk[mp][:, :], kex_fm[:, :],
                                           TAB[:, TI[kn], :], ALU.mult)
                S.task("vector", fn, reads=[r_qex_fm, r_kex_fm, r_TAB],
                       writes=[r_mkq[mp], r_mkk[mp]])
                return mp
            for (qn, kn, bank) in (("q8", "k8", 2), ("q8o", "k8o", 4),
                                   ("q16", "k16", 5), ("q32", "k32", 3),
                                   ("q64", "k64", 6)):
                mp = makes(qn, kn)
                def pmm(e, bank=bank, mp=mp):
                    last = None
                    for h in range(H):
                        for dc in range(2):
                            cc = 2 * h + dc
                            last = e.matmul(pb[bank][:, h * 128:(h + 1) * 128],
                                            mkk[mp][:, cc * 128:(cc + 1) * 128],
                                            mkq[mp][:, cc * 128:(cc + 1) * 128],
                                            start=(h == 0 and dc == 0),
                                            stop=(h == H - 1 and dc == 1))
                    return last
                S.task("tensor", pmm, reads=[r_mkq[mp], r_mkk[mp]], writes=[rb[bank]])

            def mk_m(m):
                return masks[:, m:m + 1, :].broadcast_to((128, 4, 128))
            def r3(t):
                return t.rearrange("p (a b) -> p a b", a=4)
            def pev1(e):
                sa, sb_ = scr1[:, 0:512], scr1[:, 512:1024]
                e.tensor_tensor(r3(sa), r3(pb[2][:, :]), mk_m(0), ALU.mult)
                e.tensor_tensor(r3(sb_), r3(pb[4][:, :]), mk_m(1), ALU.mult)
                e.engine_nop()
                e.tensor_tensor(sa, sa, sb_, ALU.add)
                e.tensor_tensor(r3(sb_), r3(pb[5][:, :]), mk_m(2), ALU.mult)
                e.engine_nop()
                e.tensor_tensor(sa, sa, sb_, ALU.add)
                e.tensor_tensor(r3(sb_), r3(pb[3][:, :]), mk_m(3), ALU.mult)
                e.engine_nop()
                e.tensor_tensor(sa, sa, sb_, ALU.add)
                e.tensor_tensor(r3(sb_), r3(pb[6][:, :]), mk_m(4), ALU.mult)
                e.engine_nop()
                return e.tensor_tensor(Psb[:, :], sa, sb_, ALU.add)
            S.task("vector", pev1, reads=[rb[2], rb[3], rb[4], rb[5], rb[6], r_masks],
                   writes=[r_Psb, r_scr1])

            def smm(bank, ccs):
                def fn(e, bank=bank, ccs=ccs):
                    last = None
                    n = len(ccs) * 3
                    i = 0
                    for cc in ccs:
                        h, ec = cc // 2, cc % 2
                        col = (cc % 4) * 128
                        last = e.matmul(pb[bank][:, col:col + 128],
                                        v_tm[:, cc * 128:(cc + 1) * 128],
                                        Psb[:, h * 128:(h + 1) * 128],
                                        start=(i == 0), stop=False)
                        i += 1
                        for dc in range(2):
                            ci2 = 2 * h + dc
                            last = e.matmul(pb[bank][:, col:col + 128],
                                            Abf[:, ci2, ec * 128:(ec + 1) * 128],
                                            qtint[:, ci2 * 128:(ci2 + 1) * 128],
                                            start=False, stop=(i == n - 1))
                            i += 1
                    return last
                return fn
            S.task("tensor", smm(4, [0, 1, 2, 3]),
                   reads=[r_v_tm, r_Psb, r_Abf, r_qtint], writes=[rb[4]])
            S.task("tensor", smm(5, [4, 5, 6, 7]),
                   reads=[r_v_tm, r_Psb, r_Abf, r_qtint], writes=[rb[5]])
            def zmm(e):
                last = None
                n = H * 3
                i = 0
                for h in range(H):
                    last = e.matmul(pb[6][0:1, h * 128:(h + 1) * 128],
                                    onesc_bf[:, :], Psb[:, h * 128:(h + 1) * 128],
                                    start=(i == 0), stop=False)
                    i += 1
                    for dc in range(2):
                        ci2 = 2 * h + dc
                        last = e.matmul(pb[6][0:1, h * 128:(h + 1) * 128],
                                        Abf[:, ci2, 256:257],
                                        qtint[:, ci2 * 128:(ci2 + 1) * 128],
                                        start=False, stop=(i == n - 1))
                        i += 1
                return last
            S.task("tensor", zmm, reads=[r_onesc, r_Psb, r_Abf, r_qtint],
                   writes=[rb[6]])


            S.task("scalar", lambda e: e.copy(rzrow[:, :], pb[6][0:1, 0:512]),
                   reads=[rb[6]], writes=[r_rzrow])
            S.task("tensor",
                   lambda e: e.matmul(pb[2][:, :], ones1[:, :], rzrow[:, :],
                                      start=True, stop=True),
                   reads=[r_rzrow, r_ones1], writes=[rb[2]])
            S.task("vector", lambda e: e.reciprocal(rzbc[:, :], pb[2][:, :]),
                   reads=[rb[2]], writes=[r_rzbc])
            def rzmul(e):
                rz3 = rzbc[:, :].rearrange("p (h i) -> p h i", h=H)
                rzb = rz3.unsqueeze(2).broadcast_to((128, H, 2, 128))
                g4 = scr3[:, :].rearrange("p (h d i) -> p h d i", h=H, d=2)
                return e.tensor_tensor(g4, g4, rzb, ALU.mult)
            S.task("vector", rzmul, reads=[r_rzbc, r_scr3], writes=[r_scr3])
            def epi1(e):
                last = None
                for cc in range(CC):
                    bank = pb[4] if cc < 4 else pb[5]
                    col = (cc % 4) * 128
                    last = e.scalar_tensor_tensor(
                        attf[:, cc, :], bank[:, col:col + 128], 1.0,
                        scr3[:, cc * 128:(cc + 1) * 128], ALU.mult, ALU.mult)
                return last
            S.task("vector", epi1, reads=[rb[4], rb[5], r_scr3], writes=[r_attf])

        def damm(e):
            last = None
            for b in range(CC):
                h = b // 2
                last = e.matmul(pb[b // 2][:, (b % 2) * 256:(b % 2) * 256 + 256],
                                khst[:, b * 128:(b + 1) * 128],
                                v_tm[:, h * 256:(h + 1) * 256],
                                start=(b % 2 == 0), stop=(b % 2 == 1))
            return last
        S.task("tensor", damm, reads=[r_khst, r_v_tm],
               writes=[rb[0], rb[1], rb[2], rb[3]])
        def dzmm(e):
            last = None
            for b in range(CC):
                last = e.matmul(pb[6][:, b:b + 1], khst[:, b * 128:(b + 1) * 128],
                                onesc_bf[:, :], start=(b == 0), stop=(b == CC - 1))
            return last
        S.task("tensor", dzmm, reads=[r_khst, r_onesc], writes=[rb[6]])
        def upd(e):
            last = None
            for b in range(CC):
                last = e.scalar_tensor_tensor(
                    Ast[:, b, 0:256], Ast[:, b, 0:256], wL[:, b:b + 1],
                    pb[b // 2][:, (b % 2) * 256:(b % 2) * 256 + 256],
                    ALU.mult, ALU.add)
                last = e.scalar_tensor_tensor(
                    Ast[:, b, 256:257], Ast[:, b, 256:257], wL[:, b:b + 1],
                    pb[6][:, b:b + 1], ALU.mult, ALU.add)
            return last
        if own:
            def womm(e):
                last = None
                for K in range(CC):
                    for nh in range(2):
                        last = e.matmul((pb[4], pb[5])[nh][:, :], attf[:, K, :],
                                        wo_s[:, K, nh * 512:(nh + 1) * 512],
                                        start=(K == 0), stop=(K == CC - 1))
                return last
            S.task("tensor", womm, reads=[r_attf, r_wo], writes=[rb[4], rb[5]])
        S.task("vector", upd, reads=[rb[0], rb[1], rb[2], rb[3], rb[6], r_wL],
               writes=[r_Ast])
        S.task("scalar", lambda e: e.copy(Abf[:, :, :], Ast[:, :, :]),
               reads=[r_Ast], writes=[r_Abf])

    def attn_backE_B(kind, ci):
        par, first = chctx[(kind, ci)][:2]
        if True:
            stv2, r_stv2 = stL, r_stL
            def l2sq(e):
                e.activation(scr1[:, 0:512], pb[4][:, :], AF.Square,
                             accum_out=stv2[:, 0:1])
                return e.activation(scr1[:, 512:1024], pb[5][:, :], AF.Square,
                                    accum_out=stv2[:, 1:2])
            S.task("scalar", l2sq, reads=[rb[4], rb[5]], writes=[r_scr1, r_stv2])
            def l2s(e):
                e.tensor_scalar(scr3[:, 0:512], pb[4][:, :], 1.0, 0.0, ALU.mult,
                                ALU.add, accum_out=stv2[:, 2:3])
                return e.tensor_scalar(scr3[:, 512:1024], pb[5][:, :], 1.0, 0.0,
                                       ALU.mult, ALU.add, accum_out=stv2[:, 3:4])
            S.task("vector", l2s, reads=[rb[4], rb[5], r_stv2],
                   writes=[r_scr3, r_stv2])
            def l2s2(e):
                e.tensor_tensor(stv2[:, 0:1], stv2[:, 0:1], stv2[:, 1:2], ALU.add)
                e.tensor_tensor(stv2[:, 2:3], stv2[:, 2:3], stv2[:, 3:4], ALU.add)
                e.engine_nop()
                e.tensor_scalar(stv2[:, 4:5], stv2[:, 2:3], 1.0 / C, None, ALU.mult)
                e.engine_nop()
                e.tensor_tensor(stv2[:, 5:6], stv2[:, 4:5], stv2[:, 4:5], ALU.mult)
                e.engine_nop()
                e.scalar_tensor_tensor(stv2[:, 6:7], stv2[:, 0:1], 1.0 / C,
                                       stv2[:, 5:6], ALU.mult, ALU.subtract)
                e.engine_nop()
                return e.tensor_scalar(stv2[:, 6:7], stv2[:, 6:7], 1e-5, None, ALU.add)
            S.task("vector", l2s2, reads=[r_stv2], writes=[r_stv2])
            S.task("scalar",
                   lambda e: e.activation(stv2[:, 7:8], stv2[:, 6:7], AF.Ln),
                   reads=[r_stv2], writes=[r_stv2])
            S.task("scalar",
                   lambda e: e.activation(stv2[:, 6:7], stv2[:, 7:8], AF.Exp,
                                          scale=-0.5),
                   reads=[r_stv2], writes=[r_stv2])
            def l2s3(e):
                e.tensor_tensor(stv2[:, 5:6], stv2[:, 4:5], stv2[:, 6:7], ALU.mult)
                e.engine_nop()
                return e.tensor_scalar(stv2[:, 5:6], stv2[:, 5:6], -1.0, None,
                                       ALU.mult)
            S.task("vector", l2s3, reads=[r_stv2], writes=[r_stv2])
            S.task("vector",
                   lambda e, par=par: e.tensor_scalar(sx[par][:, :], sx[par][:, :],
                                                      stv2[:, 5:6], None, ALU.add),
                   reads=[r_stv2], writes=[r_sx[par]])
            def l2b(e, par=par):
                e.scalar_tensor_tensor(sx[par][:, 0:512], pb[4][:, :], stv2[:, 6:7],
                                       sx[par][:, 0:512], ALU.mult, ALU.add)
                return e.scalar_tensor_tensor(sx[par][:, 512:1024], pb[5][:, :],
                                              stv2[:, 6:7], sx[par][:, 512:1024],
                                              ALU.mult, ALU.add)
            S.task("vector", l2b, reads=[rb[4], rb[5], r_stv2, r_sx[par]],
                   writes=[r_sx[par]])
            S.task("sync",
                   lambda e, ci=ci, par=par: e.dma_start(
                       xmid_d[ci * L:(ci + 1) * L, :], sx[par][:, :]),
                   reads=[r_sx[par]], writes=[r_xmidd], dma=True)


    def emit_state_mask():
        S.task("vector",
               lambda e: e.tensor_scalar(Ast[:, :, :], Ast[:, :, :], flag_s[:, 0:1],
                                         None, ALU.mult),
               reads=[r_flag], writes=[r_Ast])
        S.task("scalar", lambda e: e.copy(Abf[:, :, :], Ast[:, :, :]),
               reads=[r_Ast], writes=[r_Abf])

    chunks = [("pre", i) for i in range(n_pre)] + [("own", i) for i in range(n_own)]
    prep_done = prep_point()
    attn_frontA(*chunks[0])
    attn_frontB(*chunks[0])
    prev = None
    for idx, ch in enumerate(chunks):
        if prev is not None and prev[0] == "own":
            attn_backE_B(*prev)
        if idx == n_pre and n_pre > 0:
            emit_state_mask()
        prep_done = prep_point()
        if idx + 1 < len(chunks):
            attn_frontA(*chunks[idx + 1])
        prep_done = prep_point()
        attn_backSproj(*ch)
        if idx + 1 < len(chunks):
            attn_frontB(*chunks[idx + 1])
        prep_done = prep_point()
        attn_backSrest(*ch)
        prep_done = prep_point()
        prev = ch
    if prev[0] == "own":
        attn_backE_B(*prev)
    while not prep_done:
        prep_done = prep_point()

    tap_reg.update({
        "sx0": (sx[cstate["par"] ^ 1][:, :], r_sx[cstate["par"] ^ 1], "attn"),
        "stv0": (st[0][:, :], r_st[0], "attn"),
        "ln1tm": (ln1tm_t[:, :], r_ln1tm, "attn"),
        "scr1t": (scr1[:, :], r_scr1, "attn"),
        "TAB": (TAB[:, :, :], r_TAB, "attn"),
        "wLs": (wL[:, :], r_wL, "attn"),
        "ln1f": (ln1f[cstate["par"] ^ 1][:, :, :], r_ln1f[cstate["par"] ^ 1], "attn"),
        "mixo0": (mixo[0][:, :, :], r_mixo[0], "attn"),
        "mixo1": (mixo[1][:, :, :], r_mixo[1], "attn"),
        "kex": (kex_tm[:, :], r_kex_tm, "attn"),
        "qex": (qex_tm[:, :], r_qex_tm, "attn"),
        "vtm": (v_tm[:, :], r_v_tm, "attn"),
        "khst": (khst[:, :], r_khst, "attn"),
        "qexf": (qex_fm[:, :], r_qex_fm, "attn"),
        "kexf": (kex_fm[:, :], r_kex_fm, "attn"),
        "qti": (qtint[:, :], r_qtint, "attn"),
        "Psb": (Psb[:, :], r_Psb, "attn"),
        "attf": (attf[:, :, :], r_attf, "attn"),
        "Ast": (Ast[:, :, :], r_Ast, "attn"),
        "gfm": (scr3[:, :], r_scr3, "attn"),
        "nscs": (nsc_s[:, :], r_nsc, "attn"),
        "mixb": (mixb[:, :, :], r_mixb, "attn"),
    })
    emit_taps("attn")

    S.barrier()
    ctx_attn.close()
    ctx_a = ExitStack()
    wgu, r_wgu = mksb(ctx_a, "wgu", [128, 2, CC, INNER], BF16)
    hq, r_hq = mksb(ctx_a, "hq", [128, INNER], F32)
    hqb, r_hqb = zip(*[mksb(ctx_a, f"hqb{i}", [128, INNER], BF16) for i in range(2)])
    aq1, r_aq1 = mksb(ctx_a, "aq1", [128, C], BF16)
    aq1f, r_aq1f = zip(*[mksb(ctx_a, f"aq1f{i}", [128, C], BF16) for i in range(2)])
    sxm, r_sxm = zip(*[mksb(ctx_a, f"sxm{i}", [128, C], F32) for i in range(2)])
    fsc1, r_fsc1 = mksb(ctx_a, "fsc1", [128, C], F32)
    fsc2, r_fsc2 = mksb(ctx_a, "fsc2", [128, C], F32)
    fst2, r_fst2 = zip(*[mksb(ctx_a, f"fst{i}", [128, 12], F32) for i in range(2)])
    sil, r_sil = zip(*[mksb(ctx_a, f"sil{i}", [128, 512], F32) for i in range(2)])
    hsp, r_hsp = zip(*[mksb(ctx_a, f"hsp{i}", [128, 1024], BF16) for i in range(2)])

    r_wgu_s = [Res(f"wgu_s{s}") for s in range(NSL)]
    for s in range(NSL):
        for m in range(2):
            S.task("sync",
                   lambda e, m=m, s=s: e.dma_start(
                       wgu[:, m, :, s * 512:(s + 1) * 512],
                       gq_d[m, :, :, s * 512:(s + 1) * 512].transpose([1, 0, 2])),
                   reads=[r_gqd], writes=[r_wgu_s[s]], dma=True)

    def ffnA_frontA(ci):
        par = ci % 2
        fst, r_fst = fst2[par], r_fst2[par]
        S.task("scalar",
               lambda e, par=par, ci=ci: e.dma_start(sxm[par][:, :],
                                                     xmid_d[ci * L:(ci + 1) * L, :]),
               reads=[r_xmidd], writes=[r_sxm[par]], dma=True)
        def f1(e, par=par):
            e.scalar_tensor_tensor(fsc1[:, :], sxm[par][:, :], 1.0, sxm[par][:, :],
                                   ALU.mult, ALU.mult, accum_out=fst[:, 0:1])
            return e.tensor_scalar(fsc2[:, :], sxm[par][:, :], 1.0, 0.0, ALU.mult,
                                   ALU.add, accum_out=fst[:, 1:2])
        S.task("vector", f1, reads=[r_sxm[par]], writes=[r_fsc1, r_fsc2, r_fst])
        def f2(e):
            e.tensor_scalar(fst[:, 2:3], fst[:, 1:2], 1.0 / C, None, ALU.mult)
            e.engine_nop()
            e.tensor_tensor(fst[:, 3:4], fst[:, 2:3], fst[:, 2:3], ALU.mult)
            e.engine_nop()
            e.scalar_tensor_tensor(fst[:, 4:5], fst[:, 0:1], 1.0 / C, fst[:, 3:4],
                                   ALU.mult, ALU.subtract)
            e.engine_nop()
            return e.tensor_scalar(fst[:, 4:5], fst[:, 4:5], 1e-5, None, ALU.add)
        S.task("vector", f2, reads=[r_fst], writes=[r_fst])
        S.task("scalar", lambda e: e.activation(fst[:, 5:6], fst[:, 4:5], AF.Ln),
               reads=[r_fst], writes=[r_fst])
        S.task("scalar", lambda e: e.activation(fst[:, 6:7], fst[:, 5:6], AF.Exp,
                                                scale=-0.5),
               reads=[r_fst], writes=[r_fst])
        def f3(e):
            e.tensor_tensor(fst[:, 7:8], fst[:, 2:3], fst[:, 6:7], ALU.mult)
            e.engine_nop()
            return e.tensor_scalar(fst[:, 7:8], fst[:, 7:8], -1.0, None, ALU.mult)
        S.task("vector", f3, reads=[r_fst], writes=[r_fst])
        S.task("scalar",
               lambda e, par=par: e.activation(fsc1[:, :], sxm[par][:, :], AF.Identity,
                                               bias=fst[:, 7:8], scale=fst[:, 6:7]),
               reads=[r_sxm[par], r_fst], writes=[r_fsc1])
        def f4(e):
            return e.reduce_max(fst[:, 8:9], fsc1[:, :], AX.X,
                                apply_absolute_value=True)
        S.task("vector", f4, reads=[r_fsc1], writes=[r_fst])
        def f5(e):
            e.tensor_scalar(fst[:, 8:9], fst[:, 8:9], 1e-5, None, ALU.max)
            e.engine_nop()
            e.reciprocal(fst[:, 9:10], fst[:, 8:9])
            e.engine_nop()
            return e.tensor_scalar(fst[:, 9:10], fst[:, 9:10], 127.0, None, ALU.mult)
        S.task("vector", f5, reads=[r_fst], writes=[r_fst])
        def f6(e):
            e.tensor_scalar(fsc2[:, :], fsc1[:, :], fst[:, 9:10], MAGIC, ALU.mult,
                            ALU.add)
            e.engine_nop()
            return e.tensor_scalar(aq1[:, :], fsc2[:, :], MAGIC, None, ALU.subtract)
        S.task("vector", f6, reads=[r_fsc1, r_fst], writes=[r_fsc2, r_aq1])
        def f7(e):
            e.scalar_tensor_tensor(fst[:, 10:11], mgv_m[:, 0:1], 1.0 / 127.0,
                                   fst[:, 8:9], ALU.mult, ALU.mult)
            e.scalar_tensor_tensor(fst[:, 11:12], mgv_m[:, 1:2], 1.0 / 127.0,
                                   fst[:, 8:9], ALU.mult, ALU.mult)
            return e.memset(fst[:, 0:1], 0.0)
        S.task("vector", f7, reads=[r_mgv_m, r_fst], writes=[r_fst])

    def ffnA_frontB(ci):
        par = ci % 2
        def aqT(e):
            last = None
            for cc in range(CC):
                last = e.matmul(pbt[:, cc * 128:(cc + 1) * 128],
                                aq1[:, cc * 128:(cc + 1) * 128], ident[:, :],
                                is_transpose=True, start=(cc == 0), stop=(cc == CC - 1))
            return last
        S.task("tensor", aqT, reads=[r_aq1, r_ident], writes=[rbt])
        S.task("scalar", lambda e, par=par: e.copy(aq1f[par][:, :], pbt[:, :]),
               reads=[rbt], writes=[r_aq1f[par]])

    def ffnA_backG(ci):
        par = ci % 2
        fst, r_fst = fst2[par], r_fst2[par]
        for s in range(NSL):
            bg, bu = (0, 1) if s % 2 == 0 else (2, 3)
            def gumm(e, s=s, bg=bg, bu=bu, par=par):
                last = None
                for K in range(CC):
                    last = e.matmul(pb[bg][:, :], aq1f[par][:, K * 128:(K + 1) * 128],
                                    wgu[:, 0, K, s * 512:(s + 1) * 512],
                                    start=(K == 0), stop=(K == CC - 1))
                for K in range(CC):
                    last = e.matmul(pb[bu][:, :], aq1f[par][:, K * 128:(K + 1) * 128],
                                    wgu[:, 1, K, s * 512:(s + 1) * 512],
                                    start=(K == 0), stop=(K == CC - 1))
                return last
            S.task("tensor", gumm, reads=[r_aq1f[par], r_wgu_s[s]],
                   writes=[rb[bg], rb[bu]])
            sp = s % 2
            S.task("scalar",
                   lambda e, bg=bg, sp=sp: e.activation(sil[sp][:, :], pb[bg][:, :],
                                                        AF.Silu, scale=fst[:, 10:11]),
                   reads=[rb[bg], r_fst], writes=[r_sil[sp]])
            S.task("vector",
                   lambda e, bu=bu, sp=sp, s=s: e.scalar_tensor_tensor(
                       hq[:, s * 512:(s + 1) * 512], pb[bu][:, :], 1.0, sil[sp][:, :],
                       ALU.mult, ALU.mult),
                   reads=[rb[bu], r_sil[sp]], writes=[r_hq])
            def hmax(e, s=s):
                e.reduce_max(fst[:, 1:2], hq[:, s * 512:(s + 1) * 512], AX.X,
                             apply_absolute_value=True)
                e.engine_nop()
                return e.tensor_tensor(fst[:, 0:1], fst[:, 0:1], fst[:, 1:2], ALU.max)
            S.task("vector", hmax, reads=[r_hq, r_fst], writes=[r_fst])
        def fh(e, ci=ci):
            e.tensor_tensor(fst[:, 2:3], fst[:, 11:12], fst[:, 0:1], ALU.mult)
            e.engine_nop()
            e.tensor_scalar(fst[:, 2:3], fst[:, 2:3], 1e-5, None, ALU.max)
            e.engine_nop()
            e.reciprocal(fst[:, 3:4], fst[:, 2:3])
            e.engine_nop()
            e.tensor_scalar(fst[:, 3:4], fst[:, 3:4], 127.0, None, ALU.mult)
            e.engine_nop()
            e.tensor_tensor(fst[:, 4:5], fst[:, 11:12], fst[:, 3:4], ALU.mult)
            e.engine_nop()
            return e.scalar_tensor_tensor(alpha[:, ci:ci + 1], mgv_m[:, 2:3],
                                          1.0 / 127.0, fst[:, 2:3], ALU.mult, ALU.mult)
        S.task("vector", fh, reads=[r_fst, r_mgv_m], writes=[r_fst, r_alpha])
        def qh(e, par=par):
            e.tensor_scalar(hq[:, :], hq[:, :], fst[:, 4:5], MAGIC, ALU.mult, ALU.add)
            e.engine_nop()
            return e.tensor_scalar(hqb[par][:, :], hq[:, :], MAGIC, None, ALU.subtract)
        S.task("vector", qh, reads=[r_hq, r_fst], writes=[r_hq, r_hqb[par]])

    def ffnA_tail(ci):
        par = ci % 2
        for grp in range(4):
            gp = grp % 2
            def hT(e, grp=grp, par=par):
                last = None
                for tile in range(8):
                    ic = grp * 8 + tile
                    last = e.matmul(pbt[:, tile * 128:(tile + 1) * 128],
                                    hqb[par][:, ic * 128:(ic + 1) * 128], ident[:, :],
                                    is_transpose=True, start=(tile == 0),
                                    stop=(tile == 7))
                return last
            S.task("tensor", hT, reads=[r_hqb[par], r_ident], writes=[rbt])
            S.task("scalar", lambda e, gp=gp: e.copy(hsp[gp][:, :], pbt[:, :]),
                   reads=[rbt], writes=[r_hsp[gp]])
            S.task("sync",
                   lambda e, ci=ci, grp=grp, gp=gp: e.dma_start(hq_d[ci, grp, :, :],
                                                                hsp[gp][:, :]),
                   reads=[r_hsp[gp]], writes=[r_hqd], dma=True)

    ffnA_frontA(0)
    ffnA_frontB(0)
    for ci in range(n_own):
        if ci + 1 < n_own:
            ffnA_frontA(ci + 1)
        ffnA_backG(ci)
        if ci + 1 < n_own:
            ffnA_frontB(ci + 1)
        if ci > 0:
            ffnA_tail(ci - 1)
    ffnA_tail(n_own - 1)

    S.barrier()
    ctx_a.close()
    ctx_b = ExitStack()
    wdq, r_wdq = mksb(ctx_b, "wdq", [128, NIC, C], BF16)
    hqt, r_hqt = zip(*[mksb(ctx_b, f"hqt{i}", [128, INNER], BF16) for i in range(2)])
    sxb, r_sxb = zip(*[mksb(ctx_b, f"sxb{i}", [128, C], F32) for i in range(2)])
    dsc, r_dsc = mksb(ctx_b, "dsc", [128, C], F32)
    bsc, r_bsc = mksb(ctx_b, "bsc", [128, C], F32)
    yb, r_yb = zip(*[mksb(ctx_b, f"yb{i}", [128, C], F32) for i in range(2)])
    bst, r_bst = mksb(ctx_b, "bst", [128, 10], F32)

    r_wdq_g = [Res(f"wdq_g{g}") for g in range(4)]
    for g in range(4):
        S.task("sync",
               lambda e, g=g: e.dma_start(
                   wdq[:, g * 8:(g + 1) * 8, :],
                   dq_d[g * 8:(g + 1) * 8, :, :].transpose([1, 0, 2])),
               reads=[r_dqd], writes=[r_wdq_g[g]], dma=True)

    def emit_ffnB_chunk(ci):
        par = ci % 2
        for g in range(4):
            def dmm(e, par=par, g=g):
                last = None
                for K in range(g * 8, (g + 1) * 8):
                    for nh in range(2):
                        last = e.matmul((pb[0], pb[1])[nh][:, :],
                                        hqt[par][:, K * 128:(K + 1) * 128],
                                        wdq[:, K, nh * 512:(nh + 1) * 512],
                                        start=(K == 0),
                                        stop=(K == NIC - 1))
                return last
            S.task("tensor", dmm, reads=[r_hqt[par], r_wdq_g[g]],
                   writes=[rb[0], rb[1]])
        def b1(e, ci=ci):
            e.tensor_scalar(dsc[:, 0:512], pb[0][:, :], alpha[:, ci:ci + 1], 0.0,
                            ALU.mult, ALU.add, accum_out=bst[:, 0:1])
            return e.tensor_scalar(dsc[:, 512:1024], pb[1][:, :], alpha[:, ci:ci + 1],
                                   0.0, ALU.mult, ALU.add, accum_out=bst[:, 1:2])
        S.task("vector", b1, reads=[rb[0], rb[1], r_alpha], writes=[r_dsc, r_bst])
        S.task("scalar",
               lambda e: e.activation(bsc[:, :], dsc[:, :], AF.Square,
                                      accum_out=bst[:, 2:3]),
               reads=[r_dsc], writes=[r_bsc, r_bst])
        def b3(e):
            e.tensor_tensor(bst[:, 0:1], bst[:, 0:1], bst[:, 1:2], ALU.add)
            e.engine_nop()
            e.tensor_scalar(bst[:, 3:4], bst[:, 0:1], 1.0 / C, None, ALU.mult)
            e.engine_nop()
            e.tensor_tensor(bst[:, 4:5], bst[:, 3:4], bst[:, 3:4], ALU.mult)
            e.engine_nop()
            e.scalar_tensor_tensor(bst[:, 5:6], bst[:, 2:3], 1.0 / C, bst[:, 4:5],
                                   ALU.mult, ALU.subtract)
            e.engine_nop()
            return e.tensor_scalar(bst[:, 5:6], bst[:, 5:6], 1e-5, None, ALU.add)
        S.task("vector", b3, reads=[r_bst], writes=[r_bst])
        S.task("scalar", lambda e: e.activation(bst[:, 6:7], bst[:, 5:6], AF.Ln),
               reads=[r_bst], writes=[r_bst])
        S.task("scalar", lambda e: e.activation(bst[:, 7:8], bst[:, 6:7], AF.Exp,
                                                scale=-0.5),
               reads=[r_bst], writes=[r_bst])
        def b4(e):
            e.tensor_tensor(bst[:, 8:9], bst[:, 3:4], bst[:, 7:8], ALU.mult)
            e.engine_nop()
            return e.tensor_scalar(bst[:, 8:9], bst[:, 8:9], -1.0, None, ALU.mult)
        S.task("vector", b4, reads=[r_bst], writes=[r_bst])
        S.task("scalar",
               lambda e, par=par: e.activation(yb[par][:, :], sxb[par][:, :], AF.Identity,
                                               bias=bst[:, 8:9]),
               reads=[r_sxb[par], r_bst], writes=[r_yb[par]])
        S.task("vector",
               lambda e, par=par: e.scalar_tensor_tensor(yb[par][:, :], dsc[:, :],
                                                         bst[:, 7:8], yb[par][:, :],
                                                         ALU.mult, ALU.add),
               reads=[r_dsc, r_bst, r_yb[par]], writes=[r_yb[par]])
        S.task("sync",
               lambda e, par=par, ci=ci: e.dma_start(y[ci * L:(ci + 1) * L, :],
                                                     yb[par][:, :]),
               reads=[r_yb[par]], dma=True)

    def ffnB_front(ci):
        par = ci % 2
        S.task("scalar",
               lambda e, par=par, ci=ci: e.dma_start(
                   hqt[par][:, :].rearrange("p (g q) -> p g q", g=4),
                   hq_d[ci, :, :, :].transpose([1, 0, 2])),
               reads=[r_hqd], writes=[r_hqt[par]], dma=True)
        S.task("scalar",
               lambda e, par=par, ci=ci: e.dma_start(sxb[par][:, :],
                                                     xmid_d[ci * L:(ci + 1) * L, :]),
               reads=[r_xmidd], writes=[r_sxb[par]], dma=True)

    ffnB_front(0)
    for ci in range(n_own):
        if ci + 1 < n_own:
            ffnB_front(ci + 1)
        emit_ffnB_chunk(ci)

    with nc.Block() as block:
        S.emit(nc, block)
    ctx_b.close()
    ctx_common.close()
    return nc


def _host_constants():
    p = np.arange(128, dtype=np.float32)
    pos = {
        "q8": (p % 8) - 3.5, "k8": 3.5 - (p % 8),
        "q8o": (p % 16) - 7.5, "k8o": 7.5 - (p % 16),
        "q16": (p % 32) - 15.5, "k16": 15.5 - (p % 32),
        "q32": (p % 32) + 0.5, "k32": 31.5 - (p % 32),
        "q64": (p % 64) + 0.5, "k64": 63.5 - p,
        "qint": p + 1.0, "kst": 127.0 - p,
    }
    posv = np.stack([pos[k] for k in POS_ORDER])[None, :, :].astype(np.float32)
    j = np.arange(128)[:, None]
    i = np.arange(128)[None, :]
    m8 = ((j // 8 == i // 8) & (j <= i)).astype(np.float32)
    m8o = ((j // 16 == i // 16) & ((i // 8) % 2 == 1)
           & ((j // 8) % 2 == 0)).astype(np.float32)
    m16 = ((j // 32 == i // 32) & ((i // 16) % 2 == 1)
           & ((j // 16) % 2 == 0)).astype(np.float32)
    m32 = ((j // 64 == i // 64) & ((i // 32) % 2 == 1)
           & ((j // 32) % 2 == 0)).astype(np.float32)
    m64 = ((j < 64) & (i >= 64)).astype(np.float32)
    masks = np.stack([m8, m8o, m16, m32, m64], axis=1).astype(np.float32)
    ident = np.eye(128, dtype=BF)
    return posv, masks, ident


_CACHE = {}


def _get_program(n_own=NOWN, n_pre=NPRE, debug_taps=()):
    key = (n_own, n_pre, tuple(debug_taps))
    if key not in _CACHE:
        _CACHE[key] = build_program(n_own, n_pre, debug_taps)
    return _CACHE[key]


def make_in_maps(inputs, n_own=NOWN, n_pre=NPRE, n_cores=8):
    x = np.asarray(inputs["L0_attn_x"], dtype=np.float32)
    td = np.asarray(inputs["L0_attn_time_decay"], dtype=np.float32).reshape(1, C)
    nsc = np.asarray(inputs["L0_attn_norm_scale"], dtype=np.float32)
    mixes = [np.asarray(inputs["L0_attn_mix_" + n], dtype=np.float32).reshape(C)
             for n in ("q", "k", "v", "g")]
    mix_bc = np.stack([np.repeat(m.reshape(CC, 128).T[:, :, None], 128, axis=2)
                       .reshape(128, C) for m in mixes], axis=1).astype(np.float32)
    nsc_bc = np.ascontiguousarray(np.broadcast_to(nsc.reshape(1, C), (128, C)))
    posv, masks, ident = _host_constants()

    wT = {k: np.ascontiguousarray(np.asarray(inputs[f"L0_attn_W{k}"],
                                             dtype=np.float32).T)
          for k in ("q", "k", "v", "g", "o")}
    wgateT = np.ascontiguousarray(np.asarray(inputs["L0_ffn_w_gate"], np.float32).T)
    wupT = np.ascontiguousarray(np.asarray(inputs["L0_ffn_w_up"], np.float32).T)
    wdownT = np.ascontiguousarray(np.asarray(inputs["L0_ffn_w_down"], np.float32).T)

    NTOK = n_own * L
    NPRET = max(n_pre, 1) * L
    in_maps = []
    for core in range(n_cores):
        b, half = core // 2, core % 2
        start = half * (T // 2)
        xin = np.ascontiguousarray(x[b, start:start + NTOK])
        if half == 0 or n_pre == 0:
            xpre = np.zeros((NPRET, C), np.float32)
            fl = np.zeros((128, 1), np.float32)
        else:
            xpre = np.ascontiguousarray(x[b, start - NPRET:start])
            fl = np.ones((128, 1), np.float32)
        in_maps.append(dict(
            xin=xin, xpre=xpre, flag=fl, td=td, nsc_bc=nsc_bc, mix_bc=mix_bc,
            wqT=wT["q"], wkT=wT["k"], wvT=wT["v"], wgT=wT["g"], woT=wT["o"],
            wgateT=wgateT, wupT=wupT, wdownT=wdownT,
            ident_i=ident, masks_i=masks, posv=posv,
        ))
    return in_maps


def kernel(**inputs):
    nc = _get_program()
    in_maps = make_in_maps(inputs)
    res = run_bass_kernel_spmd(nc, in_maps, core_ids=list(range(8)))
    out = np.empty((B, T, C), np.float32)
    for core in range(8):
        b, half = core // 2, core % 2
        out[b, half * (T // 2):(half + 1) * (T // 2)] = res.results[core]["y"]
    return out


# version 21 (final)
# speedup vs baseline: 1.0167x; 1.0129x over previous
import numpy as np
import ml_dtypes
from contextlib import ExitStack

import concourse.bass as bass
import concourse.mybir as mybir
from concourse.bass_utils import run_bass_kernel_spmd

F32 = mybir.dt.float32
BF16 = mybir.dt.bfloat16
AF = mybir.ActivationFunctionType
ALU = mybir.AluOpType
AX = mybir.AxisListType

MAGIC = float(np.float32(1.5 * 2 ** 23))
BF = ml_dtypes.bfloat16

B, T, C, H, D = 4, 4096, 1024, 4, 256
L = 128
CC = 8
NOWN = 16
NPRE = 8
INNER = 4096
NIC = 32
NSL = 8

POS_ORDER = ["q8", "k8", "q8o", "k8o", "q16", "k16", "q32", "k32",
             "q64", "k64", "qint", "kst"]
TI = {n: i for i, n in enumerate(POS_ORDER)}


class Res:
    __slots__ = ("name", "writer", "readers")

    def __init__(self, name):
        self.name, self.writer, self.readers = name, None, []


class Task:
    __slots__ = ("tid", "engine", "stream", "fn", "deps", "inc", "cum", "needed")


class Sched:
    ENGINES = ("tensor", "vector", "scalar", "sync", "gpsimd")

    def __init__(self):
        self.tasks = []

    def task(self, engine, fn, reads=(), writes=(), deps=(), dma=False):
        t = Task()
        t.tid = len(self.tasks)
        t.engine = engine
        t.stream = ("dma_" + engine) if dma else engine
        t.fn = fn
        t.inc = 16 if dma else 1
        dep = set()
        for r in reads:
            if r.writer is not None:
                dep.add(r.writer)
        for w in writes:
            if w.writer is not None:
                dep.add(w.writer)
            for rd in w.readers:
                dep.add(rd)
        for d in deps:
            if d is not None:
                dep.add(d)
        t.deps = sorted(dep, key=lambda d: d.tid)
        for d in t.deps:
            assert d.tid < t.tid
        for r in reads:
            r.readers.append(t)
        for w in writes:
            w.writer = t
            w.readers = []
        self.tasks.append(t)
        return t

    def barrier(self):
        last = {}
        for t in self.tasks:
            last[t.stream] = t
        deps = list(last.values())
        for e in self.ENGINES:
            self.task(e, lambda eng: eng.nop(), deps=deps)

    def emit(self, nc, block):
        NSEM = 6
        streams = {}
        for t in self.tasks:
            streams.setdefault(t.stream, []).append(t)
        sems, guards = {}, []
        for s, ts in streams.items():
            if ts and ts[0].inc == 16:
                hs = []
                for i in range(NSEM):
                    g = nc.semaphore(f"sem_{s}_{i}")
                    hs.append(g.__enter__())
                    guards.append(g)
                sems[s] = hs
            else:
                g = nc.semaphore("sem_" + s)
                sems[s] = g.__enter__()
                guards.append(g)

        def skip_edge(d, t):
            return False

        for t in self.tasks:
            t.needed = (t.inc == 16)
        for t in self.tasks:
            for d in t.deps:
                if not skip_edge(d, t):
                    d.needed = True

        consumers = {}
        for t in self.tasks:
            for d in t.deps:
                if d.inc == 16:
                    consumers.setdefault(d.tid, []).append(t.tid)

        wave_wait = {}
        totals = {}
        for s, ts in streams.items():
            if not ts or ts[0].inc != 16:
                c = 0
                for t in ts:
                    if t.needed:
                        c += t.inc
                    t.cum = c
                totals[s] = c
                continue
            cums = [0] * NSEM
            hist = []
            for w, t in enumerate(ts):
                si = w % NSEM
                cums[si] += 16
                t.cum = (si, cums[si])
                hist.append((si, cums[si]))
                if w >= NSEM:
                    wave_wait[t.tid] = hist[w - NSEM]
            totals[s] = cums

        def body_for(engine):
            mine = [t for t in self.tasks if t.engine == engine]

            def body(eng):
                marks = {}
                for t in mine:
                    want = {}
                    if t.inc == 16 and t.tid in wave_wait:
                        si, v = wave_wait[t.tid]
                        want[(t.stream, si)] = v
                    for d in t.deps:
                        if skip_edge(d, t):
                            continue
                        assert d.needed
                        if d.inc == 16:
                            si, v = d.cum
                            key = (d.stream, si)
                        else:
                            key, v = (d.stream, 0), d.cum
                        want[key] = max(want.get(key, 0), v)
                    for (s, si), v in sorted(want.items()):
                        if marks.get((s, si), 0) < v:
                            h = sems[s][si] if isinstance(sems[s], list) else sems[s]
                            eng.wait_ge(h, v)
                            marks[(s, si)] = v
                    inst = t.fn(eng)
                    if t.needed:
                        assert inst is not None, f"task {t.tid} ({engine}) returned None"
                        if t.inc == 16:
                            inst.then_inc(sems[t.stream][t.cum[0]], 16)
                        else:
                            inst.then_inc(sems[t.stream], 1)
                if engine == "sync" and "dma_sync" in totals:
                    for i, v in enumerate(totals["dma_sync"]):
                        if v:
                            eng.wait_ge(sems["dma_sync"][i], v)
            return body

        for e in self.ENGINES:
            getattr(block, e)(body_for(e))
        return guards


def build_program(n_own=NOWN, n_pre=NPRE, debug_taps=(), debug_spill=False,
                  detect_races=False):
    nc = bass.Bass("TRN2", target_bir_lowering=False,
                   detect_race_conditions=detect_races)
    _lp = nc.allow_low_precision("bf16 matmul operands by design; fp32 accumulate")
    _lp.__enter__()
    nc._lp_guard = _lp
    S = Sched()
    NTOK = n_own * L
    NPRET = max(n_pre, 1) * L

    def din(name, shape, dt=F32):
        return nc.dram_tensor(name, shape, dt, kind="ExternalInput").ap()

    xin = din("xin", [NTOK, C])
    xpre = din("xpre", [NPRET, C])
    flag = din("flag", [128, 1])
    td = din("td", [1, C])
    nsc_bc = din("nsc_bc", [128, C])
    mix_bc = din("mix_bc", [128, 4, C])
    wqT = din("wqT", [C, C]); wkT = din("wkT", [C, C]); wvT = din("wvT", [C, C])
    wgT = din("wgT", [C, C]); woT = din("woT", [C, C])
    wgateT = din("wgateT", [C, INNER]); wupT = din("wupT", [C, INNER])
    wdownT = din("wdownT", [INNER, C])
    ident_in = din("ident_i", [128, 128], BF16)
    masks_in = din("masks_i", [128, 5, 128])
    posv_in = din("posv", [1, 12, 128])
    y = nc.dram_tensor("y", [NTOK, C], F32, kind="ExternalOutput").ap()

    tap_reg = {}
    def emit_taps(point):
        for tn in list(debug_taps):
            if tn in tap_reg:
                ap, res, when = tap_reg[tn]
                if when != point:
                    continue
                o = nc.dram_tensor("tap_" + tn, list(ap.shape), ap.dtype,
                                   kind="ExternalOutput").ap()
                S.task("sync", lambda e, o=o, s=ap: e.dma_start(o, s),
                       reads=[res], dma=True)

    _ki = "ExternalOutput" if debug_spill else "Internal"
    xmid_d = nc.dram_tensor("xmid_d", [NTOK, C], F32, kind=_ki).ap()
    hq_d = nc.dram_tensor("hq_d", [n_own, 4, 128, 1024], BF16, kind=_ki).ap()
    gq_d = nc.dram_tensor("gq_d", [2, CC, 128, INNER], BF16, kind=_ki).ap()
    dq_d = nc.dram_tensor("dq_d", [NIC, 128, C], BF16, kind=_ki).ap()
    r_xmidd, r_hqd, r_gqd, r_dqd = Res("xmid_d"), Res("hq_d"), Res("gq_d"), Res("dq_d")

    ctx_common = ExitStack()
    ctx_attn = ExitStack()
    ctx_pre = ExitStack()

    def mksb(ctx, name, shape, dt):
        t = ctx.enter_context(nc.sbuf_tensor(name, shape, dt))
        return t, Res(name)

    pb = [ctx_common.enter_context(nc.psum_tensor(f"pb{i}", [128, 512], F32))
          for i in range(7)]
    pbt = ctx_common.enter_context(nc.psum_tensor("pbt", [128, 1024], BF16))
    rb = [Res(f"pb{i}") for i in range(7)]
    rbt = Res("pbt")

    ident, r_ident = mksb(ctx_common, "ident", [128, 128], BF16)
    mgv_m, r_mgv_m = mksb(ctx_common, "mgv_m", [128, 3], F32)
    alpha, r_alpha = mksb(ctx_common, "alpha", [128, max(n_own, 1)], F32)

    A = ctx_attn

    ones1, r_ones1 = mksb(A, "ones1", [1, 128], F32)
    onesc_bf, r_onesc = mksb(A, "onesc_bf", [128, 1], BF16)
    onesf_col, r_onesf = mksb(A, "onesf_col", [128, 1], F32)
    flag_s, r_flag = mksb(A, "flag_s", [128, 1], F32)
    nsc_s, r_nsc = mksb(A, "nsc_s", [128, C], F32)
    mixb, r_mixb = mksb(A, "mixb", [128, 4, C], BF16)
    wL, r_wL = mksb(A, "wL", [128, CC], F32)
    TAB, r_TAB = mksb(A, "TAB", [128, 12, C], BF16)
    mgv, r_mgv = mksb(A, "mgv", [128, 3], F32)
    asum, r_asum = mksb(A, "asum", [128, 3], F32)
    Ast, r_Ast = mksb(A, "Ast", [128, CC, 257], F32)
    Abf, r_Abf = mksb(A, "Abf", [128, CC, 257], BF16)
    sx, r_sx = zip(*[mksb(A, f"sx{i}", [128, C], F32) for i in range(2)])
    ln1f, r_ln1f = zip(*[mksb(A, f"ln1f{i}", [128, CC, 132], BF16) for i in range(2)])
    mixo, r_mixo = zip(*[mksb(A, n, [128, CC, 128], BF16)
                         for n in ("xq", "xk", "xv")])
    mixg, r_mixg = zip(*[mksb(A, f"xg{i}", [128, CC, 128], BF16) for i in range(2)])
    diff, r_diff = mksb(A, "diff", [128, CC, 128], BF16)
    qex_tm, r_qex_tm = mksb(A, "qex_tm", [128, C], BF16)
    kex_tm, r_kex_tm = mksb(A, "kex_tm", [128, C], BF16)
    qex_fm, r_qex_fm = mksb(A, "qex_fm", [128, C], BF16)
    kex_fm, r_kex_fm = mksb(A, "kex_fm", [128, C], BF16)
    v_tm, r_v_tm = mksb(A, "v_tm", [128, C], BF16)
    khst, r_khst = mksb(A, "khst", [128, C], BF16)
    qtint, r_qtint = mksb(A, "qtint", [128, C], BF16)
    mkq1, r_mkq1 = mksb(A, "mkq", [128, C], BF16)
    mkk1, r_mkk1 = mksb(A, "mkk", [128, C], BF16)
    mkq, r_mkq = (mkq1, mkq1), (r_mkq1, r_mkq1)
    mkk, r_mkk = (mkk1, mkk1), (r_mkk1, r_mkk1)
    Psb, r_Psb = mksb(A, "Psb", [128, 512], BF16)
    attf, r_attf = mksb(A, "attf", [128, CC, 128], BF16)
    rzrow, r_rzrow = mksb(A, "rzrow", [1, 512], F32)
    rzbc, r_rzbc = mksb(A, "rzbc", [128, 512], BF16)
    scr1, r_scr1 = mksb(A, "scr1", [128, C], F32)
    ln1tm_t, r_ln1tm = mksb(A, "ln1tm", [128, C], BF16)
    scr3, r_scr3 = mksb(A, "scr3", [128, C], F32)
    st, r_st = zip(*[mksb(A, f"st{i}", [128, 8], F32) for i in range(4)])
    stL, r_stL = mksb(A, "stL", [128, 10], F32)
    wq_s, r_wq = mksb(A, "wq_s", [128, CC, C], BF16)
    wk_s, r_wk = mksb(A, "wk_s", [128, CC, C], BF16)
    wv_s, r_wv = mksb(A, "wv_s", [128, CC, C], BF16)
    wg_s, r_wg = mksb(A, "wg_s", [128, CC, C], BF16)
    wo_s, r_wo = mksb(A, "wo_s", [128, CC, C], BF16)

    c128, r_c128 = mksb(ctx_pre, "c128", [1, 1], F32)
    lw, r_lw = mksb(ctx_pre, "lw", [1, C], F32)
    posv_s, r_posv = mksb(ctx_pre, "posv_s", [1, 12, 128], F32)

    def dma_in(dst_ap, src_ap, wres, engine="sync", rres=()):
        return S.task(engine, lambda e, d=dst_ap, s=src_ap: e.dma_start(d, s),
                      reads=rres, writes=[wres], dma=True)

    dma_in(ident[:, :], ident_in, r_ident)
    dma_in(posv_s[:, :, :], posv_in, r_posv)
    dma_in(flag_s[:, :], flag, r_flag)
    dma_in(lw[:, :], td, r_lw)
    dma_in(nsc_s[:, :], nsc_bc, r_nsc)

    for wdst, rres, wsrc in ((wq_s, r_wq, wqT), (wk_s, r_wk, wkT), (wv_s, r_wv, wvT),
                             (wg_s, r_wg, wgT), (wo_s, r_wo, woT)):
        src = wsrc.rearrange("(cc p) o -> p cc o", p=128)
        S.task("gpsimd", lambda e, d=wdst, s=src: e.dma_start(d[:, :, :], s),
               writes=[rres], dma=True)

    S.task("vector", lambda e: e.memset(ones1[:, :], 1.0), writes=[r_ones1])
    S.task("vector", lambda e: e.memset(onesc_bf[:, :], 1.0), writes=[r_onesc])
    S.task("vector", lambda e: e.memset(onesf_col[:, :], 1.0), writes=[r_onesf])
    S.task("vector", lambda e: e.memset(c128[:, :], 128.0), writes=[r_c128])
    S.task("vector", lambda e: e.memset(Ast[:, :, :], 0.0), writes=[r_Ast])
    S.task("vector", lambda e: e.memset(Abf[:, :, :], 0.0), writes=[r_Abf])
    S.task("vector", lambda e: e.memset(asum[:, :], 0.0), writes=[r_asum])
    S.task("gpsimd", lambda e: e.dma_start(mixb[:, :, :], mix_bc),
           writes=[r_mixb], dma=True)
    S.task("scalar", lambda e: e.activation(lw[:, :], lw[:, :], AF.Exp),
           reads=[r_lw], writes=[r_lw])
    S.task("vector", lambda e: e.tensor_scalar(lw[:, :], lw[:, :], -1.0, None, ALU.mult),
           reads=[r_lw], writes=[r_lw])

    for ti, pname in enumerate(POS_ORDER):
        pvec = posv_s[:, ti, :]
        if pname == "kst":
            def mk_out(e, pv=pvec):
                e.matmul(pb[0][:, :], pv, lw[:, 0:512], start=True, stop=True)
                return e.matmul(pb[1][:, :], pv, lw[:, 512:1024], start=True, stop=True)
        else:
            def mk_out(e, pv=pvec):
                last = None
                for cc in range(CC):
                    bank = pb[cc // 4]
                    col = (cc % 4) * 128
                    last = e.matmul(bank[:, col:col + 128],
                                    lw[:, cc * 128:(cc + 1) * 128], pv,
                                    start=(cc % 4 == 0), stop=(cc % 4 == 3))
                return last
        S.task("tensor", mk_out, reads=[r_lw, r_posv], writes=[rb[0], rb[1]])
        def mk_clamp(e):
            e.tensor_scalar(scr1[:, 0:512], pb[0][:, :], 30.0, None, ALU.min)
            return e.tensor_scalar(scr1[:, 512:1024], pb[1][:, :], 30.0, None, ALU.min)
        S.task("vector", mk_clamp, reads=[rb[0], rb[1]], writes=[r_scr1])
        S.task("scalar", lambda e, ti=ti: e.activation(TAB[:, ti, :], scr1[:, :], AF.Exp),
               reads=[r_scr1], writes=[r_TAB])

    def mk_wl(e):
        last = None
        for cc in range(CC):
            last = e.matmul(pb[2][:, cc:cc + 1], lw[:, cc * 128:(cc + 1) * 128],
                            c128[:, :], start=(cc == 0), stop=(cc == CC - 1))
        return last
    S.task("tensor", mk_wl, reads=[r_lw, r_c128], writes=[rb[2]])
    S.task("scalar", lambda e: e.activation(wL[:, :], pb[2][:, 0:CC], AF.Exp),
           reads=[rb[2]], writes=[r_wL])
    S.barrier()
    ctx_pre.close()
    masks, r_masks = mksb(A, "masks", [128, 5, 128], F32)
    dma_in(masks[:, :, :], masks_in, r_masks)
    stg, r_stg = zip(*[mksb(A, f"stg{i}", [128, 1024], F32) for i in range(2)])
    stgs, r_stgs = mksb(A, "stgs", [128, 1024], F32)

    def prep_jobs(pass2):
        jobs = []
        specs = ((0, wgateT, CC, INNER, 1024), (1, wupT, CC, INNER, 1024),
                 (2, wdownT, NIC, C, 1024))
        for m, src, rows, cols, w in specs:
            per_row = cols // w
            for r0 in range(rows):
                for h in range(per_row):
                    c0 = h * w
                    sl = src[r0 * 128:(r0 + 1) * 128, c0:c0 + w]
                    if not pass2:
                        jobs.append((m, sl, None, w))
                    else:
                        if m < 2:
                            dst = gq_d[m, r0, :, c0:c0 + w]
                        else:
                            dst = dq_d[r0, :, c0:c0 + w]
                        jobs.append((m, sl, dst, w))
        return jobs

    prep_state = {"phase": 1, "jobs": prep_jobs(False), "i": 0,
                  "pending": [], "done": False}

    def finish_pass1():
        def sums(e):
            last = None
            for m in range(3):
                last = e.matmul(pb[2][0:1, m:m + 1], asum[:, m:m + 1],
                                onesf_col[:, 0:1], start=(m == 0), stop=(m == 2))
            return last
        S.task("tensor", sums, reads=[r_asum, r_onesf], writes=[rb[2]])
        def mw(e):
            n0 = float(C * INNER)
            e.tensor_scalar(st[3][0:1, 0:3], pb[2][0:1, 0:3], 1.0 / n0, 1e-5,
                            ALU.mult, ALU.max)
            e.engine_nop()
            return e.reciprocal(st[3][0:1, 3:6], st[3][0:1, 0:3])
        S.task("vector", mw, reads=[rb[2]], writes=[r_st[3]])
        S.task("tensor",
               lambda e: e.matmul(pb[2][:, 0:6], ones1[:, :], st[3][0:1, 0:6],
                                  start=True, stop=True),
               reads=[r_st[3], r_ones1], writes=[rb[2]])
        def split(e):
            e.tensor_copy(mgv_m[:, :], pb[2][:, 0:3])
            return e.tensor_copy(mgv[:, :], pb[2][:, 3:6])
        S.task("vector", split, reads=[rb[2]], writes=[r_mgv, r_mgv_m])

    def prep_compute(bi, m, dst, w, phase):
        if phase == 1:
            S.task("scalar",
                   lambda e, bi=bi, w=w: e.activation(
                       stgs[:, 0:w], stg[bi][:, 0:w], AF.Abs,
                       accum_out=st[3][:, 6:7]),
                   reads=[r_stg[bi]], writes=[r_stgs, r_st[3]])
            S.task("vector",
                   lambda e, m=m: e.tensor_tensor(
                       asum[:, m:m + 1], asum[:, m:m + 1], st[3][:, 6:7],
                       ALU.add),
                   reads=[r_st[3], r_asum], writes=[r_asum])
        else:
            def q1(e, bi=bi, m=m, w=w):
                e.tensor_scalar(stgs[:, 0:w], stg[bi][:, 0:w], mgv[:, m:m + 1],
                                MAGIC, ALU.mult, ALU.add)
                e.engine_nop()
                return e.tensor_scalar(stgs[:, 0:w], stgs[:, 0:w], MAGIC, -1.0,
                                       ALU.subtract, ALU.max)
            S.task("vector", q1, reads=[r_stg[bi], r_mgv], writes=[r_stgs])
            S.task("vector",
                   lambda e, bi=bi, w=w: e.tensor_scalar(
                       stg[bi][:, :].bitcast(BF16)[:, 0:w], stgs[:, 0:w], 1.0,
                       None, ALU.min),
                   reads=[r_stgs], writes=[r_stg[bi]])
            S.task("sync",
                   lambda e, bi=bi, dst=dst, w=w: e.dma_start(
                       dst, stg[bi][:, :].bitcast(BF16)[:, 0:w]),
                   reads=[r_stg[bi]], writes=[r_gqd if m < 2 else r_dqd],
                   dma=True)

    def prep_point():
        ps = prep_state
        if ps["done"]:
            return True
        for bi, m, dst, w, phase in ps["pending"]:
            prep_compute(bi, m, dst, w, phase)
        ps["pending"] = []
        if ps["i"] >= len(ps["jobs"]):
            if ps["phase"] == 1:
                finish_pass1()
                ps["phase"], ps["jobs"], ps["i"] = 2, prep_jobs(True), 0
            else:
                ps["done"] = True
                return True
        for bi in range(2):
            if ps["i"] < len(ps["jobs"]):
                m, src_, dst, w = ps["jobs"][ps["i"]]
                ps["i"] += 1
                dma_in(stg[bi][:, 0:w], src_, r_stg[bi], engine="gpsimd")
                ps["pending"].append((bi, m, dst, w, ps["phase"]))
        return False

    cstate = {"par": 0, "first": True}
    chctx = {}

    def attn_frontA(kind, ci):
        par = cstate["par"]; cstate["par"] ^= 1
        first = cstate["first"]; cstate["first"] = False
        chctx[(kind, ci)] = (par, first)
        own = (kind == "own")
        xsrc = (xin if own else xpre)[ci * L:(ci + 1) * L, :]
        dma_in(sx[par][:, :], xsrc, r_sx[par], engine="scalar")

        stv, r_stv = st[0], r_st[0]
        dsink = diff[:, :, :]
        def stats(e, par=par):
            e.scalar_tensor_tensor(dsink, sx[par][:, :].rearrange("p (a b) -> p a b", a=CC),
                                   1.0, sx[par][:, :].rearrange("p (a b) -> p a b", a=CC),
                                   ALU.mult, ALU.mult, accum_out=stv[:, 0:1])
            return e.tensor_scalar(dsink, sx[par][:, :].rearrange("p (a b) -> p a b", a=CC),
                                   1.0, 0.0, ALU.mult, ALU.add, accum_out=stv[:, 1:2])
        S.task("vector", stats, reads=[r_sx[par]], writes=[r_diff, r_stv])
        def stats2(e):
            e.tensor_scalar(stv[:, 2:3], stv[:, 1:2], 1.0 / C, None, ALU.mult)
            e.engine_nop()
            e.tensor_tensor(stv[:, 3:4], stv[:, 2:3], stv[:, 2:3], ALU.mult)
            e.engine_nop()
            e.scalar_tensor_tensor(stv[:, 4:5], stv[:, 0:1], 1.0 / C, stv[:, 3:4],
                                   ALU.mult, ALU.subtract)
            e.engine_nop()
            return e.tensor_scalar(stv[:, 4:5], stv[:, 4:5], 1e-5, None, ALU.add)
        S.task("vector", stats2, reads=[r_stv], writes=[r_stv])
        S.task("scalar", lambda e: e.activation(stv[:, 5:6], stv[:, 4:5], AF.Ln),
               reads=[r_stv], writes=[r_stv])
        S.task("scalar", lambda e: e.activation(stv[:, 6:7], stv[:, 5:6], AF.Exp,
                                                scale=-0.5),
               reads=[r_stv], writes=[r_stv])
        def stats4(e):
            e.tensor_tensor(stv[:, 7:8], stv[:, 2:3], stv[:, 6:7], ALU.mult)
            e.engine_nop()
            return e.tensor_scalar(stv[:, 7:8], stv[:, 7:8], -1.0, None, ALU.mult)
        S.task("vector", stats4, reads=[r_stv], writes=[r_stv])
        S.task("scalar",
               lambda e, par=par: e.activation(ln1tm_t[:, 0:C], sx[par][:, :],
                                               AF.Identity, bias=stv[:, 7:8],
                                               scale=stv[:, 6:7]),
               reads=[r_sx[par], r_stv], writes=[r_ln1tm])

    def attn_frontB(kind, ci):
        par, first = chctx[(kind, ci)]
        own = (kind == "own")
        def ln1T(e):
            last = None
            for cc in range(CC):
                last = e.matmul(pbt[:, cc * 128:(cc + 1) * 128],
                                ln1tm_t[:, cc * 128:(cc + 1) * 128], ident[:, :],
                                is_transpose=True, start=(cc == 0), stop=(cc == CC - 1))
            return last
        S.task("tensor", ln1T, reads=[r_ln1tm, r_ident], writes=[rbt])
        if first:
            S.task("vector", lambda e, par=par: e.memset(ln1f[par][:, :, 0:1], 0.0),
                   writes=[r_ln1f[par]])
        else:
            S.task("vector",
                   lambda e, par=par: e.tensor_copy(ln1f[par][:, :, 0:1],
                                                    ln1f[1 - par][:, :, 128:129]),
                   reads=[r_ln1f[1 - par]], writes=[r_ln1f[par]])
        S.task("scalar",
               lambda e, par=par: e.copy(ln1f[par][:, :, 1:129],
                                         pbt[:, :].rearrange("p (a b) -> p a b", a=CC)),
               reads=[rbt], writes=[r_ln1f[par]])
        lcur = ln1f[par][:, :, 1:129]
        lprev = ln1f[par][:, :, 0:128]
        S.task("vector", lambda e: e.tensor_tensor(diff[:, :, :], lcur, lprev,
                                                   ALU.subtract),
               reads=[r_ln1f[par]], writes=[r_diff])
        for mi in (range(4) if own else (1, 2)):
            dst = mixg[par] if mi == 3 else mixo[mi]
            rdst = r_mixg[par] if mi == 3 else r_mixo[mi]
            def domix(e, mi=mi, dst=dst):
                e.tensor_tensor(dst[:, :, :], diff[:, :, :],
                                mixb[:, mi, :].rearrange("p (a b) -> p a b", a=CC),
                                ALU.mult)
                e.engine_nop()
                return e.tensor_tensor(dst[:, :, :], dst[:, :, :], lprev, ALU.add)
            S.task("vector", domix, reads=[r_diff, r_mixb, r_ln1f[par]],
                   writes=[rdst])

    def attn_backSproj(kind, ci):
        own = (kind == "own")
        def proj_tm(mi, wten, b0, b1):
            def fn(e):
                last = None
                for K in range(CC):
                    for nh in range(2):
                        last = e.matmul((pb[b0], pb[b1])[nh][:, :],
                                        mixo[mi][:, K, :],
                                        wten[:, K, nh * 512:(nh + 1) * 512],
                                        start=(K == 0), stop=(K == CC - 1))
                return last
            return fn
        S.task("tensor", proj_tm(1, wk_s, 0, 1), reads=[r_mixo[1], r_wk],
               writes=[rb[0], rb[1]])
        S.task("tensor", proj_tm(2, wv_s, 2, 3), reads=[r_mixo[2], r_wv],
               writes=[rb[2], rb[3]])
        if own:
            S.task("tensor", proj_tm(0, wq_s, 4, 5), reads=[r_mixo[0], r_wq],
                   writes=[rb[4], rb[5]])

    def attn_backSrest(kind, ci):
        par, first = chctx[(kind, ci)]
        own = (kind == "own")

        def rms_sq(pbank0, pbank1, rbank0, rbank1, stcol, r_stc, sink):
            def s1(e):
                e.activation(sink[:, 0:512], pbank0[:, :], AF.Square,
                             accum_out=stcol[:, 0:1])
                return e.activation(sink[:, 512:1024], pbank1[:, :], AF.Square,
                                    accum_out=stcol[:, 1:2])
            S.task("scalar", s1, reads=[rbank0, rbank1], writes=[r_scr1, r_stc])
        rms_sq(pb[0], pb[1], rb[0], rb[1], st[1], r_st[1], scr1)
        if own:
            rms_sq(pb[4], pb[5], rb[4], rb[5], st[2], r_st[2], scr1)
        def rms_add(stcol, r_stc):
            def fn(e):
                e.tensor_tensor(stcol[:, 2:3], stcol[:, 0:1],
                                stcol[:, 1:2], ALU.add)
                e.engine_nop()
                return e.tensor_scalar(stcol[:, 2:3], stcol[:, 2:3], 1e-13,
                                       None, ALU.add)
            S.task("vector", fn, reads=[r_stc], writes=[r_stc])
        rms_add(st[1], r_st[1])
        if own:
            rms_add(st[2], r_st[2])
        def rms_sqrt(stcol, r_stc):
            S.task("scalar",
                   lambda e: e.activation(stcol[:, 3:4], stcol[:, 2:3], AF.Ln,
                                          scale=1.0 / C),
                   reads=[r_stc], writes=[r_stc])
            S.task("scalar",
                   lambda e: e.activation(stcol[:, 4:5], stcol[:, 3:4], AF.Exp,
                                          scale=-0.5),
                   reads=[r_stc], writes=[r_stc])
        rms_sqrt(st[1], r_st[1])
        if own:
            rms_sqrt(st[2], r_st[2])
        def rms_fin(stcol, r_stc, pbank0, pbank1, rbank0, rbank1, dst):
            def ap1(e):
                e.tensor_tensor(dst[:, 0:512], pbank0[:, :], nsc_s[:, 0:512], ALU.mult)
                return e.tensor_tensor(dst[:, 512:1024], pbank1[:, :],
                                       nsc_s[:, 512:1024], ALU.mult)
            S.task("vector", ap1, reads=[rbank0, rbank1, r_nsc], writes=[r_scr3])
        rms_fin(st[1], r_st[1], pb[0], pb[1], rb[0], rb[1], scr3)
        S.task("scalar",
               lambda e: e.activation(kex_tm[:, :], scr3[:, :], AF.Exp,
                                      scale=st[1][:, 4:5]),
               reads=[r_scr3, r_st[1]], writes=[r_kex_tm])
        def vev(e):
            e.copy(v_tm[:, 0:512], pb[2][:, :])
            return e.copy(v_tm[:, 512:1024], pb[3][:, :])
        S.task("scalar", vev, reads=[rb[2], rb[3]], writes=[r_v_tm])
        if own:
            rms_fin(st[2], r_st[2], pb[4], pb[5], rb[4], rb[5], scr1)
            S.task("scalar",
                   lambda e: e.activation(qex_tm[:, :], scr1[:, :], AF.Exp,
                                          scale=st[2][:, 4:5]),
                   reads=[r_scr1, r_st[2]], writes=[r_qex_tm])

        S.task("vector",
               lambda e: e.tensor_tensor(khst[:, :], kex_tm[:, :], TAB[:, TI["kst"], :],
                                         ALU.mult),
               reads=[r_kex_tm, r_TAB], writes=[r_khst])

        if own:
            def proj_g(e):
                last = None
                for Mc in range(CC):
                    bank = pb[6] if Mc < 4 else pb[0]
                    col = (Mc % 4) * 128
                    for K in range(CC):
                        last = e.matmul(bank[:, col:col + 128],
                                        wg_s[:, K, Mc * 128:(Mc + 1) * 128],
                                        mixg[par][:, K, :],
                                        start=(K == 0 and Mc % 4 == 0),
                                        stop=(K == CC - 1 and Mc % 4 == 3))
                return last
            S.task("tensor", proj_g, reads=[r_mixg[par], r_wg], writes=[rb[6], rb[0]])
            def gev(e):
                e.activation(scr3[:, 0:512], pb[6][:, :], AF.Tanh, scale=0.5)
                return e.activation(scr3[:, 512:1024], pb[0][:, :], AF.Tanh, scale=0.5)
            S.task("scalar", gev, reads=[rb[6], rb[0]], writes=[r_scr3])
            S.task("vector",
                   lambda e: e.tensor_scalar(scr3[:, :], scr3[:, :], 0.5, 0.5,
                                             ALU.mult, ALU.add),
                   reads=[r_scr3], writes=[r_scr3])

            def trT(src):
                def fn(e):
                    last = None
                    for cc in range(CC):
                        last = e.matmul(pbt[:, cc * 128:(cc + 1) * 128],
                                        src[:, cc * 128:(cc + 1) * 128], ident[:, :],
                                        is_transpose=True, start=(cc == 0),
                                        stop=(cc == CC - 1))
                    return last
                return fn
            S.task("tensor", trT(qex_tm), reads=[r_qex_tm, r_ident], writes=[rbt])
            S.task("scalar", lambda e: e.copy(qex_fm[:, :], pbt[:, :]),
                   reads=[rbt], writes=[r_qex_fm])
            S.task("tensor", trT(kex_tm), reads=[r_kex_tm, r_ident], writes=[rbt])
            S.task("scalar", lambda e: e.copy(kex_fm[:, :], pbt[:, :]),
                   reads=[rbt], writes=[r_kex_fm])
            S.task("vector",
                   lambda e: e.tensor_tensor(qtint[:, :], qex_fm[:, :],
                                             TAB[:, TI["qint"], :], ALU.mult),
                   reads=[r_qex_fm, r_TAB], writes=[r_qtint])

            mkp = {"i": 0}
            def makes(qn, kn):
                mp = mkp["i"] & 1
                mkp["i"] += 1
                def fn(e, qn=qn, kn=kn, mp=mp):
                    e.tensor_tensor(mkq[mp][:, :], qex_fm[:, :], TAB[:, TI[qn], :],
                                    ALU.mult)
                    return e.tensor_tensor(mkk[mp][:, :], kex_fm[:, :],
                                           TAB[:, TI[kn], :], ALU.mult)
                S.task("vector", fn, reads=[r_qex_fm, r_kex_fm, r_TAB],
                       writes=[r_mkq[mp], r_mkk[mp]])
                return mp
            for (qn, kn, bank) in (("q8", "k8", 2), ("q8o", "k8o", 4),
                                   ("q16", "k16", 5), ("q32", "k32", 3),
                                   ("q64", "k64", 6)):
                mp = makes(qn, kn)
                def pmm(e, bank=bank, mp=mp):
                    last = None
                    for h in range(H):
                        for dc in range(2):
                            cc = 2 * h + dc
                            last = e.matmul(pb[bank][:, h * 128:(h + 1) * 128],
                                            mkk[mp][:, cc * 128:(cc + 1) * 128],
                                            mkq[mp][:, cc * 128:(cc + 1) * 128],
                                            start=(h == 0 and dc == 0),
                                            stop=(h == H - 1 and dc == 1))
                    return last
                S.task("tensor", pmm, reads=[r_mkq[mp], r_mkk[mp]], writes=[rb[bank]])

            def mk_m(m):
                return masks[:, m:m + 1, :].broadcast_to((128, 4, 128))
            def r3(t):
                return t.rearrange("p (a b) -> p a b", a=4)
            def pev1(e):
                sa, sb_ = scr1[:, 0:512], scr1[:, 512:1024]
                e.tensor_tensor(r3(sa), r3(pb[2][:, :]), mk_m(0), ALU.mult)
                e.tensor_tensor(r3(sb_), r3(pb[4][:, :]), mk_m(1), ALU.mult)
                e.engine_nop()
                e.tensor_tensor(sa, sa, sb_, ALU.add)
                e.tensor_tensor(r3(sb_), r3(pb[5][:, :]), mk_m(2), ALU.mult)
                e.engine_nop()
                e.tensor_tensor(sa, sa, sb_, ALU.add)
                e.tensor_tensor(r3(sb_), r3(pb[3][:, :]), mk_m(3), ALU.mult)
                e.engine_nop()
                e.tensor_tensor(sa, sa, sb_, ALU.add)
                e.tensor_tensor(r3(sb_), r3(pb[6][:, :]), mk_m(4), ALU.mult)
                e.engine_nop()
                return e.tensor_tensor(Psb[:, :], sa, sb_, ALU.add)
            S.task("vector", pev1, reads=[rb[2], rb[3], rb[4], rb[5], rb[6], r_masks],
                   writes=[r_Psb, r_scr1])

            def smm(bank, ccs):
                def fn(e, bank=bank, ccs=ccs):
                    last = None
                    n = len(ccs) * 3
                    i = 0
                    for cc in ccs:
                        h, ec = cc // 2, cc % 2
                        col = (cc % 4) * 128
                        last = e.matmul(pb[bank][:, col:col + 128],
                                        v_tm[:, cc * 128:(cc + 1) * 128],
                                        Psb[:, h * 128:(h + 1) * 128],
                                        start=(i == 0), stop=False)
                        i += 1
                        for dc in range(2):
                            ci2 = 2 * h + dc
                            last = e.matmul(pb[bank][:, col:col + 128],
                                            Abf[:, ci2, ec * 128:(ec + 1) * 128],
                                            qtint[:, ci2 * 128:(ci2 + 1) * 128],
                                            start=False, stop=(i == n - 1))
                            i += 1
                    return last
                return fn
            S.task("tensor", smm(4, [0, 1, 2, 3]),
                   reads=[r_v_tm, r_Psb, r_Abf, r_qtint], writes=[rb[4]])
            S.task("tensor", smm(5, [4, 5, 6, 7]),
                   reads=[r_v_tm, r_Psb, r_Abf, r_qtint], writes=[rb[5]])
            def zmm(e):
                last = None
                n = H * 3
                i = 0
                for h in range(H):
                    last = e.matmul(pb[6][0:1, h * 128:(h + 1) * 128],
                                    onesc_bf[:, :], Psb[:, h * 128:(h + 1) * 128],
                                    start=(i == 0), stop=False)
                    i += 1
                    for dc in range(2):
                        ci2 = 2 * h + dc
                        last = e.matmul(pb[6][0:1, h * 128:(h + 1) * 128],
                                        Abf[:, ci2, 256:257],
                                        qtint[:, ci2 * 128:(ci2 + 1) * 128],
                                        start=False, stop=(i == n - 1))
                        i += 1
                return last
            S.task("tensor", zmm, reads=[r_onesc, r_Psb, r_Abf, r_qtint],
                   writes=[rb[6]])


            def epi1(e):
                last = None
                for cc in range(CC):
                    bank = pb[4] if cc < 4 else pb[5]
                    col = (cc % 4) * 128
                    last = e.scalar_tensor_tensor(
                        attf[:, cc, :], bank[:, col:col + 128], 1.0,
                        scr3[:, cc * 128:(cc + 1) * 128], ALU.mult, ALU.mult)
                return last
            S.task("vector", epi1, reads=[rb[4], rb[5], r_scr3], writes=[r_attf])
            S.task("scalar", lambda e: e.copy(rzrow[:, :], pb[6][0:1, 0:512]),
                   reads=[rb[6]], writes=[r_rzrow])

        def damm(e):
            last = None
            for b in range(CC):
                h = b // 2
                last = e.matmul(pb[b // 2][:, (b % 2) * 256:(b % 2) * 256 + 256],
                                khst[:, b * 128:(b + 1) * 128],
                                v_tm[:, h * 256:(h + 1) * 256],
                                start=(b % 2 == 0), stop=(b % 2 == 1))
            return last
        S.task("tensor", damm, reads=[r_khst, r_v_tm],
               writes=[rb[0], rb[1], rb[2], rb[3]])
        def dzmm(e):
            last = None
            for b in range(CC):
                last = e.matmul(pb[6][:, b:b + 1], khst[:, b * 128:(b + 1) * 128],
                                onesc_bf[:, :], start=(b == 0), stop=(b == CC - 1))
            return last
        S.task("tensor", dzmm, reads=[r_khst, r_onesc], writes=[rb[6]])
        if own:
            S.task("tensor",
                   lambda e: e.matmul(pb[4][:, :], ones1[:, :], rzrow[:, :],
                                      start=True, stop=True),
                   reads=[r_rzrow, r_ones1], writes=[rb[4]])
            S.task("vector", lambda e: e.reciprocal(rzbc[:, :], pb[4][:, :]),
                   reads=[rb[4]], writes=[r_rzbc])
        def upd(e):
            last = None
            for b in range(CC):
                last = e.scalar_tensor_tensor(
                    Ast[:, b, 0:256], Ast[:, b, 0:256], wL[:, b:b + 1],
                    pb[b // 2][:, (b % 2) * 256:(b % 2) * 256 + 256],
                    ALU.mult, ALU.add)
                last = e.scalar_tensor_tensor(
                    Ast[:, b, 256:257], Ast[:, b, 256:257], wL[:, b:b + 1],
                    pb[6][:, b:b + 1], ALU.mult, ALU.add)
            return last
        if own:
            def epi2(e):
                rz3 = rzbc[:, :].rearrange("p (h i) -> p h i", h=H)
                rzb = rz3.unsqueeze(2).broadcast_to((128, H, 2, 128))
                a4 = attf[:, :, :].rearrange("p (h d) i -> p h d i", h=H)
                return e.tensor_tensor(a4, a4, rzb, ALU.mult)
            S.task("vector", epi2, reads=[r_rzbc, r_attf], writes=[r_attf])
            def warm(e):
                last = None
                for _ in range(12):
                    last = e.matmul(pbt[:, 0:128], ident[:, :], ident[:, :],
                                    is_transpose=True, start=True, stop=True)
                return last
            S.task("tensor", warm, reads=[r_ident], writes=[rbt])
            def womm(e):
                last = None
                for K in range(CC):
                    for nh in range(2):
                        last = e.matmul((pb[4], pb[5])[nh][:, :], attf[:, K, :],
                                        wo_s[:, K, nh * 512:(nh + 1) * 512],
                                        start=(K == 0), stop=(K == CC - 1))
                return last
            S.task("tensor", womm, reads=[r_attf, r_wo], writes=[rb[4], rb[5]])
        S.task("vector", upd, reads=[rb[0], rb[1], rb[2], rb[3], rb[6], r_wL],
               writes=[r_Ast])
        S.task("scalar", lambda e: e.copy(Abf[:, :, :], Ast[:, :, :]),
               reads=[r_Ast], writes=[r_Abf])

    def attn_backE_B(kind, ci):
        par, first = chctx[(kind, ci)][:2]
        if True:
            stv2, r_stv2 = stL, r_stL
            def l2sq(e):
                e.activation(scr1[:, 0:512], pb[4][:, :], AF.Square,
                             accum_out=stv2[:, 0:1])
                return e.activation(scr1[:, 512:1024], pb[5][:, :], AF.Square,
                                    accum_out=stv2[:, 1:2])
            S.task("scalar", l2sq, reads=[rb[4], rb[5]], writes=[r_scr1, r_stv2])
            def l2s(e):
                e.tensor_scalar(scr3[:, 0:512], pb[4][:, :], 1.0, 0.0, ALU.mult,
                                ALU.add, accum_out=stv2[:, 2:3])
                return e.tensor_scalar(scr3[:, 512:1024], pb[5][:, :], 1.0, 0.0,
                                       ALU.mult, ALU.add, accum_out=stv2[:, 3:4])
            S.task("vector", l2s, reads=[rb[4], rb[5], r_stv2],
                   writes=[r_scr3, r_stv2])
            def l2s2(e):
                e.tensor_tensor(stv2[:, 0:1], stv2[:, 0:1], stv2[:, 1:2], ALU.add)
                e.tensor_tensor(stv2[:, 2:3], stv2[:, 2:3], stv2[:, 3:4], ALU.add)
                e.engine_nop()
                e.tensor_scalar(stv2[:, 4:5], stv2[:, 2:3], 1.0 / C, None, ALU.mult)
                e.engine_nop()
                e.tensor_tensor(stv2[:, 5:6], stv2[:, 4:5], stv2[:, 4:5], ALU.mult)
                e.engine_nop()
                e.scalar_tensor_tensor(stv2[:, 6:7], stv2[:, 0:1], 1.0 / C,
                                       stv2[:, 5:6], ALU.mult, ALU.subtract)
                e.engine_nop()
                return e.tensor_scalar(stv2[:, 6:7], stv2[:, 6:7], 1e-5, None, ALU.add)
            S.task("vector", l2s2, reads=[r_stv2], writes=[r_stv2])
            S.task("scalar",
                   lambda e: e.activation(stv2[:, 7:8], stv2[:, 6:7], AF.Ln),
                   reads=[r_stv2], writes=[r_stv2])
            S.task("scalar",
                   lambda e: e.activation(stv2[:, 6:7], stv2[:, 7:8], AF.Exp,
                                          scale=-0.5),
                   reads=[r_stv2], writes=[r_stv2])
            def l2s3(e):
                e.tensor_tensor(stv2[:, 5:6], stv2[:, 4:5], stv2[:, 6:7], ALU.mult)
                e.engine_nop()
                return e.tensor_scalar(stv2[:, 5:6], stv2[:, 5:6], -1.0, None,
                                       ALU.mult)
            S.task("vector", l2s3, reads=[r_stv2], writes=[r_stv2])
            S.task("vector",
                   lambda e, par=par: e.tensor_scalar(sx[par][:, :], sx[par][:, :],
                                                      stv2[:, 5:6], None, ALU.add),
                   reads=[r_stv2], writes=[r_sx[par]])
            def l2b(e, par=par):
                e.scalar_tensor_tensor(sx[par][:, 0:512], pb[4][:, :], stv2[:, 6:7],
                                       sx[par][:, 0:512], ALU.mult, ALU.add)
                return e.scalar_tensor_tensor(sx[par][:, 512:1024], pb[5][:, :],
                                              stv2[:, 6:7], sx[par][:, 512:1024],
                                              ALU.mult, ALU.add)
            S.task("vector", l2b, reads=[rb[4], rb[5], r_stv2, r_sx[par]],
                   writes=[r_sx[par]])
            S.task("sync",
                   lambda e, ci=ci, par=par: e.dma_start(
                       xmid_d[ci * L:(ci + 1) * L, :], sx[par][:, :]),
                   reads=[r_sx[par]], writes=[r_xmidd], dma=True)


    def emit_state_mask():
        S.task("vector",
               lambda e: e.tensor_scalar(Ast[:, :, :], Ast[:, :, :], flag_s[:, 0:1],
                                         None, ALU.mult),
               reads=[r_flag], writes=[r_Ast])
        S.task("scalar", lambda e: e.copy(Abf[:, :, :], Ast[:, :, :]),
               reads=[r_Ast], writes=[r_Abf])

    chunks = [("pre", i) for i in range(n_pre)] + [("own", i) for i in range(n_own)]
    prep_done = prep_point()
    attn_frontA(*chunks[0])
    attn_frontB(*chunks[0])
    prev = None
    for idx, ch in enumerate(chunks):
        if prev is not None and prev[0] == "own":
            attn_backE_B(*prev)
        if idx == n_pre and n_pre > 0:
            emit_state_mask()
        prep_done = prep_point()
        if idx + 1 < len(chunks):
            attn_frontA(*chunks[idx + 1])
        prep_done = prep_point()
        attn_backSproj(*ch)
        if idx + 1 < len(chunks):
            attn_frontB(*chunks[idx + 1])
        prep_done = prep_point()
        attn_backSrest(*ch)
        prep_done = prep_point()
        prev = ch
    if prev[0] == "own":
        attn_backE_B(*prev)
    while not prep_done:
        prep_done = prep_point()

    tap_reg.update({
        "sx0": (sx[cstate["par"] ^ 1][:, :], r_sx[cstate["par"] ^ 1], "attn"),
        "stv0": (st[0][:, :], r_st[0], "attn"),
        "ln1tm": (ln1tm_t[:, :], r_ln1tm, "attn"),
        "scr1t": (scr1[:, :], r_scr1, "attn"),
        "TAB": (TAB[:, :, :], r_TAB, "attn"),
        "wLs": (wL[:, :], r_wL, "attn"),
        "ln1f": (ln1f[cstate["par"] ^ 1][:, :, :], r_ln1f[cstate["par"] ^ 1], "attn"),
        "mixo0": (mixo[0][:, :, :], r_mixo[0], "attn"),
        "mixo1": (mixo[1][:, :, :], r_mixo[1], "attn"),
        "kex": (kex_tm[:, :], r_kex_tm, "attn"),
        "qex": (qex_tm[:, :], r_qex_tm, "attn"),
        "vtm": (v_tm[:, :], r_v_tm, "attn"),
        "khst": (khst[:, :], r_khst, "attn"),
        "qexf": (qex_fm[:, :], r_qex_fm, "attn"),
        "kexf": (kex_fm[:, :], r_kex_fm, "attn"),
        "qti": (qtint[:, :], r_qtint, "attn"),
        "Psb": (Psb[:, :], r_Psb, "attn"),
        "attf": (attf[:, :, :], r_attf, "attn"),
        "Ast": (Ast[:, :, :], r_Ast, "attn"),
        "gfm": (scr3[:, :], r_scr3, "attn"),
        "nscs": (nsc_s[:, :], r_nsc, "attn"),
        "mixb": (mixb[:, :, :], r_mixb, "attn"),
    })
    emit_taps("attn")

    S.barrier()
    ctx_attn.close()
    ctx_a = ExitStack()
    wgu, r_wgu = mksb(ctx_a, "wgu", [128, 2, CC, INNER], BF16)
    hq, r_hq = mksb(ctx_a, "hq", [128, INNER], F32)
    hqb, r_hqb = zip(*[mksb(ctx_a, f"hqb{i}", [128, INNER], BF16) for i in range(2)])
    aq1, r_aq1 = mksb(ctx_a, "aq1", [128, C], BF16)
    aq1f, r_aq1f = zip(*[mksb(ctx_a, f"aq1f{i}", [128, C], BF16) for i in range(2)])
    sxm, r_sxm = zip(*[mksb(ctx_a, f"sxm{i}", [128, C], F32) for i in range(2)])
    fsc1, r_fsc1 = mksb(ctx_a, "fsc1", [128, C], F32)
    fsc2, r_fsc2 = mksb(ctx_a, "fsc2", [128, C], F32)
    fst2, r_fst2 = zip(*[mksb(ctx_a, f"fst{i}", [128, 12], F32) for i in range(2)])
    sil, r_sil = zip(*[mksb(ctx_a, f"sil{i}", [128, 512], F32) for i in range(2)])
    hsp, r_hsp = zip(*[mksb(ctx_a, f"hsp{i}", [128, 1024], BF16) for i in range(2)])

    r_wgu_s = [Res(f"wgu_s{s}") for s in range(NSL)]
    for s in range(NSL):
        for m in range(2):
            S.task("sync",
                   lambda e, m=m, s=s: e.dma_start(
                       wgu[:, m, :, s * 512:(s + 1) * 512],
                       gq_d[m, :, :, s * 512:(s + 1) * 512].transpose([1, 0, 2])),
                   reads=[r_gqd], writes=[r_wgu_s[s]], dma=True)

    def ffnA_frontA(ci):
        par = ci % 2
        fst, r_fst = fst2[par], r_fst2[par]
        S.task("scalar",
               lambda e, par=par, ci=ci: e.dma_start(sxm[par][:, :],
                                                     xmid_d[ci * L:(ci + 1) * L, :]),
               reads=[r_xmidd], writes=[r_sxm[par]], dma=True)
        def f1(e, par=par):
            e.scalar_tensor_tensor(fsc1[:, :], sxm[par][:, :], 1.0, sxm[par][:, :],
                                   ALU.mult, ALU.mult, accum_out=fst[:, 0:1])
            return e.tensor_scalar(fsc2[:, :], sxm[par][:, :], 1.0, 0.0, ALU.mult,
                                   ALU.add, accum_out=fst[:, 1:2])
        S.task("vector", f1, reads=[r_sxm[par]], writes=[r_fsc1, r_fsc2, r_fst])
        def f2(e):
            e.tensor_scalar(fst[:, 2:3], fst[:, 1:2], 1.0 / C, None, ALU.mult)
            e.engine_nop()
            e.tensor_tensor(fst[:, 3:4], fst[:, 2:3], fst[:, 2:3], ALU.mult)
            e.engine_nop()
            e.scalar_tensor_tensor(fst[:, 4:5], fst[:, 0:1], 1.0 / C, fst[:, 3:4],
                                   ALU.mult, ALU.subtract)
            e.engine_nop()
            return e.tensor_scalar(fst[:, 4:5], fst[:, 4:5], 1e-5, None, ALU.add)
        S.task("vector", f2, reads=[r_fst], writes=[r_fst])
        S.task("scalar", lambda e: e.activation(fst[:, 5:6], fst[:, 4:5], AF.Ln),
               reads=[r_fst], writes=[r_fst])
        S.task("scalar", lambda e: e.activation(fst[:, 6:7], fst[:, 5:6], AF.Exp,
                                                scale=-0.5),
               reads=[r_fst], writes=[r_fst])
        def f3(e):
            e.tensor_tensor(fst[:, 7:8], fst[:, 2:3], fst[:, 6:7], ALU.mult)
            e.engine_nop()
            return e.tensor_scalar(fst[:, 7:8], fst[:, 7:8], -1.0, None, ALU.mult)
        S.task("vector", f3, reads=[r_fst], writes=[r_fst])
        S.task("scalar",
               lambda e, par=par: e.activation(fsc1[:, :], sxm[par][:, :], AF.Identity,
                                               bias=fst[:, 7:8], scale=fst[:, 6:7]),
               reads=[r_sxm[par], r_fst], writes=[r_fsc1])
        def f4(e):
            return e.reduce_max(fst[:, 8:9], fsc1[:, :], AX.X,
                                apply_absolute_value=True)
        S.task("vector", f4, reads=[r_fsc1], writes=[r_fst])
        def f5(e):
            e.tensor_scalar(fst[:, 8:9], fst[:, 8:9], 1e-5, None, ALU.max)
            e.engine_nop()
            e.reciprocal(fst[:, 9:10], fst[:, 8:9])
            e.engine_nop()
            return e.tensor_scalar(fst[:, 9:10], fst[:, 9:10], 127.0, None, ALU.mult)
        S.task("vector", f5, reads=[r_fst], writes=[r_fst])
        def f6(e):
            e.tensor_scalar(fsc2[:, :], fsc1[:, :], fst[:, 9:10], MAGIC, ALU.mult,
                            ALU.add)
            e.engine_nop()
            return e.tensor_scalar(aq1[:, :], fsc2[:, :], MAGIC, None, ALU.subtract)
        S.task("vector", f6, reads=[r_fsc1, r_fst], writes=[r_fsc2, r_aq1])
        def f7(e):
            e.scalar_tensor_tensor(fst[:, 10:11], mgv_m[:, 0:1], 1.0 / 127.0,
                                   fst[:, 8:9], ALU.mult, ALU.mult)
            e.scalar_tensor_tensor(fst[:, 11:12], mgv_m[:, 1:2], 1.0 / 127.0,
                                   fst[:, 8:9], ALU.mult, ALU.mult)
            return e.memset(fst[:, 0:1], 0.0)
        S.task("vector", f7, reads=[r_mgv_m, r_fst], writes=[r_fst])

    def ffnA_frontB(ci):
        par = ci % 2
        def aqT(e):
            last = None
            for cc in range(CC):
                last = e.matmul(pbt[:, cc * 128:(cc + 1) * 128],
                                aq1[:, cc * 128:(cc + 1) * 128], ident[:, :],
                                is_transpose=True, start=(cc == 0), stop=(cc == CC - 1))
            return last
        S.task("tensor", aqT, reads=[r_aq1, r_ident], writes=[rbt])
        S.task("scalar", lambda e, par=par: e.copy(aq1f[par][:, :], pbt[:, :]),
               reads=[rbt], writes=[r_aq1f[par]])

    def ffnA_backG(ci):
        par = ci % 2
        fst, r_fst = fst2[par], r_fst2[par]
        for s in range(NSL):
            bg, bu = (0, 1) if s % 2 == 0 else (2, 3)
            def gumm(e, s=s, bg=bg, bu=bu, par=par):
                last = None
                for K in range(CC):
                    last = e.matmul(pb[bg][:, :], aq1f[par][:, K * 128:(K + 1) * 128],
                                    wgu[:, 0, K, s * 512:(s + 1) * 512],
                                    start=(K == 0), stop=(K == CC - 1))
                for K in range(CC):
                    last = e.matmul(pb[bu][:, :], aq1f[par][:, K * 128:(K + 1) * 128],
                                    wgu[:, 1, K, s * 512:(s + 1) * 512],
                                    start=(K == 0), stop=(K == CC - 1))
                return last
            S.task("tensor", gumm, reads=[r_aq1f[par], r_wgu_s[s]],
                   writes=[rb[bg], rb[bu]])
            sp = s % 2
            S.task("scalar",
                   lambda e, bg=bg, sp=sp: e.activation(sil[sp][:, :], pb[bg][:, :],
                                                        AF.Silu, scale=fst[:, 10:11]),
                   reads=[rb[bg], r_fst], writes=[r_sil[sp]])
            S.task("vector",
                   lambda e, bu=bu, sp=sp, s=s: e.scalar_tensor_tensor(
                       hq[:, s * 512:(s + 1) * 512], pb[bu][:, :], 1.0, sil[sp][:, :],
                       ALU.mult, ALU.mult),
                   reads=[rb[bu], r_sil[sp]], writes=[r_hq])
            def hmax(e, s=s):
                e.reduce_max(fst[:, 1:2], hq[:, s * 512:(s + 1) * 512], AX.X,
                             apply_absolute_value=True)
                e.engine_nop()
                return e.tensor_tensor(fst[:, 0:1], fst[:, 0:1], fst[:, 1:2], ALU.max)
            S.task("vector", hmax, reads=[r_hq, r_fst], writes=[r_fst])
        def fh(e, ci=ci):
            e.tensor_tensor(fst[:, 2:3], fst[:, 11:12], fst[:, 0:1], ALU.mult)
            e.engine_nop()
            e.tensor_scalar(fst[:, 2:3], fst[:, 2:3], 1e-5, None, ALU.max)
            e.engine_nop()
            e.reciprocal(fst[:, 3:4], fst[:, 2:3])
            e.engine_nop()
            e.tensor_scalar(fst[:, 3:4], fst[:, 3:4], 127.0, None, ALU.mult)
            e.engine_nop()
            e.tensor_tensor(fst[:, 4:5], fst[:, 11:12], fst[:, 3:4], ALU.mult)
            e.engine_nop()
            return e.scalar_tensor_tensor(alpha[:, ci:ci + 1], mgv_m[:, 2:3],
                                          1.0 / 127.0, fst[:, 2:3], ALU.mult, ALU.mult)
        S.task("vector", fh, reads=[r_fst, r_mgv_m], writes=[r_fst, r_alpha])
        def qh(e, par=par):
            e.tensor_scalar(hq[:, :], hq[:, :], fst[:, 4:5], MAGIC, ALU.mult, ALU.add)
            e.engine_nop()
            return e.tensor_scalar(hqb[par][:, :], hq[:, :], MAGIC, None, ALU.subtract)
        S.task("vector", qh, reads=[r_hq, r_fst], writes=[r_hq, r_hqb[par]])

    def ffnA_tail(ci):
        par = ci % 2
        for grp in range(4):
            gp = grp % 2
            def hT(e, grp=grp, par=par):
                last = None
                for tile in range(8):
                    ic = grp * 8 + tile
                    last = e.matmul(pbt[:, tile * 128:(tile + 1) * 128],
                                    hqb[par][:, ic * 128:(ic + 1) * 128], ident[:, :],
                                    is_transpose=True, start=(tile == 0),
                                    stop=(tile == 7))
                return last
            S.task("tensor", hT, reads=[r_hqb[par], r_ident], writes=[rbt])
            S.task("scalar", lambda e, gp=gp: e.copy(hsp[gp][:, :], pbt[:, :]),
                   reads=[rbt], writes=[r_hsp[gp]])
            S.task("sync",
                   lambda e, ci=ci, grp=grp, gp=gp: e.dma_start(hq_d[ci, grp, :, :],
                                                                hsp[gp][:, :]),
                   reads=[r_hsp[gp]], writes=[r_hqd], dma=True)

    ffnA_frontA(0)
    ffnA_frontB(0)
    for ci in range(n_own):
        if ci + 1 < n_own:
            ffnA_frontA(ci + 1)
        ffnA_backG(ci)
        if ci + 1 < n_own:
            ffnA_frontB(ci + 1)
        if ci > 0:
            ffnA_tail(ci - 1)
    ffnA_tail(n_own - 1)

    S.barrier()
    ctx_a.close()
    ctx_b = ExitStack()
    wdq, r_wdq = mksb(ctx_b, "wdq", [128, NIC, C], BF16)
    hqt, r_hqt = zip(*[mksb(ctx_b, f"hqt{i}", [128, INNER], BF16) for i in range(2)])
    sxb, r_sxb = zip(*[mksb(ctx_b, f"sxb{i}", [128, C], F32) for i in range(2)])
    dsc, r_dsc = mksb(ctx_b, "dsc", [128, C], F32)
    bsc, r_bsc = mksb(ctx_b, "bsc", [128, C], F32)
    yb, r_yb = zip(*[mksb(ctx_b, f"yb{i}", [128, C], F32) for i in range(2)])
    bst, r_bst = mksb(ctx_b, "bst", [128, 10], F32)

    r_wdq_g = [Res(f"wdq_g{g}") for g in range(4)]
    for g in range(4):
        S.task("sync",
               lambda e, g=g: e.dma_start(
                   wdq[:, g * 8:(g + 1) * 8, :],
                   dq_d[g * 8:(g + 1) * 8, :, :].transpose([1, 0, 2])),
               reads=[r_dqd], writes=[r_wdq_g[g]], dma=True)

    def emit_ffnB_chunk(ci):
        par = ci % 2
        for g in range(4):
            def dmm(e, par=par, g=g):
                last = None
                for K in range(g * 8, (g + 1) * 8):
                    for nh in range(2):
                        last = e.matmul((pb[0], pb[1])[nh][:, :],
                                        hqt[par][:, K * 128:(K + 1) * 128],
                                        wdq[:, K, nh * 512:(nh + 1) * 512],
                                        start=(K == 0),
                                        stop=(K == NIC - 1))
                return last
            S.task("tensor", dmm, reads=[r_hqt[par], r_wdq_g[g]],
                   writes=[rb[0], rb[1]])
        def b1(e, ci=ci):
            e.tensor_scalar(dsc[:, 0:512], pb[0][:, :], alpha[:, ci:ci + 1], 0.0,
                            ALU.mult, ALU.add, accum_out=bst[:, 0:1])
            return e.tensor_scalar(dsc[:, 512:1024], pb[1][:, :], alpha[:, ci:ci + 1],
                                   0.0, ALU.mult, ALU.add, accum_out=bst[:, 1:2])
        S.task("vector", b1, reads=[rb[0], rb[1], r_alpha], writes=[r_dsc, r_bst])
        S.task("scalar",
               lambda e: e.activation(bsc[:, :], dsc[:, :], AF.Square,
                                      accum_out=bst[:, 2:3]),
               reads=[r_dsc], writes=[r_bsc, r_bst])
        def b3(e):
            e.tensor_tensor(bst[:, 0:1], bst[:, 0:1], bst[:, 1:2], ALU.add)
            e.engine_nop()
            e.tensor_scalar(bst[:, 3:4], bst[:, 0:1], 1.0 / C, None, ALU.mult)
            e.engine_nop()
            e.tensor_tensor(bst[:, 4:5], bst[:, 3:4], bst[:, 3:4], ALU.mult)
            e.engine_nop()
            e.scalar_tensor_tensor(bst[:, 5:6], bst[:, 2:3], 1.0 / C, bst[:, 4:5],
                                   ALU.mult, ALU.subtract)
            e.engine_nop()
            return e.tensor_scalar(bst[:, 5:6], bst[:, 5:6], 1e-5, None, ALU.add)
        S.task("vector", b3, reads=[r_bst], writes=[r_bst])
        S.task("scalar", lambda e: e.activation(bst[:, 6:7], bst[:, 5:6], AF.Ln),
               reads=[r_bst], writes=[r_bst])
        S.task("scalar", lambda e: e.activation(bst[:, 7:8], bst[:, 6:7], AF.Exp,
                                                scale=-0.5),
               reads=[r_bst], writes=[r_bst])
        def b4(e):
            e.tensor_tensor(bst[:, 8:9], bst[:, 3:4], bst[:, 7:8], ALU.mult)
            e.engine_nop()
            return e.tensor_scalar(bst[:, 8:9], bst[:, 8:9], -1.0, None, ALU.mult)
        S.task("vector", b4, reads=[r_bst], writes=[r_bst])
        S.task("scalar",
               lambda e, par=par: e.activation(yb[par][:, :], sxb[par][:, :], AF.Identity,
                                               bias=bst[:, 8:9]),
               reads=[r_sxb[par], r_bst], writes=[r_yb[par]])
        S.task("vector",
               lambda e, par=par: e.scalar_tensor_tensor(yb[par][:, :], dsc[:, :],
                                                         bst[:, 7:8], yb[par][:, :],
                                                         ALU.mult, ALU.add),
               reads=[r_dsc, r_bst, r_yb[par]], writes=[r_yb[par]])
        S.task("sync",
               lambda e, par=par, ci=ci: e.dma_start(y[ci * L:(ci + 1) * L, :],
                                                     yb[par][:, :]),
               reads=[r_yb[par]], dma=True)

    def ffnB_front(ci):
        par = ci % 2
        S.task("scalar",
               lambda e, par=par, ci=ci: e.dma_start(
                   hqt[par][:, :].rearrange("p (g q) -> p g q", g=4),
                   hq_d[ci, :, :, :].transpose([1, 0, 2])),
               reads=[r_hqd], writes=[r_hqt[par]], dma=True)
        S.task("scalar",
               lambda e, par=par, ci=ci: e.dma_start(sxb[par][:, :],
                                                     xmid_d[ci * L:(ci + 1) * L, :]),
               reads=[r_xmidd], writes=[r_sxb[par]], dma=True)

    ffnB_front(0)
    for ci in range(n_own):
        if ci + 1 < n_own:
            ffnB_front(ci + 1)
        emit_ffnB_chunk(ci)

    with nc.Block() as block:
        S.emit(nc, block)
    ctx_b.close()
    ctx_common.close()
    return nc


def _host_constants():
    p = np.arange(128, dtype=np.float32)
    pos = {
        "q8": (p % 8) - 3.5, "k8": 3.5 - (p % 8),
        "q8o": (p % 16) - 7.5, "k8o": 7.5 - (p % 16),
        "q16": (p % 32) - 15.5, "k16": 15.5 - (p % 32),
        "q32": (p % 32) + 0.5, "k32": 31.5 - (p % 32),
        "q64": (p % 64) + 0.5, "k64": 63.5 - p,
        "qint": p + 1.0, "kst": 127.0 - p,
    }
    posv = np.stack([pos[k] for k in POS_ORDER])[None, :, :].astype(np.float32)
    j = np.arange(128)[:, None]
    i = np.arange(128)[None, :]
    m8 = ((j // 8 == i // 8) & (j <= i)).astype(np.float32)
    m8o = ((j // 16 == i // 16) & ((i // 8) % 2 == 1)
           & ((j // 8) % 2 == 0)).astype(np.float32)
    m16 = ((j // 32 == i // 32) & ((i // 16) % 2 == 1)
           & ((j // 16) % 2 == 0)).astype(np.float32)
    m32 = ((j // 64 == i // 64) & ((i // 32) % 2 == 1)
           & ((j // 32) % 2 == 0)).astype(np.float32)
    m64 = ((j < 64) & (i >= 64)).astype(np.float32)
    masks = np.stack([m8, m8o, m16, m32, m64], axis=1).astype(np.float32)
    ident = np.eye(128, dtype=BF)
    return posv, masks, ident


_CACHE = {}


def _get_program(n_own=NOWN, n_pre=NPRE, debug_taps=()):
    key = (n_own, n_pre, tuple(debug_taps))
    if key not in _CACHE:
        _CACHE[key] = build_program(n_own, n_pre, debug_taps)
    return _CACHE[key]


def make_in_maps(inputs, n_own=NOWN, n_pre=NPRE, n_cores=8):
    x = np.asarray(inputs["L0_attn_x"], dtype=np.float32)
    td = np.asarray(inputs["L0_attn_time_decay"], dtype=np.float32).reshape(1, C)
    nsc = np.asarray(inputs["L0_attn_norm_scale"], dtype=np.float32)
    mixes = [np.asarray(inputs["L0_attn_mix_" + n], dtype=np.float32).reshape(C)
             for n in ("q", "k", "v", "g")]
    mix_bc = np.stack([np.repeat(m.reshape(CC, 128).T[:, :, None], 128, axis=2)
                       .reshape(128, C) for m in mixes], axis=1).astype(np.float32)
    nsc_bc = np.ascontiguousarray(np.broadcast_to(nsc.reshape(1, C), (128, C)))
    posv, masks, ident = _host_constants()

    wT = {k: np.ascontiguousarray(np.asarray(inputs[f"L0_attn_W{k}"],
                                             dtype=np.float32).T)
          for k in ("q", "k", "v", "g", "o")}
    wgateT = np.ascontiguousarray(np.asarray(inputs["L0_ffn_w_gate"], np.float32).T)
    wupT = np.ascontiguousarray(np.asarray(inputs["L0_ffn_w_up"], np.float32).T)
    wdownT = np.ascontiguousarray(np.asarray(inputs["L0_ffn_w_down"], np.float32).T)

    NTOK = n_own * L
    NPRET = max(n_pre, 1) * L
    in_maps = []
    for core in range(n_cores):
        b, half = core // 2, core % 2
        start = half * (T // 2)
        xin = np.ascontiguousarray(x[b, start:start + NTOK])
        if half == 0 or n_pre == 0:
            xpre = np.zeros((NPRET, C), np.float32)
            fl = np.zeros((128, 1), np.float32)
        else:
            xpre = np.ascontiguousarray(x[b, start - NPRET:start])
            fl = np.ones((128, 1), np.float32)
        in_maps.append(dict(
            xin=xin, xpre=xpre, flag=fl, td=td, nsc_bc=nsc_bc, mix_bc=mix_bc,
            wqT=wT["q"], wkT=wT["k"], wvT=wT["v"], wgT=wT["g"], woT=wT["o"],
            wgateT=wgateT, wupT=wupT, wdownT=wdownT,
            ident_i=ident, masks_i=masks, posv=posv,
        ))
    return in_maps


def kernel(**inputs):
    nc = _get_program()
    in_maps = make_in_maps(inputs)
    res = run_bass_kernel_spmd(nc, in_maps, core_ids=list(range(8)))
    out = np.empty((B, T, C), np.float32)
    for core in range(8):
        b, half = core // 2, core % 2
        out[b, half * (T // 2):(half + 1) * (T // 2)] = res.results[core]["y"]
    return out
